# version 14
# baseline (speedup 1.0000x reference)
import numpy as np
import ml_dtypes
import concourse.bass as bass
import concourse.mybir as mybir
from concourse.bass_utils import run_bass_kernel_spmd
from contextlib import ExitStack

F32 = mybir.dt.float32
BF16 = mybir.dt.bfloat16
I32 = mybir.dt.int32
AF = mybir.ActivationFunctionType
OP = mybir.AluOpType
AX = mybir.AxisListType.X

NC = 8
TOK = 1024
CH = 8
CT = 128
D = 2048
H = 8
DV = 256
HID = 5632
NHB = HID // 128
NKD = D // 128
EPS = 1e-5
KS = 1.0 / 16.0
NWINV = 1.0 / float(HID * D)
WIN = 4
DBG = False
STOP = 99
LAST = {}

QUANT = {"pe": 1, "v": 1, "s": 1, "g": 1}


def quant(cls):
    return 16 if cls.startswith("D.") else 1
ENG_CLS = {"tensor": "pe", "vector": "v", "scalar": "s", "gpsimd": "g"}


class Sched:

    def __init__(self):
        self.steps = []
        self.ord_ = {c: 0 for c in QUANT}
        self.lastw = {}
        self.readers = {}
        self.enabled = True

    def section(self, n):
        self.enabled = (n <= STOP)

    def _dep_add(self, waits, cls, o):
        if o > waits.get(cls, 0):
            waits[cls] = o

    def add(self, eng, fn, *, cls=None, r=(), w=(), barrier=False):
        if not self.enabled and not barrier:
            return
        if cls is None:
            cls = ENG_CLS[eng]
        if cls not in self.ord_:
            self.ord_[cls] = 0
        ordinal = self.ord_[cls] + 1
        waits = {}
        if barrier:
            for c, o in self.ord_.items():
                if o > 0:
                    waits[c] = o
        for res in r:
            lw = self.lastw.get(res)
            if lw:
                self._dep_add(waits, lw[0], lw[1])
        for res in w:
            lw = self.lastw.get(res)
            if lw:
                self._dep_add(waits, lw[0], lw[1])
            for (c, o) in self.readers.get(res, ()):
                self._dep_add(waits, c, o)
        for res in r:
            self.readers.setdefault(res, []).append((cls, ordinal))
        for res in w:
            self.lastw[res] = (cls, ordinal)
            self.readers[res] = []
        self.steps.append((eng, cls, fn, ordinal, waits))
        self.ord_[cls] += 1

    def dma(self, eng, out, in_, *, r=(), w=(), **kw):
        res = w[0] if w else "misc"
        self.add(eng, lambda e, o=out, i=in_, k=dict(kw): e.dma_start(o, i, **k),
                 cls="D." + res, r=r, w=w)

    def mm(self, out, lhsT, rhs, start=True, stop=True, *, r=(), w=()):
        self.add("tensor",
                 lambda e, o=out, l=lhsT, rr=rhs, s1=start, s2=stop:
                 e.matmul(o, l, rr, start=s1, stop=s2), r=r, w=w)

    def emit(self, nc, sems):
        steps = self.steps
        with nc.Block() as block:
            SKIP_SAME = {"tensor": "pe", "scalar": "s", "gpsimd": "g"}

            def mk(name):
                def body(e):
                    wm = {}
                    mycls = SKIP_SAME.get(name)
                    for (eng, cls, fn, ordinal, waits) in steps:
                        if eng != name:
                            continue
                        for c, o in sorted(waits.items()):
                            if c == mycls:
                                continue
                            if o > wm.get(c, 0):
                                e.wait_ge(sems(c), o * quant(c))
                                wm[c] = o
                        ins = fn(e)
                        if ins is not None:
                            ins.then_inc(sems(cls), quant(cls))
                return body
            block.sync(mk("sync"))
            block.gpsimd(mk("gpsimd"))
            block.vector(mk("vector"))
            block.scalar(mk("scalar"))
            block.tensor(mk("tensor"))
        self.steps = []


def pres(off, n):
    return [f"P{b}" for b in range(off // 512, (off + n - 1) // 512 + 1)]


def build():
    nc = bass.Bass("TRN2", target_bir_lowering=False)

    x_d = nc.dram_tensor("x", [TOK, D], F32, kind="ExternalInput")
    wq_d = nc.dram_tensor("wq", [D, 1024], F32, kind="ExternalInput")
    wk_d = nc.dram_tensor("wk", [D, 1024], F32, kind="ExternalInput")
    wkg1_d = nc.dram_tensor("wkg1", [D, 16], F32, kind="ExternalInput")
    wkg2_d = nc.dram_tensor("wkg2", [16, 1024], F32, kind="ExternalInput")
    bkg2T_d = nc.dram_tensor("bkg2T", [128, 8], F32, kind="ExternalInput")
    wv_d = nc.dram_tensor("wv", [D, D], F32, kind="ExternalInput")
    wg_d = nc.dram_tensor("wg", [D, D], F32, kind="ExternalInput")
    bgT_d = nc.dram_tensor("bgT", [128, 16], F32, kind="ExternalInput")
    wo_d = nc.dram_tensor("wo", [D, D], F32, kind="ExternalInput")
    w1s_d = nc.dram_tensor("w1s", [D // NC, HID], F32, kind="ExternalInput")
    w2s_d = nc.dram_tensor("w2s", [HID // NC, D], F32, kind="ExternalInput")
    w3s_d = nc.dram_tensor("w3s", [D // NC, HID], F32, kind="ExternalInput")
    idn_d = nc.dram_tensor("idn", [128, 128], BF16, kind="ExternalInput")
    idnf_d = nc.dram_tensor("idnf", [128, 128], F32, kind="ExternalInput")
    tri8_d = nc.dram_tensor("tri8", [128, 1024], BF16, kind="ExternalInput")
    scanm_d = nc.dram_tensor("scanm", [128, 16], F32, kind="ExternalInput")
    out_d = nc.dram_tensor("out", [TOK, D], F32, kind="ExternalOutput")

    agi = nc.dram_tensor("agi", [128, 2064], F32)
    ago = nc.dram_tensor("ago", [NC * 128, 2064], F32)
    ari = nc.dram_tensor("ari", [1, 4], F32)
    aro = nc.dram_tensor("aro", [1, 4], F32)
    t1i = nc.dram_tensor("t1i", [D // NC, HID], BF16)
    t1o = nc.dram_tensor("t1o", [D, HID], BF16)
    t2i = nc.dram_tensor("t2i", [HID // NC, D], BF16)
    t2o = nc.dram_tensor("t2o", [HID, D], BF16)
    t3i = nc.dram_tensor("t3i", [D // NC, HID], BF16)
    t3o = nc.dram_tensor("t3o", [D, HID], BF16)
    rowb = nc.dram_tensor("rowb", [8, 128], F32)
    h_dr = nc.dram_tensor("h_dr", [HID, TOK], BF16)
    zh_dr = nc.dram_tensor("zh_dr", [HID, TOK], BF16)
    tp = {}
    if DBG:
        tp["xT"] = nc.dram_tensor("tp_xT", [128, NKD * TOK], BF16, kind="ExternalOutput")
        tp["Q"] = nc.dram_tensor("tp_Q", [128, H * TOK], F32, kind="ExternalOutput")
        tp["qdT"] = nc.dram_tensor("tp_qdT", [128, H * TOK], BF16, kind="ExternalOutput")
        tp["kd"] = nc.dram_tensor("tp_kd", [128, CH * H * 128], BF16, kind="ExternalOutput")
        tp["AT"] = nc.dram_tensor("tp_AT", [128, CH * H * 128], BF16, kind="ExternalOutput")
        tp["V"] = nc.dram_tensor("tp_V", [128, CH * D], BF16, kind="ExternalOutput")
        tp["L"] = nc.dram_tensor("tp_L", [128, 2048], F32, kind="ExternalOutput")
        tp["Sst"] = nc.dram_tensor("tp_Sst", [128, 2048], F32, kind="ExternalOutput")
        tp["orw"] = nc.dram_tensor("tp_orw", [128, CH * D], BF16, kind="ExternalOutput")
        tp["oo"] = nc.dram_tensor("tp_oo", [128, CH * D], BF16, kind="ExternalOutput")
        tp["x1"] = nc.dram_tensor("tp_x1", [128, CH * D], BF16, kind="ExternalOutput")
        tp["stat"] = nc.dram_tensor("tp_stat", [128, 512], F32, kind="ExternalOutput")
        tp["zxT"] = nc.dram_tensor("tp_zxT", [128, NKD * TOK], BF16, kind="ExternalOutput")
        tp["x2p"] = nc.dram_tensor("tp_x2p", [128, CH * D], BF16, kind="ExternalOutput")
        tp["msc"] = nc.dram_tensor("tp_msc", [1, 16], F32, kind="ExternalOutput")

    S = Sched()

    DLS, QLS, OMU, OSSQ, ORS, SCR64 = 0, 64, 72, 136, 200, 264
    XMU, XSSQ, XRS, SX, CMX, HCM = 328, 336, 344, 352, 360, 376
    DCORE, DPC, THR, NTHR, X2MU, X2SSQ, X2RS, SCR8 = 384, 400, 408, 412, 416, 424, 432, 440

    ctx = ExitStack()
    with ctx:
        idn = ctx.enter_context(nc.sbuf_tensor("idn_s", [128, 128], BF16))
        idnf = ctx.enter_context(nc.sbuf_tensor("idnf_s", [128, 128], F32))
        tri8 = ctx.enter_context(nc.sbuf_tensor("tri8_s", [128, 1024], BF16))
        scanm = ctx.enter_context(nc.sbuf_tensor("scanm_s", [128, 16], F32))
        bgT = ctx.enter_context(nc.sbuf_tensor("bgT_s", [128, 16], F32))
        nbkg = ctx.enter_context(nc.sbuf_tensor("nbkg", [128, 8], F32))
        ones_c = ctx.enter_context(nc.sbuf_tensor("ones_c", [128, 1], F32))
        onesr = ctx.enter_context(nc.sbuf_tensor("onesr", [1, 128], F32))
        eps_c = ctx.enter_context(nc.sbuf_tensor("eps_c", [128, 1], F32))
        Wb = [ctx.enter_context(nc.sbuf_tensor(f"Wb{i}", [128, 512], BF16))
              for i in range(4)]
        xT = ctx.enter_context(nc.sbuf_tensor("xT", [128, NKD * TOK], BF16))
        ox1 = ctx.enter_context(nc.sbuf_tensor("ox1", [128, CH * D], BF16))
        stat = ctx.enter_context(nc.sbuf_tensor("stat", [128, 512], F32))
        row1 = ctx.enter_context(nc.sbuf_tensor("row1", [1, 1024], F32))
        row2 = ctx.enter_context(nc.sbuf_tensor("row2", [1, 1024], F32))
        rtmp = ctx.enter_context(nc.sbuf_tensor("rtmp", [8, 128], F32))
        msc = ctx.enter_context(nc.sbuf_tensor("msc", [1, 16], F32))
        PS = ctx.enter_context(nc.psum_tensor("PS", [128, 4096], F32))
        _semmap = {}

        def sems(c):
            if c not in _semmap:
                _semmap[c] = ctx.enter_context(
                    nc.semaphore("sem_%d" % len(_semmap)))
            return _semmap[c]

        NQ = NKD * TOK

        def xTs(k, a, b):
            return xT[:, TOK * k + a: TOK * k + b]

        def stc(a, n=1):
            return stat[:, a:a + n]

        def v3(t, total_cols, offset, s1, n1, s2, n2):
            return bass.AP(t, offset, [[total_cols, 128], [s1, n1], [s2, n2]])

        def tap(name, buf, res):
            if DBG:
                S.dma("sync", tp[name][:, :], buf, r=[res])

        CONSTS = ["c.idn", "c.idnf", "c.tri8", "c.scanm", "c.bgT", "c.nbkg",
                  "c.ones", "c.onesr", "c.eps"]

        def transpose_tiles(srcs, src_res, dst_view, dst_res, ps_off=0):
            banks = pres(ps_off, 128 * len(srcs))
            for t, s in enumerate(srcs):
                S.mm(PS[:, ps_off + 128 * t: ps_off + 128 * (t + 1)], s, idn[:, :],
                     r=[src_res, "c.idn"], w=pres(ps_off + 128 * t, 128))
            n = len(srcs)
            pv = v3(PS, 4096, ps_off, 128, n, 1, 128)
            S.add("vector", lambda e, d=dst_view, p=pv: e.tensor_copy(d, p),
                  r=banks, w=[dst_res])

        S.dma("sync", idn[:, :], idn_d[:, :], w=["c.idn"])
        S.dma("sync", idnf[:, :], idnf_d[:, :], w=["c.idnf"])
        S.dma("sync", tri8[:, :], tri8_d[:, :], w=["c.tri8"])
        S.dma("sync", scanm[:, :], scanm_d[:, :], w=["c.scanm"])
        S.dma("sync", bgT[:, :], bgT_d[:, :], w=["c.bgT"])
        S.dma("sync", nbkg[:, :], bkg2T_d[:, :], w=["c.nbkg"])
        S.add("vector", lambda e: e.tensor_scalar(
            nbkg[:, :], nbkg[:, :], -1.0, None, OP.mult), r=["c.nbkg"], w=["c.nbkg"])
        S.add("vector", lambda e: e.memset(ones_c[:, :], 1.0), w=["c.ones"])
        S.add("vector", lambda e: e.memset(onesr[:, :], 1.0), w=["c.onesr"])
        S.add("vector", lambda e: e.memset(eps_c[:, :], EPS), w=["c.eps"])
        S.add("vector", lambda e: e.memset(stat[:, :], 0.0),
              w=["st.dls", "st.qls", "st.omu", "st.ossq", "st.ors", "st.s64",
                 "st.xmu", "st.xssq", "st.xrs", "st.sx", "st.cmx", "st.hcm",
                 "st.dc", "st.dpc", "st.thr", "st.nthr", "st.x2mu", "st.x2ssq",
                 "st.x2rs", "st.s8"])

        S.section(10)
        s1 = ExitStack()
        with s1:
            qdT = s1.enter_context(nc.sbuf_tensor("qdT", [128, H * TOK], BF16))
            wsum = s1.enter_context(nc.sbuf_tensor("wsum", [128, 24], F32))
            kd = s1.enter_context(nc.sbuf_tensor("kd", [128, CH * H * 128], BF16))
            AT = s1.enter_context(nc.sbuf_tensor("AT", [128, CH * H * 128], BF16))

            s1a = ExitStack()
            with s1a:
                xbf = [s1a.enter_context(nc.sbuf_tensor(f"xbf{i}", [128, D], BF16))
                       for i in range(2)]
                for c in range(CH):
                    xb = xbf[c % 2]
                    xr = f"xbf{c % 2}"
                    S.dma("gpsimd", xb[:, :], x_d[CT * c:CT * (c + 1), :], w=[xr])
                    transpose_tiles([xb[:, 128 * t:128 * (t + 1)] for t in range(NKD)],
                                    xr, v3(xT, NQ, CT * c, TOK, NKD, 1, CT), "xT",
                                    ps_off=2048)
                tap("xT", xT[:, :], "xT")
                S.emit(nc, sems)

            S.section(20)
            s1b = ExitStack()
            with s1b:
                Q = s1b.enter_context(nc.sbuf_tensor("Q", [128, H * TOK], F32))
                Etmp = s1b.enter_context(nc.sbuf_tensor("Etmp", [128, 1024], F32))
                Escr = [s1b.enter_context(nc.sbuf_tensor(f"Escr{i}", [128, 1024], F32))
                        for i in range(2)]
                Escr2 = [s1b.enter_context(nc.sbuf_tensor(f"Escr2{i}", [128, 1024], F32))
                         for i in range(2)]
                kkt2 = [s1b.enter_context(nc.sbuf_tensor(f"kkt2{i}", [128, 2048], BF16))
                        for i in range(2)]
                kdt2 = [s1b.enter_context(nc.sbuf_tensor(f"kdt2{i}", [128, 2048], BF16))
                        for i in range(2)]
                t1a = s1b.enter_context(nc.sbuf_tensor("t1a", [16, 1024], BF16))
                wkg1s = s1b.enter_context(nc.sbuf_tensor("wkg1s", [128, 256], BF16))
                wkg2a = s1b.enter_context(nc.sbuf_tensor("wkg2a", [16, 1024], BF16))
                nql = s1b.enter_context(nc.sbuf_tensor("nql", [128, 64], F32))

                S.dma("gpsimd", v3(wkg1s, 256, 0, 16, 16, 1, 16),
                      wkg1_d.ap().rearrange("(k p) c -> p k c", p=128), w=["wkg1s"])
                S.dma("gpsimd", wkg2a[0:16, :], wkg2_d[:, :], w=["wkg2a"])
                for k in range(NKD):
                    for hf in range(2):
                        S.mm(PS[0:16, 512 * hf:512 * (hf + 1)],
                             wkg1s[:, 16 * k:16 * (k + 1)],
                             xTs(k, 512 * hf, 512 * (hf + 1)),
                             start=(k == 0), stop=(k == NKD - 1),
                             r=["wkg1s", "xT"], w=pres(512 * hf, 512))
                S.add("vector", lambda e: e.tensor_copy(t1a[0:16, :], PS[0:16, 0:1024]),
                      r=pres(0, 1024), w=["t1a"])
                for h in range(H):
                    reg = 1024 + 1024 * (h % 2)
                    for hf in range(2):
                        S.mm(PS[:, reg + 512 * hf:reg + 512 * (hf + 1)],
                             wkg2a[:, 128 * h:128 * (h + 1)],
                             t1a[:, 512 * hf:512 * (hf + 1)],
                             r=["wkg2a", "t1a"], w=pres(reg + 512 * hf, 512))
                    S.add("scalar", lambda e, h=h, reg=reg: e.activation(
                        Etmp[:, :], PS[:, reg:reg + 1024], AF.Exp, scale=-1.0,
                        bias=nbkg[:, h:h + 1]),
                        r=pres(reg, 1024) + ["c.nbkg"], w=["Etmp"])
                    er = Escr[h % 2]
                    err = f"Escr{h % 2}"
                    S.add("scalar", lambda e, er=er: e.activation(
                        er[:, :], Etmp[:, :], AF.Ln, bias=ones_c[:, :]),
                        r=["Etmp", "c.ones"], w=[err])
                    for c in range(CH):
                        S.add("vector", lambda e, h=h, c=c, er=er: e.tensor_tensor_scan(
                            Q[:, TOK * h + CT * c:TOK * h + CT * (c + 1)],
                            er[:, CT * c:CT * (c + 1)],
                            er[:, CT * c:CT * (c + 1)], 0.0, OP.add, OP.bypass),
                            r=[err], w=["Q"])
                    qlv = v3(Q, H * TOK, TOK * h + CT - 1, 1, 1, CT, CH)
                    S.add("vector", lambda e, h=h, v=qlv: e.reduce_sum(
                        stat[:, QLS + h:QLS + h + 1], v, AX), r=["Q"], w=["st.qls"])
                    S.add("vector", lambda e, h=h, v=qlv: e.tensor_scalar(
                        nql[:, 8 * h:8 * h + 8], v, -KS, None, OP.mult),
                        r=["Q"], w=["nql"])
                    S.add("scalar", lambda e, h=h, v=qlv: e.activation(
                        stat[:, DLS + 8 * h:DLS + 8 * h + 8], v, AF.Exp, scale=-KS),
                        r=["Q"], w=["st.dls"])
                tap("Q", Q[:, :], "Q")

                for pr in range(4):
                    reg = 2048 * (pr % 2)
                    for k in range(NKD):
                        wb = Wb[(pr * NKD + k) % 4]
                        wbr = f"Wb{(pr * NKD + k) % 4}"
                        S.dma("gpsimd", wb[:, 0:256],
                              wq_d[128 * k:128 * (k + 1), 256 * pr:256 * (pr + 1)],
                              w=[wbr])
                        for hl in range(2):
                            for hf in range(2):
                                off = reg + 1024 * hl + 512 * hf
                                S.mm(PS[:, off:off + 512],
                                     wb[:, 128 * hl:128 * (hl + 1)],
                                     xTs(k, 512 * hf, 512 * (hf + 1)),
                                     start=(k == 0), stop=(k == NKD - 1),
                                     r=[wbr, "xT"], w=pres(off, 512))
                    for hl in range(2):
                        h = 2 * pr + hl
                        er = Escr[hl]
                        err = f"Escr{hl}"
                        S.add("scalar", lambda e, h=h, er=er: e.activation(
                            er[:, :], Q[:, TOK * h:TOK * (h + 1)], AF.Exp, scale=-KS),
                            r=["Q"], w=[err])
                        S.add("vector", lambda e, h=h, hl=hl, reg=reg, er=er:
                              e.tensor_mul(qdT[:, TOK * h:TOK * (h + 1)],
                                           PS[:, reg + 1024 * hl:reg + 1024 * (hl + 1)],
                                           er[:, :]),
                              r=pres(reg + 1024 * hl, 1024) + [err], w=["qdT"])
                tap("qdT", qdT[:, :], "qdT")

                for pr in range(4):
                    reg = 2048 * (pr % 2)
                    oreg = 2048 * ((pr + 1) % 2)
                    kk = kkt2[pr % 2]
                    kkr = f"kkt2{pr % 2}"
                    kdt = kdt2[pr % 2]
                    kdtr = f"kdt2{pr % 2}"
                    for k in range(NKD):
                        wb = Wb[(pr * NKD + k) % 4]
                        wbr = f"Wb{(pr * NKD + k) % 4}"
                        S.dma("gpsimd", wb[:, 0:256],
                              wk_d[128 * k:128 * (k + 1), 256 * pr:256 * (pr + 1)],
                              w=[wbr])
                        for hl in range(2):
                            for hf in range(2):
                                off = reg + 1024 * hl + 512 * hf
                                S.mm(PS[:, off:off + 512],
                                     wb[:, 128 * hl:128 * (hl + 1)],
                                     xTs(k, 512 * hf, 512 * (hf + 1)),
                                     start=(k == 0), stop=(k == NKD - 1),
                                     r=[wbr, "xT"], w=pres(off, 512))
                    for hl in range(2):
                        h = 2 * pr + hl
                        er = Escr[hl]
                        err = f"Escr{hl}"
                        er2 = Escr2[hl]
                        err2 = f"Escr2{hl}"
                        S.add("scalar", lambda e, h=h, er=er: e.activation(
                            er[:, :], Q[:, TOK * h:TOK * (h + 1)], AF.Exp, scale=KS),
                            r=["Q"], w=[err])
                        S.add("vector", lambda e, hl=hl, reg=reg, kk=kk, er=er:
                              e.scalar_tensor_tensor(
                                  kk[:, 1024 * hl:1024 * (hl + 1)],
                                  PS[:, reg + 1024 * hl:reg + 1024 * (hl + 1)],
                                  KS, er[:, :], OP.mult, OP.mult),
                              r=pres(reg + 1024 * hl, 1024) + [err], w=[kkr])
                        for c in range(CH):
                            S.add("scalar", lambda e, h=h, c=c, er2=er2: e.activation(
                                er2[:, CT * c:CT * (c + 1)],
                                Q[:, TOK * h + CT * c:TOK * h + CT * (c + 1)],
                                AF.Exp, scale=KS, bias=nql[:, 8 * h + c:8 * h + c + 1]),
                                r=["Q", "nql"], w=[err2])
                        S.add("vector", lambda e, hl=hl, reg=reg, kdt=kdt, er2=er2:
                              e.scalar_tensor_tensor(
                                  kdt[:, 1024 * hl:1024 * (hl + 1)],
                                  PS[:, reg + 1024 * hl:reg + 1024 * (hl + 1)],
                                  KS, er2[:, :], OP.mult, OP.mult),
                              r=pres(reg + 1024 * hl, 1024) + [err2], w=[kdtr])
                    for hl in range(2):
                        h = 2 * pr + hl
                        for c in range(CH):
                            off = oreg + 1024 * hl + 128 * c
                            S.mm(PS[:, off:off + 128],
                                 kk[:, 1024 * hl + 128 * c:1024 * hl + 128 * (c + 1)],
                                 qdT[:, TOK * h + CT * c:TOK * h + CT * (c + 1)],
                                 r=[kkr, "qdT"], w=pres(off, 128))
                        atv = v3(AT, CH * H * 128, 128 * h, 1024, CH, 1, 128)
                        psv = v3(PS, 4096, oreg + 1024 * hl, 128, CH, 1, 128)
                        t8v = v3(tri8, 1024, 0, 128, CH, 1, 128)
                        S.add("vector", lambda e, a=atv, b=psv, t=t8v:
                              e.tensor_mul(a, b, t),
                              r=pres(oreg + 1024 * hl, 1024) + ["c.tri8"], w=["AT"])
                    for hl in range(2):
                        for c in range(CH):
                            off = oreg + 1024 * hl + 128 * c
                            S.mm(PS[:, off:off + 128],
                                 kdt[:, 1024 * hl + 128 * c:1024 * hl + 128 * (c + 1)],
                                 idn[:, :],
                                 r=[kdtr, "c.idn"], w=pres(off, 128))
                    for hl in range(2):
                        h = 2 * pr + hl
                        kdv = v3(kd, CH * H * 128, 128 * h, 1024, CH, 1, 128)
                        psv = v3(PS, 4096, oreg + 1024 * hl, 128, CH, 1, 128)
                        S.add("vector", lambda e, a=kdv, b=psv: e.tensor_copy(a, b),
                              r=pres(oreg + 1024 * hl, 1024), w=["kd"])
                tap("kd", kd[:, :], "kd")
                tap("AT", AT[:, :], "AT")
                S.emit(nc, sems)

            S.section(30)
            if True:
                fstg = [s1.enter_context(nc.sbuf_tensor(f"fstgA{i}", [128, 2048], F32))
                        for i in range(2)]
                fjunk2 = [s1.enter_context(nc.sbuf_tensor(f"fjunkA{i}", [128, 2048],
                                                          BF16)) for i in range(2)]
                fj22 = [s1.enter_context(nc.sbuf_tensor("fj2", [128, 2048], BF16))] * 2
                fjunk = fjunk2[0]
                S.add("vector", lambda e: e.memset(wsum[:, :], 0.0), w=["wsum"])
                tiles = []
                for w_i, wd in enumerate((w1s_d, w2s_d, w3s_d)):
                    R, Cc = wd.shape
                    ti = 0
                    for rr0 in range(0, R, 128):
                        rr = min(128, R - rr0)
                        for c0 in range(0, Cc, 2048):
                            cw = min(2048, Cc - c0)
                            tiles.append((w_i, wd, rr0, rr, c0, cw, ti))
                            ti += 1
                for tn, (w_i, wd, rr0, rr, c0, cw, ti) in enumerate(tiles):
                    fb = fstg[tn % 2]
                    fr = f"fstg{tn % 2}"
                    S.dma("sync", fb[0:rr, 0:cw], wd[rr0:rr0 + rr, c0:c0 + cw], w=[fr])
                    S.add("scalar", lambda e, rr=rr, cw=cw, w_i=w_i, ti=ti, fb=fb:
                          e.activation(fjunk[0:rr, 0:cw], fb[0:rr, 0:cw], AF.Abs,
                                       accum_out=wsum[0:rr, 6 * w_i + ti:6 * w_i + ti + 1]),
                          r=[fr], w=["fjunk0", "wsum"])
                for w_i in range(3):
                    S.add("scalar", lambda e, w_i=w_i:
                          e.activation(fjunk[:, 0:6], wsum[:, 6 * w_i:6 * w_i + 6],
                                       AF.Identity,
                                       accum_out=wsum[:, 18 + w_i:19 + w_i]),
                          r=["wsum"], w=["fjunk0", "wsum"])
            S.section(31)
            if True:
                V = ox1
                vtmp = [s1.enter_context(nc.sbuf_tensor(f"vtmp{i}", [128, 2048], BF16))
                        for i in range(2)]
                L = s1.enter_context(nc.sbuf_tensor("L", [128, 2048], F32))
                Sst = s1.enter_context(nc.sbuf_tensor("Sst", [128, 2048], F32))
                Sb = [s1.enter_context(nc.sbuf_tensor(f"Sb{i}", [128, 2048], BF16))
                      for i in range(2)]
                AGl = [s1.enter_context(nc.sbuf_tensor(f"AGl{i}", [128, 2064], F32))
                       for i in range(2)]

                for vp in range(8):
                    reg = 2048 * (vp % 2)
                    vt = vtmp[vp % 2]
                    vtr = f"vtmp{vp % 2}"
                    for k in range(NKD):
                        wb = Wb[(vp * NKD + k) % 4]
                        wbr = f"Wb{(vp * NKD + k) % 4}"
                        S.dma("gpsimd", wb[:, 0:256],
                              wv_d[128 * k:128 * (k + 1), 256 * vp:256 * (vp + 1)],
                              w=[wbr])
                        for mb in range(2):
                            for hf in range(2):
                                off = reg + 1024 * mb + 512 * hf
                                S.mm(PS[:, off:off + 512],
                                     wb[:, 128 * mb:128 * (mb + 1)],
                                     xTs(k, 512 * hf, 512 * (hf + 1)),
                                     start=(k == 0), stop=(k == NKD - 1),
                                     r=[wbr, "xT"], w=pres(off, 512))
                    S.add("vector", lambda e, reg=reg, vt=vt:
                          e.tensor_copy(vt[:, :], PS[:, reg:reg + 2048]),
                          r=pres(reg, 2048), w=[vtr])
                    for mb in range(2):
                        for c in range(CH):
                            off = reg + 1024 * mb + 128 * c
                            S.mm(PS[:, off:off + 128],
                                 vt[:, 1024 * mb + 128 * c:1024 * mb + 128 * (c + 1)],
                                 idn[:, :],
                                 r=[vtr, "c.idn"], w=pres(off, 128))
                    for mb in range(2):
                        fb = 2 * vp + mb
                        vv = v3(V, CH * D, 128 * fb, D, CH, 1, 128)
                        psv = v3(PS, 4096, reg + 1024 * mb, 128, CH, 1, 128)
                        S.add("vector", lambda e, a=vv, b=psv: e.tensor_copy(a, b),
                              r=pres(reg + 1024 * mb, 1024), w=["ox1"])
                    if vp == 2:
                        S.mm(PS[0:1, 0:3], ones_c[:, :], wsum[:, 18:21],
                             r=["c.ones", "wsum"], w=pres(0, 3))
                        S.add("vector", lambda e: e.tensor_copy(
                            msc[0:1, 0:3], PS[0:1, 0:3]), r=pres(0, 3), w=["msc"])
                        S.add("vector", lambda e: e.memset(msc[0:1, 3:4], 0.0),
                              r=["msc"], w=["msc"])
                        S.dma("sync", ari[:, :], msc[0:1, 0:4], r=["msc"], w=["d.ari"])
                tap("V", V[:, :], "ox1")

                S.section(32)
                S.add("vector", lambda e: e.memset(L[:, :], 0.0), w=["L"])
                for c in range(CH):
                    reg = 2048 * (c % 2)
                    for h in range(H):
                        off = reg + 256 * h
                        S.mm(PS[:, off:off + 256],
                             kd[:, 1024 * c + 128 * h:1024 * c + 128 * (h + 1)],
                             V[:, D * c + DV * h:D * c + DV * (h + 1)],
                             r=["kd", "ox1"], w=pres(off, 256))
                    for h in range(H):
                        off = reg + 256 * h
                        S.add("vector", lambda e, c=c, h=h, off=off:
                              e.scalar_tensor_tensor(
                                  L[:, 256 * h:256 * (h + 1)], L[:, 256 * h:256 * (h + 1)],
                                  stat[:, DLS + 8 * h + c:DLS + 8 * h + c + 1],
                                  PS[:, off:off + 256], OP.mult, OP.add),
                              r=["L", "st.dls"] + pres(off, 256), w=["L"])
                tap("L", L[:, :], "L")
                S.add("gpsimd", lambda e: e.collective_compute(
                    "AllReduce", OP.add, replica_groups=[list(range(NC))],
                    ins=[ari.ap()], outs=[aro.ap()]),
                    cls="cc.ar", r=["d.ari"], w=["d.aro"])
                S.dma("scalar", msc[0:1, 4:8], aro[:, :], r=["d.aro"], w=["msc"])
                S.section(33)
                S.add("scalar", lambda e: e.activation(
                    stc(DCORE, 8), stc(QLS, 8), AF.Exp, scale=-KS),
                    r=["st.qls"], w=["st.dc"])
                S.dma("sync", agi[:, 0:2048], L[:, :], r=["L"], w=["d.agi"])
                S.dma("sync", agi[:, 2048:2056], stc(DCORE, 8), r=["st.dc"], w=["d.agi"])
                S.add("vector", lambda e: e.memset(stc(DCORE + 8, 8), 0.0), w=["st.dc8"])
                S.dma("sync", agi[:, 2056:2064], stc(DCORE + 8, 8), r=["st.dc8"],
                      w=["d.agi"])
                S.add("gpsimd", lambda e: e.collective_compute(
                    "AllGather", OP.bypass, replica_groups=[list(range(NC))],
                    ins=[agi.ap()], outs=[ago.ap()]),
                    cls="cc.st", r=["d.agi"], w=["d.ago"])
                S.add("vector", lambda e: e.tensor_scalar(
                    msc[0:1, 8:11], msc[0:1, 4:7], NWINV, EPS, OP.mult, OP.max),
                    r=["msc"], w=["msc"])
                S.add("vector", lambda e: e.tensor_scalar(
                    msc[0:1, 12:15], msc[0:1, 8:11], 0.5, None, OP.mult),
                    r=["msc"], w=["msc"])
                S.mm(PS[:, 0:3], onesr[:, :], msc[0:1, 12:15],
                     r=["c.onesr", "msc"], w=pres(0, 3))
                S.add("vector", lambda e: e.tensor_copy(stc(THR, 3), PS[:, 0:3]),
                      r=pres(0, 3), w=["st.thr"])
                S.add("vector", lambda e: e.tensor_scalar(
                    stc(NTHR, 3), stc(THR, 3), -1.0, None, OP.mult),
                    r=["st.thr"], w=["st.nthr"])

                tern_tiles = []
                for w_i, wd in ((2, w3s_d), (0, w1s_d), (1, w2s_d)):
                    R, Cc = wd.shape
                    ti = 0
                    for rr0 in range(0, R, 128):
                        rr = min(128, R - rr0)
                        for c0 in range(0, Cc, 2048):
                            cw = min(2048, Cc - c0)
                            tern_tiles.append((w_i, wd, rr0, rr, c0, cw, ti))
                            ti += 1
                NTT = len(tern_tiles)
                NT1 = sum(1 for t in tern_tiles if t[0] != 1)

                def tern_load(tn):
                    if tn >= NTT:
                        return
                    w_i, wd, rr0, rr, c0, cw, ti = tern_tiles[tn]
                    fb = fstg[tn % 2]
                    S.dma("sync", fb[0:rr, 0:cw], wd[rr0:rr0 + rr, c0:c0 + cw],
                          w=[f"fstg{tn % 2}"])

                def tern_ops(tn):
                    w_i, wd, rr0, rr, c0, cw, ti = tern_tiles[tn]
                    td = (t1i, t2i, t3i)[w_i]
                    fb = fstg[tn % 2]
                    fr = f"fstg{tn % 2}"
                    fjb = fjunk2[tn % 2]
                    fjr = f"fjunk{tn % 2}"
                    fj2b = fj22[tn % 2]
                    fj2r = f"fjj{tn % 2}"
                    S.add("vector", lambda e, rr=rr, cw=cw, w_i=w_i, fb=fb, fjb=fjb:
                          e.tensor_scalar(fjb[0:rr, 0:cw], fb[0:rr, 0:cw],
                                          stat[0:rr, THR + w_i:THR + w_i + 1],
                                          None, OP.is_ge),
                          r=[fr, "st.thr"], w=[fjr])
                    S.add("vector", lambda e, rr=rr, cw=cw, w_i=w_i, fb=fb, fj2b=fj2b:
                          e.tensor_scalar(fj2b[0:rr, 0:cw], fb[0:rr, 0:cw],
                                          stat[0:rr, NTHR + w_i:NTHR + w_i + 1],
                                          None, OP.is_le),
                          r=[fr, "st.nthr"], w=[fj2r])
                    S.add("vector", lambda e, rr=rr, cw=cw, fjb=fjb, fj2b=fj2b:
                          e.tensor_sub(fjb[0:rr, 0:cw], fjb[0:rr, 0:cw],
                                       fj2b[0:rr, 0:cw]),
                          r=[fjr, fj2r], w=[fjr])
                    S.dma("sync", td[rr0:rr0 + rr, c0:c0 + cw], fjb[0:rr, 0:cw],
                          r=[fjr], w=[f"d.t{w_i}i"])
                    tern_load(tn + 2)

                tern_load(0)
                tern_load(1)

                S.section(34)
                S.add("vector", lambda e: e.memset(Sst[:, :], 0.0),
                      r=["Sst"], w=["Sst"])
                S.add("scalar", lambda e: e.copy(Sb[0][:, :], Sst[:, :]),
                      r=["Sst"], w=["Sb0"])

                def cmb(j):
                    if j == 0:
                        S.add("vector", lambda e: e.memset(L[:, :], 0.0),
                              r=["L"], w=["L"])
                    ag = AGl[j % 2]
                    agr = f"AGl{j % 2}"
                    S.dma("scalar", ag[:, :], ago[128 * j:128 * (j + 1), :],
                          r=["d.ago"], w=[agr])
                    S.add("vector", lambda e, j=j, ag=ag: e.tensor_scalar(
                        stc(DPC, 8), ag[:, 2048:2056],
                        scanm[:, j:j + 1], scanm[:, 8 + j:9 + j], OP.mult, OP.add),
                        r=[agr, "c.scanm"], w=["st.dpc"])
                    for h in range(H):
                        S.add("vector", lambda e, h=h: e.tensor_scalar(
                            L[:, 256 * h:256 * (h + 1)], L[:, 256 * h:256 * (h + 1)],
                            stat[:, DPC + h:DPC + h + 1], None, OP.mult),
                            r=["L", "st.dpc"], w=["L"])
                        S.add("vector", lambda e, j=j, h=h, ag=ag:
                              e.scalar_tensor_tensor(
                                  L[:, 256 * h:256 * (h + 1)],
                                  ag[:, 256 * h:256 * (h + 1)],
                                  scanm[:, j:j + 1], L[:, 256 * h:256 * (h + 1)],
                                  OP.mult, OP.add),
                              r=[agr, "c.scanm", "L"], w=["L"])

                for c in range(CH):
                    reg = 0
                    greg = 2048
                    sb = Sb[c % 2]
                    sbr = f"Sb{c % 2}"
                    sbn = Sb[(c + 1) % 2]
                    sbnr = f"Sb{(c + 1) % 2}"
                    for h in range(H):
                        off = reg + 256 * h
                        S.mm(PS[:, off:off + 256],
                             AT[:, 1024 * c + 128 * h:1024 * c + 128 * (h + 1)],
                             V[:, D * c + DV * h:D * c + DV * (h + 1)],
                             start=True, stop=False,
                             r=["AT", "ox1"], w=pres(off, 256))
                        S.mm(PS[:, off:off + 256],
                             qdT[:, TOK * h + CT * c:TOK * h + CT * (c + 1)],
                             sb[:, 256 * h:256 * (h + 1)],
                             start=False, stop=True,
                             r=["qdT", sbr], w=pres(off, 256))
                    for h in range(H):
                        off = greg + 256 * h
                        S.mm(PS[:, off:off + 256],
                             kd[:, 1024 * c + 128 * h:1024 * c + 128 * (h + 1)],
                             V[:, D * c + DV * h:D * c + DV * (h + 1)],
                             r=["kd", "ox1"], w=pres(off, 256))
                    for h in range(H):
                        off = greg + 256 * h
                        S.add("vector", lambda e, c=c, h=h, off=off:
                              e.scalar_tensor_tensor(
                                  Sst[:, 256 * h:256 * (h + 1)],
                                  Sst[:, 256 * h:256 * (h + 1)],
                                  stat[:, DLS + 8 * h + c:DLS + 8 * h + c + 1],
                                  PS[:, off:off + 256], OP.mult, OP.add),
                              r=["Sst", "st.dls"] + pres(off, 256), w=["Sst"])
                    if c < CH - 1:
                        S.add("scalar", lambda e, sbn=sbn: e.copy(sbn[:, :], Sst[:, :]),
                              r=["Sst"], w=[sbnr])
                    S.add("scalar", lambda e, c=c: e.copy(
                        ox1[:, D * c:D * (c + 1)], PS[:, 0:2048]),
                        r=pres(0, 2048), w=["ox1"])
                    if c >= 3:
                        cmb(c - 3)
                    if c == CH - 1:
                        cmb(5)
                        cmb(6)
                    for tn in range(-(-NT1 * c // CH), -(-NT1 * (c + 1) // CH)):
                        tern_ops(tn)

                tap("Sst", L[:, :], "L")

                for c in range(CH):
                    S.add("scalar", lambda e: e.copy(Sb[0][:, :], L[:, :]),
                          r=["L"], w=["Sb0"])
                    for h in range(H):
                        off = 256 * h
                        S.mm(PS[:, off:off + 256],
                             qdT[:, TOK * h + CT * c:TOK * h + CT * (c + 1)],
                             Sb[0][:, 256 * h:256 * (h + 1)],
                             r=["qdT", "Sb0"], w=pres(off, 256))
                    S.add("vector", lambda e, c=c: e.tensor_add(
                        ox1[:, D * c:D * (c + 1)], ox1[:, D * c:D * (c + 1)],
                        PS[:, 0:2048]),
                        r=["ox1"] + pres(0, 2048), w=["ox1"])
                    if c < CH - 1:
                        for h in range(H):
                            S.add("vector", lambda e, c=c, h=h: e.tensor_scalar(
                                L[:, 256 * h:256 * (h + 1)],
                                L[:, 256 * h:256 * (h + 1)],
                                stat[:, DLS + 8 * h + c:DLS + 8 * h + c + 1],
                                None, OP.mult),
                                r=["L", "st.dls"], w=["L"])
                    ov3 = v3(ox1, CH * D, D * c, 256, 8, 1, 256)
                    S.add("vector", lambda e, c=c, o=ov3: e.reduce_sum(
                        stat[:, OMU + 8 * c:OMU + 8 * c + 8], o, AX),
                        r=["ox1"], w=["st.omu"])
                    for h in range(H):
                        S.add("scalar", lambda e, c=c, h=h: e.activation(
                            vtmp[0][:, 0:256],
                            ox1[:, D * c + DV * h:D * c + DV * (h + 1)], AF.Square,
                            accum_out=stat[:, OSSQ + 8 * c + h:OSSQ + 8 * c + h + 1]),
                            r=["ox1"], w=["vtmp0", "st.ossq"])
                tap("orw", ox1[:, :], "ox1")

            S.section(35)
            S.add("vector", lambda e: e.tensor_scalar(
                stc(OMU, 64), stc(OMU, 64), 1.0 / DV, None, OP.mult),
                r=["st.omu"], w=["st.omu"])
            S.add("vector", lambda e: e.tensor_scalar(
                stc(OSSQ, 64), stc(OSSQ, 64), 1.0 / DV, None, OP.mult),
                r=["st.ossq"], w=["st.ossq"])
            S.add("vector", lambda e: e.tensor_mul(
                stc(SCR64, 64), stc(OMU, 64), stc(OMU, 64)),
                r=["st.omu"], w=["st.s64"])
            S.add("vector", lambda e: e.tensor_sub(
                stc(OSSQ, 64), stc(OSSQ, 64), stc(SCR64, 64)),
                r=["st.ossq", "st.s64"], w=["st.ossq"])
            S.add("scalar", lambda e: e.activation(
                stc(SCR64, 64), stc(OSSQ, 64), AF.Sqrt, bias=eps_c[:, :]),
                r=["st.ossq", "c.eps"], w=["st.s64"])
            S.add("vector", lambda e: e.reciprocal(stc(ORS, 64), stc(SCR64, 64)),
                  r=["st.s64"], w=["st.ors"])
            for c in range(CH):
                for h in range(H):
                    S.add("vector", lambda e, c=c, h=h: e.tensor_scalar(
                        ox1[:, D * c + DV * h:D * c + DV * (h + 1)],
                        ox1[:, D * c + DV * h:D * c + DV * (h + 1)],
                        stat[:, OMU + 8 * c + h:OMU + 8 * c + h + 1],
                        stat[:, ORS + 8 * c + h:ORS + 8 * c + h + 1],
                        OP.subtract, OP.mult),
                        r=["ox1", "st.omu", "st.ors"], w=["ox1"])
            for tn in range(NT1, NTT):
                tern_ops(tn)

            S.section(40)
            if True:
                S.add("gpsimd", lambda e: e.collective_compute(
                    "AllGather", OP.bypass, replica_groups=[list(range(NC))],
                    ins=[t3i.ap()], outs=[t3o.ap()]),
                    cls="cc.t2", r=["d.t2i"], w=["d.t2o"])

                S.emit(nc, sems)

        S.section(50)
        s3 = ExitStack()
        with s3:
            ooT = s3.enter_context(nc.sbuf_tensor("ooT", [128, NKD * TOK], BF16))
            gt2 = [s3.enter_context(nc.sbuf_tensor(f"gt2{i}", [128, 2048], BF16))
                   for i in range(2)]
            att2 = [s3.enter_context(nc.sbuf_tensor(f"att2{i}", [128, 2048], BF16))
                    for i in range(2)]
            xs = [s3.enter_context(nc.sbuf_tensor(f"xs{i}", [128, 1024], F32))
                  for i in range(2)]

            ox1v = ox1.ap().rearrange("p (c f) -> p c f", c=CH)
            S.add("gpsimd", lambda e: e.collective_compute(
                "AllGather", OP.bypass, replica_groups=[list(range(NC))],
                ins=[t1i.ap()], outs=[t1o.ap()]),
                cls="cc.t0", r=["d.t0i"], w=["d.t0o"])
            tap("oo_pre", ox1[:, :], "ox1") if False else None
            for pg in range(8):
                reg = 2048 * (pg % 2)
                gt = gt2[pg % 2]
                gtr = f"gt2{pg % 2}"
                for k in range(NKD):
                    wb = Wb[(pg * NKD + k) % 4]
                    wbr = f"Wb{(pg * NKD + k) % 4}"
                    S.dma("gpsimd", wb[:, 0:256],
                          wg_d[128 * k:128 * (k + 1), 256 * pg:256 * (pg + 1)],
                          w=[wbr])
                    for mb in range(2):
                        for hf in range(2):
                            off = reg + 1024 * mb + 512 * hf
                            S.mm(PS[:, off:off + 512],
                                 wb[:, 128 * mb:128 * (mb + 1)],
                                 xTs(k, 512 * hf, 512 * (hf + 1)),
                                 start=(k == 0), stop=(k == NKD - 1),
                                 r=[wbr, "xT"], w=pres(off, 512))
                for mb in range(2):
                    fb = 2 * pg + mb
                    off = reg + 1024 * mb
                    S.add("scalar", lambda e, fb=fb, off=off, gt=gt, mb=mb:
                          e.activation(gt[:, 1024 * mb:1024 * (mb + 1)],
                                       PS[:, off:off + 1024],
                                       AF.Silu, bias=bgT[:, fb:fb + 1]),
                          r=pres(off, 1024) + ["c.bgT"], w=[gtr])
                for mb in range(2):
                    for c in range(CH):
                        off = reg + 1024 * mb + 128 * c
                        S.mm(PS[:, off:off + 128],
                             gt[:, 1024 * mb + 128 * c:1024 * mb + 128 * (c + 1)],
                             idn[:, :], r=[gtr, "c.idn"], w=pres(off, 128))
                for mb in range(2):
                    fb = 2 * pg + mb
                    ov = v3(ox1, CH * D, 128 * fb, D, CH, 1, 128)
                    psv = v3(PS, 4096, reg + 1024 * mb, 128, CH, 1, 128)
                    S.add("vector", lambda e, a=ov, b=psv: e.tensor_mul(a, b, a),
                          r=pres(reg + 1024 * mb, 1024) + ["ox1"], w=["ox1"])
            tap("oo", ox1[:, :], "ox1")

            for kf in range(NKD):
                reg = 2048 * (kf % 2)
                for c in range(CH):
                    off = reg + 128 * c
                    S.mm(PS[:, off:off + 128],
                         ox1[:, D * c + 128 * kf:D * c + 128 * (kf + 1)], idn[:, :],
                         r=["ox1", "c.idn"], w=pres(off, 128))
                S.add("vector", lambda e, kf=kf, reg=reg: e.tensor_copy(
                    ooT[:, TOK * kf:TOK * (kf + 1)], PS[:, reg:reg + 1024]),
                    r=pres(reg, 1024), w=["ooT"])

            for po in range(8):
                reg = 2048 * (po % 2)
                at = att2[po % 2]
                atr = f"att2{po % 2}"
                for k in range(NKD):
                    wb = Wb[(po * NKD + k) % 4]
                    wbr = f"Wb{(po * NKD + k) % 4}"
                    S.dma("gpsimd", wb[:, 0:256],
                          wo_d[128 * k:128 * (k + 1), 256 * po:256 * (po + 1)],
                          w=[wbr])
                    for mb in range(2):
                        for hf in range(2):
                            off = reg + 1024 * mb + 512 * hf
                            S.mm(PS[:, off:off + 512],
                                 wb[:, 128 * mb:128 * (mb + 1)],
                                 ooT[:, TOK * k + 512 * hf:TOK * k + 512 * (hf + 1)],
                                 start=(k == 0), stop=(k == NKD - 1),
                                 r=[wbr, "ooT"], w=pres(off, 512))
                S.add("vector", lambda e, reg=reg, at=at:
                      e.tensor_copy(at[:, :], PS[:, reg:reg + 2048]),
                      r=pres(reg, 2048), w=[atr])
                for mb in range(2):
                    for c in range(CH):
                        off = reg + 1024 * mb + 128 * c
                        S.mm(PS[:, off:off + 128],
                             at[:, 1024 * mb + 128 * c:1024 * mb + 128 * (c + 1)],
                             idn[:, :], r=[atr, "c.idn"], w=pres(off, 128))
                for mb in range(2):
                    ob = 2 * po + mb
                    xsb = xs[mb]
                    xsr = f"xs{mb}"
                    xin = bass.AP(x_d, 128 * ob, [[D, 128], [128 * D, CH], [1, 128]])
                    xsv = v3(xsb, 1024, 0, 128, CH, 1, 128)
                    S.dma("sync", xsv, xin, w=[xsr])
                    ov = v3(ox1, CH * D, 128 * ob, D, CH, 1, 128)
                    psv = v3(PS, 4096, reg + 1024 * mb, 128, CH, 1, 128)
                    S.add("vector", lambda e, o=ov, p=psv, xv=xsv:
                          e.scalar_tensor_tensor(o, p, 1.0, xv, OP.mult, OP.add),
                          r=pres(reg + 1024 * mb, 1024) + [xsr, "ox1"], w=["ox1"])
            S.add("gpsimd", lambda e: e.collective_compute(
                "AllGather", OP.bypass, replica_groups=[list(range(NC))],
                ins=[t2i.ap()], outs=[t2o.ap()]),
                cls="cc.t1", r=["d.t1i"], w=["d.t1o"])

            for c in range(CH):
                S.add("vector", lambda e, c=c: e.reduce_sum(
                    stat[:, XMU + c:XMU + c + 1], ox1[:, D * c:D * (c + 1)], AX),
                    r=["ox1"], w=["st.xmu"])
                S.add("scalar", lambda e, c=c: e.activation(
                    att2[0][:, :], ox1[:, D * c:D * (c + 1)], AF.Square,
                    accum_out=stat[:, XSSQ + c:XSSQ + c + 1]),
                    r=["ox1"], w=["att20", "st.xssq"])
            S.add("vector", lambda e: e.tensor_scalar(
                stc(XMU, 8), stc(XMU, 8), 1.0 / D, None, OP.mult),
                r=["st.xmu"], w=["st.xmu"])
            S.add("vector", lambda e: e.tensor_scalar(
                stc(XSSQ, 8), stc(XSSQ, 8), 1.0 / D, None, OP.mult),
                r=["st.xssq"], w=["st.xssq"])
            S.add("vector", lambda e: e.tensor_mul(
                stc(SCR8, 8), stc(XMU, 8), stc(XMU, 8)), r=["st.xmu"], w=["st.s8"])
            S.add("vector", lambda e: e.tensor_sub(
                stc(XSSQ, 8), stc(XSSQ, 8), stc(SCR8, 8)),
                r=["st.xssq", "st.s8"], w=["st.xssq"])
            S.add("scalar", lambda e: e.activation(
                stc(SCR8, 8), stc(XSSQ, 8), AF.Sqrt, bias=eps_c[:, :]),
                r=["st.xssq", "c.eps"], w=["st.s8"])
            S.add("vector", lambda e: e.reciprocal(stc(XRS, 8), stc(SCR8, 8)),
                  r=["st.s8"], w=["st.xrs"])
            for c in range(CH):
                S.add("vector", lambda e, c=c: e.tensor_scalar(
                    ox1[:, D * c:D * (c + 1)], ox1[:, D * c:D * (c + 1)],
                    stat[:, XMU + c:XMU + c + 1], stat[:, XRS + c:XRS + c + 1],
                    OP.subtract, OP.mult),
                    r=["ox1", "st.xmu", "st.xrs"], w=["ox1"])
            tap("x1", ox1[:, :], "ox1")
            S.emit(nc, sems)

        S.section(60)
        s4 = ExitStack()
        with s4:
            rscr = s4.enter_context(nc.sbuf_tensor("rscr", [128, 2048], F32))
            f1scr = s4.enter_context(nc.sbuf_tensor("f1scr", [128, 1024], F32))
            pfscr = s4.enter_context(nc.sbuf_tensor("pfscr", [128, 1024], F32))
            ri = s4.enter_context(nc.sbuf_tensor("ri", [128, 1024], I32))
            zbf = s4.enter_context(nc.sbuf_tensor("zbf", [128, 2048], BF16))
            B1 = s4.enter_context(nc.sbuf_tensor("B1", [128, 1024], F32))
            B3 = s4.enter_context(nc.sbuf_tensor("B3", [128, 1024], F32))
            Bh = s4.enter_context(nc.sbuf_tensor("Bh", [128, 1024], F32))
            Bo = s4.enter_context(nc.sbuf_tensor("Bo", [128, 1024], F32))
            hbb = [s4.enter_context(nc.sbuf_tensor(f"hbb{i}", [128, 1024], BF16))
                   for i in range(2)]
            zhn = [s4.enter_context(nc.sbuf_tensor(f"zhn{i}", [128, 1024], BF16))
                   for i in range(2)]

            def round_neg(src_off, dst_bf16, w, dst_res):
                src = rscr[:, src_off:src_off + w]
                S.add("vector", lambda e, s_=src, w=w: e.tensor_copy(ri[:, 0:w], s_),
                      r=["rscr"], w=["ri"])
                S.add("vector", lambda e, w=w: e.tensor_copy(f1scr[:, 0:w], ri[:, 0:w]),
                      r=["ri"], w=["f1scr"])
                S.add("vector", lambda e, s_=src, w=w: e.tensor_sub(
                    pfscr[:, 0:w], s_, f1scr[:, 0:w]),
                    r=["rscr", "f1scr"], w=["pfscr"])
                S.add("vector", lambda e, w=w: e.scalar_tensor_tensor(
                    f1scr[:, 0:w], pfscr[:, 0:w], 0.5, f1scr[:, 0:w], OP.is_ge, OP.add),
                    r=["pfscr", "f1scr"], w=["f1scr"])
                S.add("vector", lambda e, w=w, d=dst_bf16: e.scalar_tensor_tensor(
                    d, pfscr[:, 0:w], -0.5, f1scr[:, 0:w], OP.is_le, OP.subtract),
                    r=["pfscr", "f1scr"], w=[dst_res])

            for c in range(CH):
                S.add("vector", lambda e, c=c: e.reduce_max(
                    stat[:, SX + c:SX + c + 1], ox1[:, D * c:D * (c + 1)], AX,
                    apply_absolute_value=True), r=["ox1"], w=["st.sx"])
            S.add("vector", lambda e: e.tensor_scalar(
                stc(CMX, 8), stc(SX, 8), EPS, None, OP.max),
                r=["st.sx"], w=["st.cmx"])
            S.add("vector", lambda e: e.reciprocal(stc(SX, 8), stc(CMX, 8)),
                  r=["st.cmx"], w=["st.sx"])
            S.add("vector", lambda e: e.tensor_scalar(
                stc(SX, 8), stc(SX, 8), 127.0, None, OP.mult),
                r=["st.sx"], w=["st.sx"])
            for c in range(CH):
                S.add("vector", lambda e, c=c: e.tensor_scalar(
                    rscr[:, :], ox1[:, D * c:D * (c + 1)],
                    stat[:, SX + c:SX + c + 1], None, OP.mult),
                    r=["ox1", "st.sx", "rscr"], w=["rscr"])
                round_neg(0, zbf[:, 0:1024], 1024, "zbf")
                round_neg(1024, zbf[:, 1024:2048], 1024, "zbf")
                transpose_tiles([zbf[:, 128 * t:128 * (t + 1)] for t in range(NKD)],
                                "zbf", v3(xT, NQ, CT * c, TOK, NKD, 1, CT), "xT",
                                ps_off=2048 * (c % 2))
            tap("zxT", xT[:, :], "xT")

            S.add("tensor", lambda e: e.transpose(PS[0:8, 0:128], stc(CMX, 8),
                                                  idnf[:, :]),
                  r=["st.cmx", "c.idnf"], w=pres(0, 128))
            S.add("vector", lambda e: e.tensor_copy(rtmp[:, :], PS[0:8, 0:128]),
                  r=pres(0, 128), w=["rtmp"])
            S.dma("sync", rowb[:, :], rtmp[0:8, 0:128], r=["rtmp"], w=["d.rowb"])
            S.dma("sync", row1[0:1, :], rowb.ap().rearrange("a b -> (a b)").rearrange("(o n) -> o n", o=1),
                  r=["d.rowb"], w=["row1"])
            for (Bt, Btr, mi) in ((B1, "B1", 8), (B3, "B3", 10)):
                S.add("vector", lambda e, mi=mi: e.tensor_scalar(
                    row2[0:1, :], row1[0:1, :], msc[0:1, mi:mi + 1],
                    -1.0 / 127.0, OP.mult, OP.mult),
                    r=["row1", "msc", "row2"], w=["row2"])
                for hf in range(2):
                    S.mm(PS[:, 512 * hf:512 * (hf + 1)], onesr[:, :],
                         row2[0:1, 512 * hf:512 * (hf + 1)],
                         r=["c.onesr", "row2"], w=pres(512 * hf, 512))
                S.add("vector", lambda e, b=Bt: e.tensor_copy(b[:, :], PS[:, 0:1024]),
                      r=pres(0, 1024), w=[Btr])

            s4a = ExitStack()
            with s4a:
                Mx = s4a.enter_context(nc.sbuf_tensor("Mx", [128, 1024], F32))
                Mn = s4a.enter_context(nc.sbuf_tensor("Mn", [128, 1024], F32))
                T1w = [s4a.enter_context(
                    nc.sbuf_tensor(f"T1w{i}", [128, NKD * WIN * 128], BF16))
                    for i in range(2)]
                T3w = [s4a.enter_context(
                    nc.sbuf_tensor(f"T3w{i}", [128, NKD * WIN * 128], BF16))
                    for i in range(2)]
                S.add("vector", lambda e: e.memset(Mx[:, :], 0.0), w=["Mx"])
                S.add("vector", lambda e: e.memset(Mn[:, :], 0.0), w=["Mn"])
                wins = []
                hb0 = 0
                while hb0 < NHB:
                    wins.append((hb0, min(WIN, NHB - hb0)))
                    hb0 += WIN
                WW = WIN * 128

                def win_loads(wi):
                    if wi >= len(wins):
                        return
                    h0, nhb = wins[wi]
                    t1 = T1w[wi % 2]
                    t3 = T3w[wi % 2]
                    for k in range(NKD):
                        S.dma("sync", t1[:, WW * k:WW * k + 128 * nhb],
                              t1o[128 * k:128 * (k + 1), 128 * h0:128 * (h0 + nhb)],
                              r=["d.t0o"], w=[f"T1w{wi % 2}"])
                        S.dma("scalar", t3[:, WW * k:WW * k + 128 * nhb],
                              t3o[128 * k:128 * (k + 1), 128 * h0:128 * (h0 + nhb)],
                              r=["d.t2o"], w=[f"T3w{wi % 2}"])

                win_loads(0)
                win_loads(1)
                for wi, (h0, nhb) in enumerate(wins):
                    t1 = T1w[wi % 2]
                    t1r = f"T1w{wi % 2}"
                    t3 = T3w[wi % 2]
                    t3r = f"T3w{wi % 2}"
                    for j in range(nhb):
                        hb = h0 + j
                        reg = 2048 * (hb % 2)
                        hbbb = hbb[hb % 2]
                        hbr = f"hbb{hb % 2}"
                        for k in range(NKD):
                            for hf in range(2):
                                S.mm(PS[:, reg + 512 * hf:reg + 512 * (hf + 1)],
                                     t1[:, WW * k + 128 * j:WW * k + 128 * (j + 1)],
                                     xTs(k, 512 * hf, 512 * (hf + 1)),
                                     start=(k == 0), stop=(k == NKD - 1),
                                     r=[t1r, "xT"], w=pres(reg + 512 * hf, 512))
                                S.mm(PS[:, reg + 1024 + 512 * hf:reg + 1024 + 512 * (hf + 1)],
                                     t3[:, WW * k + 128 * j:WW * k + 128 * (j + 1)],
                                     xTs(k, 512 * hf, 512 * (hf + 1)),
                                     start=(k == 0), stop=(k == NKD - 1),
                                     r=[t3r, "xT"], w=pres(reg + 1024 + 512 * hf, 512))
                        S.add("vector", lambda e, reg=reg: e.tensor_mul(
                            rscr[:, 0:1024], PS[:, reg:reg + 1024], B1[:, :]),
                            r=pres(reg, 1024) + ["B1", "rscr"], w=["rscr"])
                        S.add("scalar", lambda e: e.activation(
                            f1scr[:, 0:1024], rscr[:, 0:1024], AF.Silu),
                            r=["rscr"], w=["f1scr"])
                        S.add("vector", lambda e: e.tensor_mul(
                            f1scr[:, 0:1024], f1scr[:, 0:1024], B3[:, :]),
                            r=["f1scr", "B3"], w=["f1scr"])
                        S.add("vector", lambda e, reg=reg, hbbb=hbbb: e.tensor_mul(
                            hbbb[:, :], f1scr[:, 0:1024], PS[:, reg + 1024:reg + 2048]),
                            r=["f1scr"] + pres(reg + 1024, 1024), w=[hbr])
                        S.add("vector", lambda e, hbbb=hbbb: e.tensor_tensor(
                            Mx[:, :], Mx[:, :], hbbb[:, :], OP.max),
                            r=["Mx", hbr], w=["Mx"])
                        S.add("vector", lambda e, hbbb=hbbb: e.tensor_tensor(
                            Mn[:, :], Mn[:, :], hbbb[:, :], OP.min),
                            r=["Mn", hbr], w=["Mn"])
                        S.dma("gpsimd", h_dr[128 * hb:128 * (hb + 1), :], hbbb[:, :],
                              r=[hbr], w=["d.h"])
                    win_loads(wi + 2)

                S.add("vector", lambda e: e.tensor_scalar(
                    Mn[:, :], Mn[:, :], -1.0, None, OP.mult), r=["Mn"], w=["Mn"])
                S.add("vector", lambda e: e.tensor_max(Mx[:, :], Mx[:, :], Mn[:, :]),
                      r=["Mx", "Mn"], w=["Mx"])
                S.add("vector", lambda e: e.tensor_copy(zbf[:, 0:1024], Mx[:, :]),
                      r=["Mx", "zbf"], w=["zbf"])
                for c in range(CH):
                    S.mm(PS[:, 128 * c:128 * (c + 1)], zbf[:, 128 * c:128 * (c + 1)],
                         idn[:, :], r=["zbf", "c.idn"], w=pres(128 * c, 128))
                S.add("vector", lambda e: e.tensor_copy(rscr[:, 0:1024], PS[:, 0:1024]),
                      r=pres(0, 1024) + ["rscr"], w=["rscr"])
                for c in range(CH):
                    S.add("vector", lambda e, c=c: e.reduce_max(
                        stat[:, HCM + c:HCM + c + 1], rscr[:, 128 * c:128 * (c + 1)], AX),
                        r=["rscr"], w=["st.hcm"])
                S.add("vector", lambda e: e.tensor_scalar(
                    stc(HCM, 8), stc(HCM, 8), EPS, None, OP.max),
                    r=["st.hcm"], w=["st.hcm"])
                S.add("tensor", lambda e: e.transpose(PS[0:8, 0:128], stc(HCM, 8),
                                                      idnf[:, :]),
                      r=["st.hcm", "c.idnf"], w=pres(0, 128))
                S.add("vector", lambda e: e.tensor_copy(rtmp[:, :], PS[0:8, 0:128]),
                      r=pres(0, 128), w=["rtmp"])
                S.dma("sync", rowb[:, :], rtmp[0:8, 0:128], r=["rtmp"], w=["d.rowb"])
                S.dma("sync", row1[0:1, :], rowb.ap().rearrange("a b -> (a b)").rearrange("(o n) -> o n", o=1),
                      r=["d.rowb"], w=["row1"])
                S.add("vector", lambda e: e.reciprocal(row2[0:1, :], row1[0:1, :]),
                      r=["row1", "row2"], w=["row2"])
                S.add("vector", lambda e: e.tensor_scalar(
                    row2[0:1, :], row2[0:1, :], 127.0, None, OP.mult),
                    r=["row2"], w=["row2"])
                for hf in range(2):
                    S.mm(PS[:, 512 * hf:512 * (hf + 1)], onesr[:, :],
                         row2[0:1, 512 * hf:512 * (hf + 1)],
                         r=["c.onesr", "row2"], w=pres(512 * hf, 512))
                S.add("vector", lambda e: e.tensor_copy(Bh[:, :], PS[:, 0:1024]),
                      r=pres(0, 1024), w=["Bh"])
                S.add("vector", lambda e: e.tensor_scalar(
                    row2[0:1, :], row1[0:1, :], msc[0:1, 9:10],
                    -1.0 / 127.0, OP.mult, OP.mult),
                    r=["row1", "msc", "row2"], w=["row2"])
                for hf in range(2):
                    S.mm(PS[:, 2048 + 512 * hf:2048 + 512 * (hf + 1)], onesr[:, :],
                         row2[0:1, 512 * hf:512 * (hf + 1)],
                         r=["c.onesr", "row2"], w=pres(2048 + 512 * hf, 512))
                S.add("vector", lambda e: e.tensor_copy(Bo[:, :], PS[:, 2048:3072]),
                      r=pres(2048, 1024), w=["Bo"])
                S.emit(nc, sems)

            s4b = ExitStack()
            with s4b:
                T2w = [s4b.enter_context(nc.sbuf_tensor(f"T2w{i}", [128, 512], BF16))
                       for i in range(2)]
                x2p = s4b.enter_context(nc.sbuf_tensor("x2p", [128, CH * D], BF16))
                ft = [s4b.enter_context(nc.sbuf_tensor(f"ft{i}", [128, 4096], BF16))
                      for i in range(2)]
                outst = s4b.enter_context(nc.sbuf_tensor("outst", [128, 2048], F32))

                for og in range(4):
                    ftb = ft[og % 2]
                    ftr = f"ft{og % 2}"
                    for k in range(NHB):
                        hbbb = hbb[k % 2]
                        hbr = f"hbb{k % 2}"
                        zb = zhn[k % 2]
                        zbr = f"zhn{k % 2}"
                        if og == 0:
                            S.dma("gpsimd", hbbb[:, :], h_dr[128 * k:128 * (k + 1), :],
                                  r=["d.h"], w=[hbr])
                            S.add("vector", lambda e, hbbb=hbbb: e.tensor_mul(
                                rscr[:, 0:1024], hbbb[:, :], Bh[:, :]),
                                r=[hbr, "Bh", "rscr"], w=["rscr"])
                            round_neg(0, zb[:, :], 1024, zbr)
                            S.dma("sync", zh_dr[128 * k:128 * (k + 1), :], zb[:, :],
                                  r=[zbr], w=["d.zh"])
                        else:
                            S.dma("gpsimd", zb[:, :], zh_dr[128 * k:128 * (k + 1), :],
                                  r=["d.zh"], w=[zbr])
                        t2 = T2w[k % 2]
                        t2r = f"T2w{k % 2}"
                        S.dma("scalar", t2[:, :],
                              t2o[128 * k:128 * (k + 1), 512 * og:512 * (og + 1)],
                              r=["d.t1o"], w=[t2r])
                        for ob in range(4):
                            for hf in range(2):
                                off = 1024 * ob + 512 * hf
                                S.mm(PS[:, off:off + 512],
                                     t2[:, 128 * ob:128 * (ob + 1)],
                                     zb[:, 512 * hf:512 * (hf + 1)],
                                     start=(k == 0), stop=(k == NHB - 1),
                                     r=[t2r, zbr], w=pres(off, 512))
                    for ob in range(4):
                        S.add("vector", lambda e, ob=ob, ftb=ftb: e.tensor_mul(
                            ftb[:, 1024 * ob:1024 * (ob + 1)],
                            PS[:, 1024 * ob:1024 * (ob + 1)], Bo[:, :]),
                            r=pres(1024 * ob, 1024) + ["Bo"], w=[ftr])
                    for ob in range(4):
                        for c in range(CH):
                            off = 1024 * ob + 128 * c
                            S.mm(PS[:, off:off + 128],
                                 ftb[:, off:off + 128], idn[:, :],
                                 r=[ftr, "c.idn"], w=pres(off, 128))
                    for ob in range(4):
                        fb = 4 * og + ob
                        x2v = v3(x2p, CH * D, 128 * fb, D, CH, 1, 128)
                        x1v = v3(ox1, CH * D, 128 * fb, D, CH, 1, 128)
                        psv = v3(PS, 4096, 1024 * ob, 128, CH, 1, 128)
                        S.add("vector", lambda e, a=x2v, p=psv, b=x1v:
                              e.scalar_tensor_tensor(a, p, 1.0, b, OP.mult, OP.add),
                              r=pres(1024 * ob, 1024) + ["ox1", "x2p"], w=["x2p"])
                tap("x2p", x2p[:, :], "x2p")
                tap("stat", stat[:, :], "st.dc")
                if DBG:
                    S.dma("sync", tp["msc"][:, :], msc[:, :], r=["msc"])

                for c in range(CH):
                    S.add("vector", lambda e, c=c: e.reduce_sum(
                        stat[:, X2MU + c:X2MU + c + 1], x2p[:, D * c:D * (c + 1)], AX),
                        r=["x2p"], w=["st.x2mu"])
                    S.add("scalar", lambda e, c=c: e.activation(
                        rscr[:, :], x2p[:, D * c:D * (c + 1)], AF.Square,
                        accum_out=stat[:, X2SSQ + c:X2SSQ + c + 1]),
                        r=["x2p"], w=["rscr", "st.x2ssq"])
                S.add("vector", lambda e: e.tensor_scalar(
                    stc(X2MU, 8), stc(X2MU, 8), 1.0 / D, None, OP.mult),
                    r=["st.x2mu"], w=["st.x2mu"])
                S.add("vector", lambda e: e.tensor_scalar(
                    stc(X2SSQ, 8), stc(X2SSQ, 8), 1.0 / D, None, OP.mult),
                    r=["st.x2ssq"], w=["st.x2ssq"])
                S.add("vector", lambda e: e.tensor_mul(
                    stc(SCR8, 8), stc(X2MU, 8), stc(X2MU, 8)),
                    r=["st.x2mu"], w=["st.s8"])
                S.add("vector", lambda e: e.tensor_sub(
                    stc(X2SSQ, 8), stc(X2SSQ, 8), stc(SCR8, 8)),
                    r=["st.x2ssq", "st.s8"], w=["st.x2ssq"])
                S.add("scalar", lambda e: e.activation(
                    stc(SCR8, 8), stc(X2SSQ, 8), AF.Sqrt, bias=eps_c[:, :]),
                    r=["st.x2ssq", "c.eps"], w=["st.s8"])
                S.add("vector", lambda e: e.reciprocal(stc(X2RS, 8), stc(SCR8, 8)),
                      r=["st.s8"], w=["st.x2rs"])
                for c in range(CH):
                    S.add("vector", lambda e, c=c: e.tensor_scalar(
                        outst[:, :], x2p[:, D * c:D * (c + 1)],
                        stat[:, X2MU + c:X2MU + c + 1],
                        stat[:, X2RS + c:X2RS + c + 1], OP.subtract, OP.mult),
                        r=["x2p", "st.x2mu", "st.x2rs", "outst"], w=["outst"])
                    S.dma("sync", out_d[CT * c:CT * (c + 1), :], outst[:, :],
                          r=["outst"], w=["d.out"])
                S.emit(nc, sems)
        S.enabled = True
        S.add("gpsimd", lambda e: None, barrier=True)
        S.emit(nc, sems)
    return nc


def kernel(L0_attn_x, L0_attn_Wq, L0_attn_Wk, L0_attn_Wkg1, L0_attn_Wkg2,
           L0_attn_bkg2, L0_attn_Wv, L0_attn_Wg, L0_attn_bg, L0_attn_Wo,
           L0_ffn_w1, L0_ffn_w2, L0_ffn_w3):
    f32 = np.float32
    bf16 = ml_dtypes.bfloat16
    x = np.ascontiguousarray(np.asarray(L0_attn_x, f32).reshape(NC * TOK, D))
    wq = np.ascontiguousarray(np.asarray(L0_attn_Wq, f32))
    wk = np.ascontiguousarray(np.asarray(L0_attn_Wk, f32))
    wkg1 = np.ascontiguousarray(np.asarray(L0_attn_Wkg1, f32))
    wkg2 = np.ascontiguousarray(np.asarray(L0_attn_Wkg2, f32))
    bkg2T = np.ascontiguousarray(np.asarray(L0_attn_bkg2, f32).reshape(8, 128).T)
    wv = np.ascontiguousarray(np.asarray(L0_attn_Wv, f32))
    wg = np.ascontiguousarray(np.asarray(L0_attn_Wg, f32))
    bgT = np.ascontiguousarray(np.asarray(L0_attn_bg, f32).reshape(16, 128).T)
    wo = np.ascontiguousarray(np.asarray(L0_attn_Wo, f32))
    w1 = np.ascontiguousarray(np.asarray(L0_ffn_w1, f32))
    w2 = np.ascontiguousarray(np.asarray(L0_ffn_w2, f32))
    w3 = np.ascontiguousarray(np.asarray(L0_ffn_w3, f32))

    idn = np.eye(128).astype(bf16)
    idnf = np.eye(128).astype(f32)
    tri = np.triu(np.ones((128, 128), f32))
    tri8 = np.tile(tri, (1, 8)).astype(bf16)

    nc = build()
    in_maps = []
    for i in range(NC):
        m = np.zeros((128, 16), f32)
        m[:, 0:i] = 1.0
        m[:, 8:16] = 1.0 - m[:, 0:8]
        in_maps.append({
            "x": x[TOK * i:TOK * (i + 1)],
            "wq": wq, "wk": wk, "wkg1": wkg1, "wkg2": wkg2, "bkg2T": bkg2T,
            "wv": wv, "wg": wg, "bgT": bgT, "wo": wo,
            "w1s": np.ascontiguousarray(w1.T[(D // NC) * i:(D // NC) * (i + 1)]),
            "w2s": np.ascontiguousarray(w2.T[(HID // NC) * i:(HID // NC) * (i + 1)]),
            "w3s": np.ascontiguousarray(w3.T[(D // NC) * i:(D // NC) * (i + 1)]),
            "idn": idn, "idnf": idnf, "tri8": tri8, "scanm": m,
        })
    res = run_bass_kernel_spmd(nc, in_maps, core_ids=list(range(NC)))
    LAST["res"] = res
    out = np.concatenate([res.results[i]["out"] for i in range(NC)], axis=0)
    return out.reshape(1, NC * TOK, D).astype(np.float32)


# version 15
# speedup vs baseline: 1.0054x; 1.0054x over previous
import numpy as np
import ml_dtypes
import concourse.bass as bass
import concourse.mybir as mybir
from concourse.bass_utils import run_bass_kernel_spmd
from contextlib import ExitStack

F32 = mybir.dt.float32
BF16 = mybir.dt.bfloat16
I32 = mybir.dt.int32
AF = mybir.ActivationFunctionType
OP = mybir.AluOpType
AX = mybir.AxisListType.X

NC = 8
TOK = 1024
CH = 8
CT = 128
D = 2048
H = 8
DV = 256
HID = 5632
NHB = HID // 128
NKD = D // 128
EPS = 1e-5
KS = 1.0 / 16.0
NWINV = 1.0 / float(HID * D)
WIN = 4
DBG = False
STOP = 99
LAST = {}

QUANT = {"pe": 1, "v": 1, "s": 1, "g": 1}


def quant(cls):
    return 16 if cls.startswith("D.") else 1
ENG_CLS = {"tensor": "pe", "vector": "v", "scalar": "s", "gpsimd": "g"}


class Sched:

    def __init__(self):
        self.steps = []
        self.ord_ = {c: 0 for c in QUANT}
        self.lastw = {}
        self.readers = {}
        self.enabled = True

    def section(self, n):
        self.enabled = (n <= STOP)

    def _dep_add(self, waits, cls, o):
        if o > waits.get(cls, 0):
            waits[cls] = o

    def add(self, eng, fn, *, cls=None, r=(), w=(), barrier=False):
        if not self.enabled and not barrier:
            return
        if cls is None:
            cls = ENG_CLS[eng]
        if cls not in self.ord_:
            self.ord_[cls] = 0
        ordinal = self.ord_[cls] + 1
        waits = {}
        if barrier:
            for c, o in self.ord_.items():
                if o > 0:
                    waits[c] = o
        for res in r:
            lw = self.lastw.get(res)
            if lw:
                self._dep_add(waits, lw[0], lw[1])
        for res in w:
            lw = self.lastw.get(res)
            if lw:
                self._dep_add(waits, lw[0], lw[1])
            for (c, o) in self.readers.get(res, ()):
                self._dep_add(waits, c, o)
        for res in r:
            self.readers.setdefault(res, []).append((cls, ordinal))
        for res in w:
            self.lastw[res] = (cls, ordinal)
            self.readers[res] = []
        self.steps.append((eng, cls, fn, ordinal, waits))
        self.ord_[cls] += 1

    def dma(self, eng, out, in_, *, r=(), w=(), **kw):
        res = w[0] if w else "misc"
        self.add(eng, lambda e, o=out, i=in_, k=dict(kw): e.dma_start(o, i, **k),
                 cls="D." + res, r=r, w=w)

    def mm(self, out, lhsT, rhs, start=True, stop=True, *, r=(), w=()):
        self.add("tensor",
                 lambda e, o=out, l=lhsT, rr=rhs, s1=start, s2=stop:
                 e.matmul(o, l, rr, start=s1, stop=s2), r=r, w=w)

    def emit(self, nc, sems):
        steps = self.steps
        with nc.Block() as block:
            SKIP_SAME = {"tensor": "pe", "scalar": "s", "gpsimd": "g"}

            def mk(name):
                def body(e):
                    wm = {}
                    mycls = SKIP_SAME.get(name)
                    for (eng, cls, fn, ordinal, waits) in steps:
                        if eng != name:
                            continue
                        for c, o in sorted(waits.items()):
                            if c == mycls:
                                continue
                            if o > wm.get(c, 0):
                                e.wait_ge(sems(c), o * quant(c))
                                wm[c] = o
                        ins = fn(e)
                        if ins is not None:
                            ins.then_inc(sems(cls), quant(cls))
                return body
            block.sync(mk("sync"))
            block.gpsimd(mk("gpsimd"))
            block.vector(mk("vector"))
            block.scalar(mk("scalar"))
            block.tensor(mk("tensor"))
        self.steps = []


def pres(off, n):
    return [f"P{b}" for b in range(off // 512, (off + n - 1) // 512 + 1)]


def build():
    nc = bass.Bass("TRN2", target_bir_lowering=False)

    x_d = nc.dram_tensor("x", [TOK, D], F32, kind="ExternalInput")
    wq_d = nc.dram_tensor("wq", [D, 1024], F32, kind="ExternalInput")
    wk_d = nc.dram_tensor("wk", [D, 1024], F32, kind="ExternalInput")
    wkg1_d = nc.dram_tensor("wkg1", [D, 16], F32, kind="ExternalInput")
    wkg2_d = nc.dram_tensor("wkg2", [16, 1024], F32, kind="ExternalInput")
    bkg2T_d = nc.dram_tensor("bkg2T", [128, 8], F32, kind="ExternalInput")
    wv_d = nc.dram_tensor("wv", [D, D], F32, kind="ExternalInput")
    wg_d = nc.dram_tensor("wg", [D, D], F32, kind="ExternalInput")
    bgT_d = nc.dram_tensor("bgT", [128, 16], F32, kind="ExternalInput")
    wo_d = nc.dram_tensor("wo", [D, D], F32, kind="ExternalInput")
    w1s_d = nc.dram_tensor("w1s", [D // NC, HID], F32, kind="ExternalInput")
    w2s_d = nc.dram_tensor("w2s", [HID // NC, D], F32, kind="ExternalInput")
    w3s_d = nc.dram_tensor("w3s", [D // NC, HID], F32, kind="ExternalInput")
    idn_d = nc.dram_tensor("idn", [128, 128], BF16, kind="ExternalInput")
    idnf_d = nc.dram_tensor("idnf", [128, 128], F32, kind="ExternalInput")
    tri8_d = nc.dram_tensor("tri8", [128, 1024], BF16, kind="ExternalInput")
    scanm_d = nc.dram_tensor("scanm", [128, 16], F32, kind="ExternalInput")
    out_d = nc.dram_tensor("out", [TOK, D], F32, kind="ExternalOutput")

    agi = nc.dram_tensor("agi", [128, 2064], F32)
    ago = nc.dram_tensor("ago", [NC * 128, 2064], F32)
    ari = nc.dram_tensor("ari", [1, 4], F32)
    aro = nc.dram_tensor("aro", [1, 4], F32)
    t1i = nc.dram_tensor("t1i", [D // NC, HID], BF16)
    t1o = nc.dram_tensor("t1o", [D, HID], BF16)
    t2i = nc.dram_tensor("t2i", [HID // NC, D], BF16)
    t2o = nc.dram_tensor("t2o", [HID, D], BF16)
    t3i = nc.dram_tensor("t3i", [D // NC, HID], BF16)
    t3o = nc.dram_tensor("t3o", [D, HID], BF16)
    rowb = nc.dram_tensor("rowb", [8, 128], F32)
    h_dr = nc.dram_tensor("h_dr", [HID, TOK], BF16)
    zh_dr = nc.dram_tensor("zh_dr", [HID, TOK], BF16)
    tp = {}
    if DBG:
        tp["xT"] = nc.dram_tensor("tp_xT", [128, NKD * TOK], BF16, kind="ExternalOutput")
        tp["Q"] = nc.dram_tensor("tp_Q", [128, H * TOK], F32, kind="ExternalOutput")
        tp["qdT"] = nc.dram_tensor("tp_qdT", [128, H * TOK], BF16, kind="ExternalOutput")
        tp["kd"] = nc.dram_tensor("tp_kd", [128, CH * H * 128], BF16, kind="ExternalOutput")
        tp["AT"] = nc.dram_tensor("tp_AT", [128, CH * H * 128], BF16, kind="ExternalOutput")
        tp["V"] = nc.dram_tensor("tp_V", [128, CH * D], BF16, kind="ExternalOutput")
        tp["L"] = nc.dram_tensor("tp_L", [128, 2048], F32, kind="ExternalOutput")
        tp["Sst"] = nc.dram_tensor("tp_Sst", [128, 2048], F32, kind="ExternalOutput")
        tp["orw"] = nc.dram_tensor("tp_orw", [128, CH * D], BF16, kind="ExternalOutput")
        tp["oo"] = nc.dram_tensor("tp_oo", [128, CH * D], BF16, kind="ExternalOutput")
        tp["x1"] = nc.dram_tensor("tp_x1", [128, CH * D], BF16, kind="ExternalOutput")
        tp["stat"] = nc.dram_tensor("tp_stat", [128, 512], F32, kind="ExternalOutput")
        tp["zxT"] = nc.dram_tensor("tp_zxT", [128, NKD * TOK], BF16, kind="ExternalOutput")
        tp["x2p"] = nc.dram_tensor("tp_x2p", [128, CH * D], BF16, kind="ExternalOutput")
        tp["msc"] = nc.dram_tensor("tp_msc", [1, 16], F32, kind="ExternalOutput")

    S = Sched()

    DLS, QLS, OMU, OSSQ, ORS, SCR64 = 0, 64, 72, 136, 200, 264
    XMU, XSSQ, XRS, SX, CMX, HCM = 328, 336, 344, 352, 360, 376
    DCORE, DPC, THR, NTHR, X2MU, X2SSQ, X2RS, SCR8 = 384, 400, 408, 412, 416, 424, 432, 440

    ctx = ExitStack()
    with ctx:
        idn = ctx.enter_context(nc.sbuf_tensor("idn_s", [128, 128], BF16))
        idnf = ctx.enter_context(nc.sbuf_tensor("idnf_s", [128, 128], F32))
        tri8 = ctx.enter_context(nc.sbuf_tensor("tri8_s", [128, 1024], BF16))
        scanm = ctx.enter_context(nc.sbuf_tensor("scanm_s", [128, 16], F32))
        bgT = ctx.enter_context(nc.sbuf_tensor("bgT_s", [128, 16], F32))
        nbkg = ctx.enter_context(nc.sbuf_tensor("nbkg", [128, 8], F32))
        ones_c = ctx.enter_context(nc.sbuf_tensor("ones_c", [128, 1], F32))
        onesr = ctx.enter_context(nc.sbuf_tensor("onesr", [1, 128], F32))
        eps_c = ctx.enter_context(nc.sbuf_tensor("eps_c", [128, 1], F32))
        Wb = [ctx.enter_context(nc.sbuf_tensor(f"Wb{i}", [128, 512], BF16))
              for i in range(4)]
        xT = ctx.enter_context(nc.sbuf_tensor("xT", [128, NKD * TOK], BF16))
        ox1 = ctx.enter_context(nc.sbuf_tensor("ox1", [128, CH * D], BF16))
        stat = ctx.enter_context(nc.sbuf_tensor("stat", [128, 512], F32))
        row1 = ctx.enter_context(nc.sbuf_tensor("row1", [1, 1024], F32))
        row2 = ctx.enter_context(nc.sbuf_tensor("row2", [1, 1024], F32))
        rtmp = ctx.enter_context(nc.sbuf_tensor("rtmp", [8, 128], F32))
        msc = ctx.enter_context(nc.sbuf_tensor("msc", [1, 16], F32))
        PS = ctx.enter_context(nc.psum_tensor("PS", [128, 4096], F32))
        _semmap = {}

        def sems(c):
            if c not in _semmap:
                _semmap[c] = ctx.enter_context(
                    nc.semaphore("sem_%d" % len(_semmap)))
            return _semmap[c]

        NQ = NKD * TOK

        def xTs(k, a, b):
            return xT[:, TOK * k + a: TOK * k + b]

        def stc(a, n=1):
            return stat[:, a:a + n]

        def v3(t, total_cols, offset, s1, n1, s2, n2):
            return bass.AP(t, offset, [[total_cols, 128], [s1, n1], [s2, n2]])

        def tap(name, buf, res):
            if DBG:
                S.dma("sync", tp[name][:, :], buf, r=[res])

        CONSTS = ["c.idn", "c.idnf", "c.tri8", "c.scanm", "c.bgT", "c.nbkg",
                  "c.ones", "c.onesr", "c.eps"]

        def transpose_tiles(srcs, src_res, dst_view, dst_res, ps_off=0):
            banks = pres(ps_off, 128 * len(srcs))
            for t, s in enumerate(srcs):
                S.mm(PS[:, ps_off + 128 * t: ps_off + 128 * (t + 1)], s, idn[:, :],
                     r=[src_res, "c.idn"], w=pres(ps_off + 128 * t, 128))
            n = len(srcs)
            pv = v3(PS, 4096, ps_off, 128, n, 1, 128)
            S.add("vector", lambda e, d=dst_view, p=pv: e.tensor_copy(d, p),
                  r=banks, w=[dst_res])

        S.dma("sync", idn[:, :], idn_d[:, :], w=["c.idn"])
        S.dma("sync", idnf[:, :], idnf_d[:, :], w=["c.idnf"])
        S.dma("sync", tri8[:, :], tri8_d[:, :], w=["c.tri8"])
        S.dma("sync", scanm[:, :], scanm_d[:, :], w=["c.scanm"])
        S.dma("sync", bgT[:, :], bgT_d[:, :], w=["c.bgT"])
        S.dma("sync", nbkg[:, :], bkg2T_d[:, :], w=["c.nbkg"])
        S.add("vector", lambda e: e.tensor_scalar(
            nbkg[:, :], nbkg[:, :], -1.0, None, OP.mult), r=["c.nbkg"], w=["c.nbkg"])
        S.add("vector", lambda e: e.memset(ones_c[:, :], 1.0), w=["c.ones"])
        S.add("vector", lambda e: e.memset(onesr[:, :], 1.0), w=["c.onesr"])
        S.add("vector", lambda e: e.memset(eps_c[:, :], EPS), w=["c.eps"])
        S.add("vector", lambda e: e.memset(stat[:, :], 0.0),
              w=["st.dls", "st.qls", "st.omu", "st.ossq", "st.ors", "st.s64",
                 "st.xmu", "st.xssq", "st.xrs", "st.sx", "st.cmx", "st.hcm",
                 "st.dc", "st.dpc", "st.thr", "st.nthr", "st.x2mu", "st.x2ssq",
                 "st.x2rs", "st.s8"])

        S.section(10)
        s1 = ExitStack()
        with s1:
            qdT = s1.enter_context(nc.sbuf_tensor("qdT", [128, H * TOK], BF16))
            wsum = s1.enter_context(nc.sbuf_tensor("wsum", [128, 24], F32))
            kd = s1.enter_context(nc.sbuf_tensor("kd", [128, CH * H * 128], BF16))
            AT = s1.enter_context(nc.sbuf_tensor("AT", [128, CH * H * 128], BF16))

            s1a = ExitStack()
            with s1a:
                xbf = [s1a.enter_context(nc.sbuf_tensor(f"xbf{i}", [128, D], BF16))
                       for i in range(2)]
                for c in range(CH):
                    xb = xbf[c % 2]
                    xr = f"xbf{c % 2}"
                    S.dma("gpsimd", xb[:, :], x_d[CT * c:CT * (c + 1), :], w=[xr])
                    transpose_tiles([xb[:, 128 * t:128 * (t + 1)] for t in range(NKD)],
                                    xr, v3(xT, NQ, CT * c, TOK, NKD, 1, CT), "xT",
                                    ps_off=2048)
                tap("xT", xT[:, :], "xT")
                S.emit(nc, sems)

            S.section(20)
            s1b = ExitStack()
            with s1b:
                Q = s1b.enter_context(nc.sbuf_tensor("Q", [128, H * TOK], F32))
                Etmp = s1b.enter_context(nc.sbuf_tensor("Etmp", [128, 1024], F32))
                Escr = [s1b.enter_context(nc.sbuf_tensor(f"Escr{i}", [128, 1024], F32))
                        for i in range(2)]
                Escr2 = [s1b.enter_context(nc.sbuf_tensor(f"Escr2{i}", [128, 1024], F32))
                         for i in range(2)]
                kkt2 = [s1b.enter_context(nc.sbuf_tensor(f"kkt2{i}", [128, 2048], BF16))
                        for i in range(2)]
                kdt2 = [s1b.enter_context(nc.sbuf_tensor(f"kdt2{i}", [128, 2048], BF16))
                        for i in range(2)]
                t1a = s1b.enter_context(nc.sbuf_tensor("t1a", [16, 1024], BF16))
                wkg1s = s1b.enter_context(nc.sbuf_tensor("wkg1s", [128, 256], BF16))
                wkg2a = s1b.enter_context(nc.sbuf_tensor("wkg2a", [16, 1024], BF16))
                nql = s1b.enter_context(nc.sbuf_tensor("nql", [128, 64], F32))

                S.dma("gpsimd", v3(wkg1s, 256, 0, 16, 16, 1, 16),
                      wkg1_d.ap().rearrange("(k p) c -> p k c", p=128), w=["wkg1s"])
                S.dma("gpsimd", wkg2a[0:16, :], wkg2_d[:, :], w=["wkg2a"])
                for k in range(NKD):
                    for hf in range(2):
                        S.mm(PS[0:16, 512 * hf:512 * (hf + 1)],
                             wkg1s[:, 16 * k:16 * (k + 1)],
                             xTs(k, 512 * hf, 512 * (hf + 1)),
                             start=(k == 0), stop=(k == NKD - 1),
                             r=["wkg1s", "xT"], w=pres(512 * hf, 512))
                S.add("vector", lambda e: e.tensor_copy(t1a[0:16, :], PS[0:16, 0:1024]),
                      r=pres(0, 1024), w=["t1a"])
                for h in range(H):
                    reg = 1024 + 1024 * (h % 2)
                    for hf in range(2):
                        S.mm(PS[:, reg + 512 * hf:reg + 512 * (hf + 1)],
                             wkg2a[:, 128 * h:128 * (h + 1)],
                             t1a[:, 512 * hf:512 * (hf + 1)],
                             r=["wkg2a", "t1a"], w=pres(reg + 512 * hf, 512))
                    S.add("scalar", lambda e, h=h, reg=reg: e.activation(
                        Etmp[:, :], PS[:, reg:reg + 1024], AF.Exp, scale=-1.0,
                        bias=nbkg[:, h:h + 1]),
                        r=pres(reg, 1024) + ["c.nbkg"], w=["Etmp"])
                    er = Escr[h % 2]
                    err = f"Escr{h % 2}"
                    S.add("scalar", lambda e, er=er: e.activation(
                        er[:, :], Etmp[:, :], AF.Ln, bias=ones_c[:, :]),
                        r=["Etmp", "c.ones"], w=[err])
                    for c in range(CH):
                        S.add("vector", lambda e, h=h, c=c, er=er: e.tensor_tensor_scan(
                            Q[:, TOK * h + CT * c:TOK * h + CT * (c + 1)],
                            er[:, CT * c:CT * (c + 1)],
                            er[:, CT * c:CT * (c + 1)], 0.0, OP.add, OP.bypass),
                            r=[err], w=["Q"])
                    qlv = v3(Q, H * TOK, TOK * h + CT - 1, 1, 1, CT, CH)
                    S.add("vector", lambda e, h=h, v=qlv: e.reduce_sum(
                        stat[:, QLS + h:QLS + h + 1], v, AX), r=["Q"], w=["st.qls"])
                    S.add("vector", lambda e, h=h, v=qlv: e.tensor_scalar(
                        nql[:, 8 * h:8 * h + 8], v, -KS, None, OP.mult),
                        r=["Q"], w=["nql"])
                    S.add("scalar", lambda e, h=h, v=qlv: e.activation(
                        stat[:, DLS + 8 * h:DLS + 8 * h + 8], v, AF.Exp, scale=-KS),
                        r=["Q"], w=["st.dls"])
                tap("Q", Q[:, :], "Q")

                for pr in range(4):
                    reg = 2048 * (pr % 2)
                    for k in range(NKD):
                        wb = Wb[(pr * NKD + k) % 4]
                        wbr = f"Wb{(pr * NKD + k) % 4}"
                        S.dma("gpsimd", wb[:, 0:256],
                              wq_d[128 * k:128 * (k + 1), 256 * pr:256 * (pr + 1)],
                              w=[wbr])
                        for hl in range(2):
                            for hf in range(2):
                                off = reg + 1024 * hl + 512 * hf
                                S.mm(PS[:, off:off + 512],
                                     wb[:, 128 * hl:128 * (hl + 1)],
                                     xTs(k, 512 * hf, 512 * (hf + 1)),
                                     start=(k == 0), stop=(k == NKD - 1),
                                     r=[wbr, "xT"], w=pres(off, 512))
                    for hl in range(2):
                        h = 2 * pr + hl
                        er = Escr[hl]
                        err = f"Escr{hl}"
                        S.add("scalar", lambda e, h=h, er=er: e.activation(
                            er[:, :], Q[:, TOK * h:TOK * (h + 1)], AF.Exp, scale=-KS),
                            r=["Q"], w=[err])
                        S.add("vector", lambda e, h=h, hl=hl, reg=reg, er=er:
                              e.tensor_mul(qdT[:, TOK * h:TOK * (h + 1)],
                                           PS[:, reg + 1024 * hl:reg + 1024 * (hl + 1)],
                                           er[:, :]),
                              r=pres(reg + 1024 * hl, 1024) + [err], w=["qdT"])
                tap("qdT", qdT[:, :], "qdT")

                for pr in range(4):
                    reg = 2048 * (pr % 2)
                    oreg = 2048 * ((pr + 1) % 2)
                    kk = kkt2[pr % 2]
                    kkr = f"kkt2{pr % 2}"
                    kdt = kdt2[pr % 2]
                    kdtr = f"kdt2{pr % 2}"
                    for k in range(NKD):
                        wb = Wb[(pr * NKD + k) % 4]
                        wbr = f"Wb{(pr * NKD + k) % 4}"
                        S.dma("gpsimd", wb[:, 0:256],
                              wk_d[128 * k:128 * (k + 1), 256 * pr:256 * (pr + 1)],
                              w=[wbr])
                        for hl in range(2):
                            for hf in range(2):
                                off = reg + 1024 * hl + 512 * hf
                                S.mm(PS[:, off:off + 512],
                                     wb[:, 128 * hl:128 * (hl + 1)],
                                     xTs(k, 512 * hf, 512 * (hf + 1)),
                                     start=(k == 0), stop=(k == NKD - 1),
                                     r=[wbr, "xT"], w=pres(off, 512))
                    for hl in range(2):
                        h = 2 * pr + hl
                        er = Escr[hl]
                        err = f"Escr{hl}"
                        er2 = Escr2[hl]
                        err2 = f"Escr2{hl}"
                        S.add("scalar", lambda e, h=h, er=er: e.activation(
                            er[:, :], Q[:, TOK * h:TOK * (h + 1)], AF.Exp, scale=KS),
                            r=["Q"], w=[err])
                        S.add("vector", lambda e, hl=hl, reg=reg, kk=kk, er=er:
                              e.scalar_tensor_tensor(
                                  kk[:, 1024 * hl:1024 * (hl + 1)],
                                  PS[:, reg + 1024 * hl:reg + 1024 * (hl + 1)],
                                  KS, er[:, :], OP.mult, OP.mult),
                              r=pres(reg + 1024 * hl, 1024) + [err], w=[kkr])
                        for c in range(CH):
                            S.add("scalar", lambda e, h=h, c=c, er2=er2: e.activation(
                                er2[:, CT * c:CT * (c + 1)],
                                Q[:, TOK * h + CT * c:TOK * h + CT * (c + 1)],
                                AF.Exp, scale=KS, bias=nql[:, 8 * h + c:8 * h + c + 1]),
                                r=["Q", "nql"], w=[err2])
                        S.add("vector", lambda e, hl=hl, reg=reg, kdt=kdt, er2=er2:
                              e.scalar_tensor_tensor(
                                  kdt[:, 1024 * hl:1024 * (hl + 1)],
                                  PS[:, reg + 1024 * hl:reg + 1024 * (hl + 1)],
                                  KS, er2[:, :], OP.mult, OP.mult),
                              r=pres(reg + 1024 * hl, 1024) + [err2], w=[kdtr])
                    for hl in range(2):
                        h = 2 * pr + hl
                        for c in range(CH):
                            off = oreg + 1024 * hl + 128 * c
                            S.mm(PS[:, off:off + 128],
                                 kk[:, 1024 * hl + 128 * c:1024 * hl + 128 * (c + 1)],
                                 qdT[:, TOK * h + CT * c:TOK * h + CT * (c + 1)],
                                 r=[kkr, "qdT"], w=pres(off, 128))
                        atv = v3(AT, CH * H * 128, 128 * h, 1024, CH, 1, 128)
                        psv = v3(PS, 4096, oreg + 1024 * hl, 128, CH, 1, 128)
                        t8v = v3(tri8, 1024, 0, 128, CH, 1, 128)
                        S.add("vector", lambda e, a=atv, b=psv, t=t8v:
                              e.tensor_mul(a, b, t),
                              r=pres(oreg + 1024 * hl, 1024) + ["c.tri8"], w=["AT"])
                    for hl in range(2):
                        for c in range(CH):
                            off = oreg + 1024 * hl + 128 * c
                            S.mm(PS[:, off:off + 128],
                                 kdt[:, 1024 * hl + 128 * c:1024 * hl + 128 * (c + 1)],
                                 idn[:, :],
                                 r=[kdtr, "c.idn"], w=pres(off, 128))
                    for hl in range(2):
                        h = 2 * pr + hl
                        kdv = v3(kd, CH * H * 128, 128 * h, 1024, CH, 1, 128)
                        psv = v3(PS, 4096, oreg + 1024 * hl, 128, CH, 1, 128)
                        S.add("vector", lambda e, a=kdv, b=psv: e.tensor_copy(a, b),
                              r=pres(oreg + 1024 * hl, 1024), w=["kd"])
                tap("kd", kd[:, :], "kd")
                tap("AT", AT[:, :], "AT")
                S.emit(nc, sems)

            S.section(30)
            if True:
                fstg = [s1.enter_context(nc.sbuf_tensor(f"fstgA{i}", [128, 2048], F32))
                        for i in range(2)]
                fjunk2 = [s1.enter_context(nc.sbuf_tensor(f"fjunkA{i}", [128, 2048],
                                                          BF16)) for i in range(2)]
                fj22 = [s1.enter_context(nc.sbuf_tensor("fj2", [128, 2048], BF16))] * 2
                fjunk = fjunk2[0]
                S.add("vector", lambda e: e.memset(wsum[:, :], 0.0), w=["wsum"])
                tiles = []
                for w_i, wd in enumerate((w1s_d, w2s_d, w3s_d)):
                    R, Cc = wd.shape
                    ti = 0
                    for rr0 in range(0, R, 128):
                        rr = min(128, R - rr0)
                        for c0 in range(0, Cc, 2048):
                            cw = min(2048, Cc - c0)
                            tiles.append((w_i, wd, rr0, rr, c0, cw, ti))
                            ti += 1
                for tn, (w_i, wd, rr0, rr, c0, cw, ti) in enumerate(tiles):
                    fb = fstg[tn % 2]
                    fr = f"fstg{tn % 2}"
                    S.dma("sync", fb[0:rr, 0:cw], wd[rr0:rr0 + rr, c0:c0 + cw], w=[fr])
                    S.add("scalar", lambda e, rr=rr, cw=cw, w_i=w_i, ti=ti, fb=fb:
                          e.activation(fjunk[0:rr, 0:cw], fb[0:rr, 0:cw], AF.Abs,
                                       accum_out=wsum[0:rr, 6 * w_i + ti:6 * w_i + ti + 1]),
                          r=[fr], w=["fjunk0", "wsum"])
                for w_i in range(3):
                    S.add("scalar", lambda e, w_i=w_i:
                          e.activation(fjunk[:, 0:6], wsum[:, 6 * w_i:6 * w_i + 6],
                                       AF.Identity,
                                       accum_out=wsum[:, 18 + w_i:19 + w_i]),
                          r=["wsum"], w=["fjunk0", "wsum"])
            S.section(31)
            if True:
                V = ox1
                vtmp = [s1.enter_context(nc.sbuf_tensor(f"vtmp{i}", [128, 2048], BF16))
                        for i in range(2)]
                L = s1.enter_context(nc.sbuf_tensor("L", [128, 2048], F32))
                Sst = s1.enter_context(nc.sbuf_tensor("Sst", [128, 2048], F32))
                Sb = [s1.enter_context(nc.sbuf_tensor(f"Sb{i}", [128, 2048], BF16))
                      for i in range(2)]
                AGl = [s1.enter_context(nc.sbuf_tensor(f"AGl{i}", [128, 2064], F32))
                       for i in range(2)]

                for vp in range(8):
                    reg = 2048 * (vp % 2)
                    vt = vtmp[vp % 2]
                    vtr = f"vtmp{vp % 2}"
                    for k in range(NKD):
                        wb = Wb[(vp * NKD + k) % 4]
                        wbr = f"Wb{(vp * NKD + k) % 4}"
                        S.dma("gpsimd", wb[:, 0:256],
                              wv_d[128 * k:128 * (k + 1), 256 * vp:256 * (vp + 1)],
                              w=[wbr])
                        for mb in range(2):
                            for hf in range(2):
                                off = reg + 1024 * mb + 512 * hf
                                S.mm(PS[:, off:off + 512],
                                     wb[:, 128 * mb:128 * (mb + 1)],
                                     xTs(k, 512 * hf, 512 * (hf + 1)),
                                     start=(k == 0), stop=(k == NKD - 1),
                                     r=[wbr, "xT"], w=pres(off, 512))
                    S.add("vector", lambda e, reg=reg, vt=vt:
                          e.tensor_copy(vt[:, :], PS[:, reg:reg + 2048]),
                          r=pres(reg, 2048), w=[vtr])
                    for mb in range(2):
                        for c in range(CH):
                            off = reg + 1024 * mb + 128 * c
                            S.mm(PS[:, off:off + 128],
                                 vt[:, 1024 * mb + 128 * c:1024 * mb + 128 * (c + 1)],
                                 idn[:, :],
                                 r=[vtr, "c.idn"], w=pres(off, 128))
                    for mb in range(2):
                        fb = 2 * vp + mb
                        vv = v3(V, CH * D, 128 * fb, D, CH, 1, 128)
                        psv = v3(PS, 4096, reg + 1024 * mb, 128, CH, 1, 128)
                        S.add("vector", lambda e, a=vv, b=psv: e.tensor_copy(a, b),
                              r=pres(reg + 1024 * mb, 1024), w=["ox1"])
                    if vp == 2:
                        S.mm(PS[0:1, 0:3], ones_c[:, :], wsum[:, 18:21],
                             r=["c.ones", "wsum"], w=pres(0, 3))
                        S.add("vector", lambda e: e.tensor_copy(
                            msc[0:1, 0:3], PS[0:1, 0:3]), r=pres(0, 3), w=["msc"])
                        S.add("vector", lambda e: e.memset(msc[0:1, 3:4], 0.0),
                              r=["msc"], w=["msc"])
                        S.dma("sync", ari[:, :], msc[0:1, 0:4], r=["msc"], w=["d.ari"])
                tap("V", V[:, :], "ox1")

                S.section(32)
                S.add("vector", lambda e: e.memset(L[:, :], 0.0), w=["L"])
                for c in range(CH):
                    reg = 2048 * (c % 2)
                    for h in range(H):
                        off = reg + 256 * h
                        S.mm(PS[:, off:off + 256],
                             kd[:, 1024 * c + 128 * h:1024 * c + 128 * (h + 1)],
                             V[:, D * c + DV * h:D * c + DV * (h + 1)],
                             r=["kd", "ox1"], w=pres(off, 256))
                    for h in range(H):
                        off = reg + 256 * h
                        S.add("vector", lambda e, c=c, h=h, off=off:
                              e.scalar_tensor_tensor(
                                  L[:, 256 * h:256 * (h + 1)], L[:, 256 * h:256 * (h + 1)],
                                  stat[:, DLS + 8 * h + c:DLS + 8 * h + c + 1],
                                  PS[:, off:off + 256], OP.mult, OP.add),
                              r=["L", "st.dls"] + pres(off, 256), w=["L"])
                tap("L", L[:, :], "L")
                S.add("gpsimd", lambda e: e.collective_compute(
                    "AllReduce", OP.add, replica_groups=[list(range(NC))],
                    ins=[ari.ap()], outs=[aro.ap()]),
                    cls="cc.ar", r=["d.ari"], w=["d.aro"])
                S.dma("scalar", msc[0:1, 4:8], aro[:, :], r=["d.aro"], w=["msc"])
                S.section(33)
                S.add("scalar", lambda e: e.activation(
                    stc(DCORE, 8), stc(QLS, 8), AF.Exp, scale=-KS),
                    r=["st.qls"], w=["st.dc"])
                S.dma("sync", agi[:, 0:2048], L[:, :], r=["L"], w=["d.agi"])
                S.dma("sync", agi[:, 2048:2056], stc(DCORE, 8), r=["st.dc"], w=["d.agi"])
                S.add("vector", lambda e: e.memset(stc(DCORE + 8, 8), 0.0), w=["st.dc8"])
                S.dma("sync", agi[:, 2056:2064], stc(DCORE + 8, 8), r=["st.dc8"],
                      w=["d.agi"])
                S.add("gpsimd", lambda e: e.collective_compute(
                    "AllGather", OP.bypass, replica_groups=[list(range(NC))],
                    ins=[agi.ap()], outs=[ago.ap()]),
                    cls="cc.st", r=["d.agi"], w=["d.ago"])
                S.add("vector", lambda e: e.tensor_scalar(
                    msc[0:1, 8:11], msc[0:1, 4:7], NWINV, EPS, OP.mult, OP.max),
                    r=["msc"], w=["msc"])
                S.add("vector", lambda e: e.tensor_scalar(
                    msc[0:1, 12:15], msc[0:1, 8:11], 0.5, None, OP.mult),
                    r=["msc"], w=["msc"])
                S.mm(PS[:, 0:3], onesr[:, :], msc[0:1, 12:15],
                     r=["c.onesr", "msc"], w=pres(0, 3))
                S.add("vector", lambda e: e.tensor_copy(stc(THR, 3), PS[:, 0:3]),
                      r=pres(0, 3), w=["st.thr"])
                S.add("vector", lambda e: e.tensor_scalar(
                    stc(NTHR, 3), stc(THR, 3), -1.0, None, OP.mult),
                    r=["st.thr"], w=["st.nthr"])

                tern_tiles = []
                for w_i, wd in ((2, w3s_d), (0, w1s_d), (1, w2s_d)):
                    R, Cc = wd.shape
                    ti = 0
                    for rr0 in range(0, R, 128):
                        rr = min(128, R - rr0)
                        for c0 in range(0, Cc, 2048):
                            cw = min(2048, Cc - c0)
                            tern_tiles.append((w_i, wd, rr0, rr, c0, cw, ti))
                            ti += 1
                NTT = len(tern_tiles)
                NT1 = sum(1 for t in tern_tiles if t[0] != 1)

                def tern_load(tn):
                    if tn >= NTT:
                        return
                    w_i, wd, rr0, rr, c0, cw, ti = tern_tiles[tn]
                    fb = fstg[tn % 2]
                    S.dma("sync", fb[0:rr, 0:cw], wd[rr0:rr0 + rr, c0:c0 + cw],
                          w=[f"fstg{tn % 2}"])

                def tern_ops(tn):
                    w_i, wd, rr0, rr, c0, cw, ti = tern_tiles[tn]
                    td = (t1i, t2i, t3i)[w_i]
                    fb = fstg[tn % 2]
                    fr = f"fstg{tn % 2}"
                    fjb = fjunk2[tn % 2]
                    fjr = f"fjunk{tn % 2}"
                    fj2b = fj22[tn % 2]
                    fj2r = f"fjj{tn % 2}"
                    S.add("vector", lambda e, rr=rr, cw=cw, w_i=w_i, fb=fb, fjb=fjb:
                          e.tensor_scalar(fjb[0:rr, 0:cw], fb[0:rr, 0:cw],
                                          stat[0:rr, THR + w_i:THR + w_i + 1],
                                          None, OP.is_ge),
                          r=[fr, "st.thr"], w=[fjr])
                    S.add("vector", lambda e, rr=rr, cw=cw, w_i=w_i, fb=fb, fj2b=fj2b:
                          e.tensor_scalar(fj2b[0:rr, 0:cw], fb[0:rr, 0:cw],
                                          stat[0:rr, NTHR + w_i:NTHR + w_i + 1],
                                          None, OP.is_le),
                          r=[fr, "st.nthr"], w=[fj2r])
                    S.add("vector", lambda e, rr=rr, cw=cw, fjb=fjb, fj2b=fj2b:
                          e.tensor_sub(fjb[0:rr, 0:cw], fjb[0:rr, 0:cw],
                                       fj2b[0:rr, 0:cw]),
                          r=[fjr, fj2r], w=[fjr])
                    S.dma("sync", td[rr0:rr0 + rr, c0:c0 + cw], fjb[0:rr, 0:cw],
                          r=[fjr], w=[f"d.t{w_i}i"])
                    tern_load(tn + 2)

                tern_load(0)
                tern_load(1)

                S.section(34)
                S.add("vector", lambda e: e.memset(Sst[:, :], 0.0),
                      r=["Sst"], w=["Sst"])
                S.add("scalar", lambda e: e.copy(Sb[0][:, :], Sst[:, :]),
                      r=["Sst"], w=["Sb0"])

                def cmb(j):
                    if j == 0:
                        S.add("vector", lambda e: e.memset(L[:, :], 0.0),
                              r=["L"], w=["L"])
                    ag = AGl[j % 2]
                    agr = f"AGl{j % 2}"
                    S.dma("scalar", ag[:, :], ago[128 * j:128 * (j + 1), :],
                          r=["d.ago"], w=[agr])
                    S.add("vector", lambda e, j=j, ag=ag: e.tensor_scalar(
                        stc(DPC, 8), ag[:, 2048:2056],
                        scanm[:, j:j + 1], scanm[:, 8 + j:9 + j], OP.mult, OP.add),
                        r=[agr, "c.scanm"], w=["st.dpc"])
                    for h in range(H):
                        S.add("vector", lambda e, h=h: e.tensor_scalar(
                            L[:, 256 * h:256 * (h + 1)], L[:, 256 * h:256 * (h + 1)],
                            stat[:, DPC + h:DPC + h + 1], None, OP.mult),
                            r=["L", "st.dpc"], w=["L"])
                        S.add("vector", lambda e, j=j, h=h, ag=ag:
                              e.scalar_tensor_tensor(
                                  L[:, 256 * h:256 * (h + 1)],
                                  ag[:, 256 * h:256 * (h + 1)],
                                  scanm[:, j:j + 1], L[:, 256 * h:256 * (h + 1)],
                                  OP.mult, OP.add),
                              r=[agr, "c.scanm", "L"], w=["L"])

                for c in range(CH):
                    reg = 0
                    greg = 2048
                    sb = Sb[c % 2]
                    sbr = f"Sb{c % 2}"
                    sbn = Sb[(c + 1) % 2]
                    sbnr = f"Sb{(c + 1) % 2}"
                    for h in range(H):
                        off = reg + 256 * h
                        S.mm(PS[:, off:off + 256],
                             AT[:, 1024 * c + 128 * h:1024 * c + 128 * (h + 1)],
                             V[:, D * c + DV * h:D * c + DV * (h + 1)],
                             start=True, stop=False,
                             r=["AT", "ox1"], w=pres(off, 256))
                        S.mm(PS[:, off:off + 256],
                             qdT[:, TOK * h + CT * c:TOK * h + CT * (c + 1)],
                             sb[:, 256 * h:256 * (h + 1)],
                             start=False, stop=True,
                             r=["qdT", sbr], w=pres(off, 256))
                    for h in range(H):
                        off = greg + 256 * h
                        S.mm(PS[:, off:off + 256],
                             kd[:, 1024 * c + 128 * h:1024 * c + 128 * (h + 1)],
                             V[:, D * c + DV * h:D * c + DV * (h + 1)],
                             r=["kd", "ox1"], w=pres(off, 256))
                    for h in range(H):
                        off = greg + 256 * h
                        S.add("vector", lambda e, c=c, h=h, off=off:
                              e.scalar_tensor_tensor(
                                  Sst[:, 256 * h:256 * (h + 1)],
                                  Sst[:, 256 * h:256 * (h + 1)],
                                  stat[:, DLS + 8 * h + c:DLS + 8 * h + c + 1],
                                  PS[:, off:off + 256], OP.mult, OP.add),
                              r=["Sst", "st.dls"] + pres(off, 256), w=["Sst"])
                    if c < CH - 1:
                        S.add("scalar", lambda e, sbn=sbn: e.copy(sbn[:, :], Sst[:, :]),
                              r=["Sst"], w=[sbnr])
                    S.add("scalar", lambda e, c=c: e.copy(
                        ox1[:, D * c:D * (c + 1)], PS[:, 0:2048]),
                        r=pres(0, 2048), w=["ox1"])
                    if c >= 3:
                        cmb(c - 3)
                    if c == CH - 1:
                        cmb(5)
                        cmb(6)
                    for tn in range(-(-NT1 * c // CH), -(-NT1 * (c + 1) // CH)):
                        tern_ops(tn)

                tap("Sst", L[:, :], "L")

                for c in range(CH):
                    S.add("scalar", lambda e: e.copy(Sb[0][:, :], L[:, :]),
                          r=["L"], w=["Sb0"])
                    for h in range(H):
                        off = 256 * h
                        S.mm(PS[:, off:off + 256],
                             qdT[:, TOK * h + CT * c:TOK * h + CT * (c + 1)],
                             Sb[0][:, 256 * h:256 * (h + 1)],
                             r=["qdT", "Sb0"], w=pres(off, 256))
                    S.add("vector", lambda e, c=c: e.tensor_add(
                        ox1[:, D * c:D * (c + 1)], ox1[:, D * c:D * (c + 1)],
                        PS[:, 0:2048]),
                        r=["ox1"] + pres(0, 2048), w=["ox1"])
                    if c < CH - 1:
                        for h in range(H):
                            S.add("vector", lambda e, c=c, h=h: e.tensor_scalar(
                                L[:, 256 * h:256 * (h + 1)],
                                L[:, 256 * h:256 * (h + 1)],
                                stat[:, DLS + 8 * h + c:DLS + 8 * h + c + 1],
                                None, OP.mult),
                                r=["L", "st.dls"], w=["L"])
                    ov3 = v3(ox1, CH * D, D * c, 256, 8, 1, 256)
                    S.add("vector", lambda e, c=c, o=ov3: e.reduce_sum(
                        stat[:, OMU + 8 * c:OMU + 8 * c + 8], o, AX),
                        r=["ox1"], w=["st.omu"])
                    for h in range(H):
                        S.add("scalar", lambda e, c=c, h=h: e.activation(
                            vtmp[0][:, 0:256],
                            ox1[:, D * c + DV * h:D * c + DV * (h + 1)], AF.Square,
                            accum_out=stat[:, OSSQ + 8 * c + h:OSSQ + 8 * c + h + 1]),
                            r=["ox1"], w=["vtmp0", "st.ossq"])
                tap("orw", ox1[:, :], "ox1")

            S.section(35)
            S.add("vector", lambda e: e.tensor_scalar(
                stc(OMU, 64), stc(OMU, 64), 1.0 / DV, None, OP.mult),
                r=["st.omu"], w=["st.omu"])
            S.add("vector", lambda e: e.tensor_scalar(
                stc(OSSQ, 64), stc(OSSQ, 64), 1.0 / DV, None, OP.mult),
                r=["st.ossq"], w=["st.ossq"])
            S.add("vector", lambda e: e.tensor_mul(
                stc(SCR64, 64), stc(OMU, 64), stc(OMU, 64)),
                r=["st.omu"], w=["st.s64"])
            S.add("vector", lambda e: e.tensor_sub(
                stc(OSSQ, 64), stc(OSSQ, 64), stc(SCR64, 64)),
                r=["st.ossq", "st.s64"], w=["st.ossq"])
            S.add("scalar", lambda e: e.activation(
                stc(SCR64, 64), stc(OSSQ, 64), AF.Sqrt, bias=eps_c[:, :]),
                r=["st.ossq", "c.eps"], w=["st.s64"])
            S.add("vector", lambda e: e.reciprocal(stc(ORS, 64), stc(SCR64, 64)),
                  r=["st.s64"], w=["st.ors"])
            for c in range(CH):
                for h in range(H):
                    S.add("vector", lambda e, c=c, h=h: e.tensor_scalar(
                        ox1[:, D * c + DV * h:D * c + DV * (h + 1)],
                        ox1[:, D * c + DV * h:D * c + DV * (h + 1)],
                        stat[:, OMU + 8 * c + h:OMU + 8 * c + h + 1],
                        stat[:, ORS + 8 * c + h:ORS + 8 * c + h + 1],
                        OP.subtract, OP.mult),
                        r=["ox1", "st.omu", "st.ors"], w=["ox1"])
            for tn in range(NT1, NTT):
                tern_ops(tn)

            S.section(40)
            if True:
                S.add("gpsimd", lambda e: e.collective_compute(
                    "AllGather", OP.bypass, replica_groups=[list(range(NC))],
                    ins=[t3i.ap()], outs=[t3o.ap()]),
                    cls="cc.t2", r=["d.t2i"], w=["d.t2o"])

                S.emit(nc, sems)

        S.section(50)
        s3 = ExitStack()
        with s3:
            ooT = s3.enter_context(nc.sbuf_tensor("ooT", [128, NKD * TOK], BF16))
            gt2 = [s3.enter_context(nc.sbuf_tensor(f"gt2{i}", [128, 2048], BF16))
                   for i in range(2)]
            att2 = [s3.enter_context(nc.sbuf_tensor(f"att2{i}", [128, 2048], BF16))
                    for i in range(2)]
            xs = [s3.enter_context(nc.sbuf_tensor(f"xs{i}", [128, 1024], F32))
                  for i in range(2)]
            Wf = [s3.enter_context(nc.sbuf_tensor(f"Wf{i}", [128, 256], F32))
                  for i in range(2)]

            def wload(wd_src, k, pr, wb, wbr):
                src = wd_src[128 * k:128 * (k + 1), 256 * pr:256 * (pr + 1)]
                if k % 2 == 0:
                    S.dma("gpsimd", wb[:, 0:256], src, w=[wbr])
                else:
                    wf = Wf[(k // 2) % 2]
                    wfr = f"Wf{(k // 2) % 2}"
                    S.dma("sync", wf[:, :], src, w=[wfr])
                    S.add("scalar", lambda e, wf=wf, wb=wb: e.copy(
                        wb[:, 0:256], wf[:, :]), r=[wfr], w=[wbr])

            ox1v = ox1.ap().rearrange("p (c f) -> p c f", c=CH)
            S.add("gpsimd", lambda e: e.collective_compute(
                "AllGather", OP.bypass, replica_groups=[list(range(NC))],
                ins=[t1i.ap()], outs=[t1o.ap()]),
                cls="cc.t0", r=["d.t0i"], w=["d.t0o"])
            tap("oo_pre", ox1[:, :], "ox1") if False else None
            for pg in range(8):
                reg = 2048 * (pg % 2)
                gt = gt2[pg % 2]
                gtr = f"gt2{pg % 2}"
                for k in range(NKD):
                    wb = Wb[(pg * NKD + k) % 4]
                    wbr = f"Wb{(pg * NKD + k) % 4}"
                    wload(wg_d, k, pg, wb, wbr)
                    for mb in range(2):
                        for hf in range(2):
                            off = reg + 1024 * mb + 512 * hf
                            S.mm(PS[:, off:off + 512],
                                 wb[:, 128 * mb:128 * (mb + 1)],
                                 xTs(k, 512 * hf, 512 * (hf + 1)),
                                 start=(k == 0), stop=(k == NKD - 1),
                                 r=[wbr, "xT"], w=pres(off, 512))
                for mb in range(2):
                    fb = 2 * pg + mb
                    off = reg + 1024 * mb
                    S.add("scalar", lambda e, fb=fb, off=off, gt=gt, mb=mb:
                          e.activation(gt[:, 1024 * mb:1024 * (mb + 1)],
                                       PS[:, off:off + 1024],
                                       AF.Silu, bias=bgT[:, fb:fb + 1]),
                          r=pres(off, 1024) + ["c.bgT"], w=[gtr])
                for mb in range(2):
                    for c in range(CH):
                        off = reg + 1024 * mb + 128 * c
                        S.mm(PS[:, off:off + 128],
                             gt[:, 1024 * mb + 128 * c:1024 * mb + 128 * (c + 1)],
                             idn[:, :], r=[gtr, "c.idn"], w=pres(off, 128))
                for mb in range(2):
                    fb = 2 * pg + mb
                    ov = v3(ox1, CH * D, 128 * fb, D, CH, 1, 128)
                    psv = v3(PS, 4096, reg + 1024 * mb, 128, CH, 1, 128)
                    S.add("vector", lambda e, a=ov, b=psv: e.tensor_mul(a, b, a),
                          r=pres(reg + 1024 * mb, 1024) + ["ox1"], w=["ox1"])
            tap("oo", ox1[:, :], "ox1")

            for kf in range(NKD):
                reg = 2048 * (kf % 2)
                for c in range(CH):
                    off = reg + 128 * c
                    S.mm(PS[:, off:off + 128],
                         ox1[:, D * c + 128 * kf:D * c + 128 * (kf + 1)], idn[:, :],
                         r=["ox1", "c.idn"], w=pres(off, 128))
                S.add("vector", lambda e, kf=kf, reg=reg: e.tensor_copy(
                    ooT[:, TOK * kf:TOK * (kf + 1)], PS[:, reg:reg + 1024]),
                    r=pres(reg, 1024), w=["ooT"])

            for po in range(8):
                reg = 2048 * (po % 2)
                at = att2[po % 2]
                atr = f"att2{po % 2}"
                for k in range(NKD):
                    wb = Wb[(po * NKD + k) % 4]
                    wbr = f"Wb{(po * NKD + k) % 4}"
                    wload(wo_d, k, po, wb, wbr)
                    for mb in range(2):
                        for hf in range(2):
                            off = reg + 1024 * mb + 512 * hf
                            S.mm(PS[:, off:off + 512],
                                 wb[:, 128 * mb:128 * (mb + 1)],
                                 ooT[:, TOK * k + 512 * hf:TOK * k + 512 * (hf + 1)],
                                 start=(k == 0), stop=(k == NKD - 1),
                                 r=[wbr, "ooT"], w=pres(off, 512))
                S.add("vector", lambda e, reg=reg, at=at:
                      e.tensor_copy(at[:, :], PS[:, reg:reg + 2048]),
                      r=pres(reg, 2048), w=[atr])
                for mb in range(2):
                    for c in range(CH):
                        off = reg + 1024 * mb + 128 * c
                        S.mm(PS[:, off:off + 128],
                             at[:, 1024 * mb + 128 * c:1024 * mb + 128 * (c + 1)],
                             idn[:, :], r=[atr, "c.idn"], w=pres(off, 128))
                for mb in range(2):
                    ob = 2 * po + mb
                    xsb = xs[mb]
                    xsr = f"xs{mb}"
                    xin = bass.AP(x_d, 128 * ob, [[D, 128], [128 * D, CH], [1, 128]])
                    xsv = v3(xsb, 1024, 0, 128, CH, 1, 128)
                    S.dma("sync", xsv, xin, w=[xsr])
                    ov = v3(ox1, CH * D, 128 * ob, D, CH, 1, 128)
                    psv = v3(PS, 4096, reg + 1024 * mb, 128, CH, 1, 128)
                    S.add("vector", lambda e, o=ov, p=psv, xv=xsv:
                          e.scalar_tensor_tensor(o, p, 1.0, xv, OP.mult, OP.add),
                          r=pres(reg + 1024 * mb, 1024) + [xsr, "ox1"], w=["ox1"])
            S.add("gpsimd", lambda e: e.collective_compute(
                "AllGather", OP.bypass, replica_groups=[list(range(NC))],
                ins=[t2i.ap()], outs=[t2o.ap()]),
                cls="cc.t1", r=["d.t1i"], w=["d.t1o"])

            for c in range(CH):
                S.add("vector", lambda e, c=c: e.reduce_sum(
                    stat[:, XMU + c:XMU + c + 1], ox1[:, D * c:D * (c + 1)], AX),
                    r=["ox1"], w=["st.xmu"])
                S.add("scalar", lambda e, c=c: e.activation(
                    att2[0][:, :], ox1[:, D * c:D * (c + 1)], AF.Square,
                    accum_out=stat[:, XSSQ + c:XSSQ + c + 1]),
                    r=["ox1"], w=["att20", "st.xssq"])
            S.add("vector", lambda e: e.tensor_scalar(
                stc(XMU, 8), stc(XMU, 8), 1.0 / D, None, OP.mult),
                r=["st.xmu"], w=["st.xmu"])
            S.add("vector", lambda e: e.tensor_scalar(
                stc(XSSQ, 8), stc(XSSQ, 8), 1.0 / D, None, OP.mult),
                r=["st.xssq"], w=["st.xssq"])
            S.add("vector", lambda e: e.tensor_mul(
                stc(SCR8, 8), stc(XMU, 8), stc(XMU, 8)), r=["st.xmu"], w=["st.s8"])
            S.add("vector", lambda e: e.tensor_sub(
                stc(XSSQ, 8), stc(XSSQ, 8), stc(SCR8, 8)),
                r=["st.xssq", "st.s8"], w=["st.xssq"])
            S.add("scalar", lambda e: e.activation(
                stc(SCR8, 8), stc(XSSQ, 8), AF.Sqrt, bias=eps_c[:, :]),
                r=["st.xssq", "c.eps"], w=["st.s8"])
            S.add("vector", lambda e: e.reciprocal(stc(XRS, 8), stc(SCR8, 8)),
                  r=["st.s8"], w=["st.xrs"])
            for c in range(CH):
                S.add("vector", lambda e, c=c: e.tensor_scalar(
                    ox1[:, D * c:D * (c + 1)], ox1[:, D * c:D * (c + 1)],
                    stat[:, XMU + c:XMU + c + 1], stat[:, XRS + c:XRS + c + 1],
                    OP.subtract, OP.mult),
                    r=["ox1", "st.xmu", "st.xrs"], w=["ox1"])
            tap("x1", ox1[:, :], "ox1")
            S.emit(nc, sems)

        S.section(60)
        s4 = ExitStack()
        with s4:
            rscr = s4.enter_context(nc.sbuf_tensor("rscr", [128, 2048], F32))
            f1scr = s4.enter_context(nc.sbuf_tensor("f1scr", [128, 1024], F32))
            pfscr = s4.enter_context(nc.sbuf_tensor("pfscr", [128, 1024], F32))
            ri = s4.enter_context(nc.sbuf_tensor("ri", [128, 1024], I32))
            zbf = s4.enter_context(nc.sbuf_tensor("zbf", [128, 2048], BF16))
            B1 = s4.enter_context(nc.sbuf_tensor("B1", [128, 1024], F32))
            B3 = s4.enter_context(nc.sbuf_tensor("B3", [128, 1024], F32))
            Bh = s4.enter_context(nc.sbuf_tensor("Bh", [128, 1024], F32))
            Bo = s4.enter_context(nc.sbuf_tensor("Bo", [128, 1024], F32))
            hbb = [s4.enter_context(nc.sbuf_tensor(f"hbb{i}", [128, 1024], BF16))
                   for i in range(2)]
            zhn = [s4.enter_context(nc.sbuf_tensor(f"zhn{i}", [128, 1024], BF16))
                   for i in range(2)]

            def round_neg(src_off, dst_bf16, w, dst_res):
                src = rscr[:, src_off:src_off + w]
                S.add("vector", lambda e, s_=src, w=w: e.tensor_copy(ri[:, 0:w], s_),
                      r=["rscr"], w=["ri"])
                S.add("vector", lambda e, w=w: e.tensor_copy(f1scr[:, 0:w], ri[:, 0:w]),
                      r=["ri"], w=["f1scr"])
                S.add("vector", lambda e, s_=src, w=w: e.tensor_sub(
                    pfscr[:, 0:w], s_, f1scr[:, 0:w]),
                    r=["rscr", "f1scr"], w=["pfscr"])
                S.add("vector", lambda e, w=w: e.scalar_tensor_tensor(
                    f1scr[:, 0:w], pfscr[:, 0:w], 0.5, f1scr[:, 0:w], OP.is_ge, OP.add),
                    r=["pfscr", "f1scr"], w=["f1scr"])
                S.add("vector", lambda e, w=w, d=dst_bf16: e.scalar_tensor_tensor(
                    d, pfscr[:, 0:w], -0.5, f1scr[:, 0:w], OP.is_le, OP.subtract),
                    r=["pfscr", "f1scr"], w=[dst_res])

            for c in range(CH):
                S.add("vector", lambda e, c=c: e.reduce_max(
                    stat[:, SX + c:SX + c + 1], ox1[:, D * c:D * (c + 1)], AX,
                    apply_absolute_value=True), r=["ox1"], w=["st.sx"])
            S.add("vector", lambda e: e.tensor_scalar(
                stc(CMX, 8), stc(SX, 8), EPS, None, OP.max),
                r=["st.sx"], w=["st.cmx"])
            S.add("vector", lambda e: e.reciprocal(stc(SX, 8), stc(CMX, 8)),
                  r=["st.cmx"], w=["st.sx"])
            S.add("vector", lambda e: e.tensor_scalar(
                stc(SX, 8), stc(SX, 8), 127.0, None, OP.mult),
                r=["st.sx"], w=["st.sx"])
            for c in range(CH):
                S.add("vector", lambda e, c=c: e.tensor_scalar(
                    rscr[:, :], ox1[:, D * c:D * (c + 1)],
                    stat[:, SX + c:SX + c + 1], None, OP.mult),
                    r=["ox1", "st.sx", "rscr"], w=["rscr"])
                round_neg(0, zbf[:, 0:1024], 1024, "zbf")
                round_neg(1024, zbf[:, 1024:2048], 1024, "zbf")
                transpose_tiles([zbf[:, 128 * t:128 * (t + 1)] for t in range(NKD)],
                                "zbf", v3(xT, NQ, CT * c, TOK, NKD, 1, CT), "xT",
                                ps_off=2048 * (c % 2))
            tap("zxT", xT[:, :], "xT")

            S.add("tensor", lambda e: e.transpose(PS[0:8, 0:128], stc(CMX, 8),
                                                  idnf[:, :]),
                  r=["st.cmx", "c.idnf"], w=pres(0, 128))
            S.add("vector", lambda e: e.tensor_copy(rtmp[:, :], PS[0:8, 0:128]),
                  r=pres(0, 128), w=["rtmp"])
            S.dma("sync", rowb[:, :], rtmp[0:8, 0:128], r=["rtmp"], w=["d.rowb"])
            S.dma("sync", row1[0:1, :], rowb.ap().rearrange("a b -> (a b)").rearrange("(o n) -> o n", o=1),
                  r=["d.rowb"], w=["row1"])
            for (Bt, Btr, mi) in ((B1, "B1", 8), (B3, "B3", 10)):
                S.add("vector", lambda e, mi=mi: e.tensor_scalar(
                    row2[0:1, :], row1[0:1, :], msc[0:1, mi:mi + 1],
                    -1.0 / 127.0, OP.mult, OP.mult),
                    r=["row1", "msc", "row2"], w=["row2"])
                for hf in range(2):
                    S.mm(PS[:, 512 * hf:512 * (hf + 1)], onesr[:, :],
                         row2[0:1, 512 * hf:512 * (hf + 1)],
                         r=["c.onesr", "row2"], w=pres(512 * hf, 512))
                S.add("vector", lambda e, b=Bt: e.tensor_copy(b[:, :], PS[:, 0:1024]),
                      r=pres(0, 1024), w=[Btr])

            s4a = ExitStack()
            with s4a:
                Mx = s4a.enter_context(nc.sbuf_tensor("Mx", [128, 1024], F32))
                Mn = s4a.enter_context(nc.sbuf_tensor("Mn", [128, 1024], F32))
                T1w = [s4a.enter_context(
                    nc.sbuf_tensor(f"T1w{i}", [128, NKD * WIN * 128], BF16))
                    for i in range(2)]
                T3w = [s4a.enter_context(
                    nc.sbuf_tensor(f"T3w{i}", [128, NKD * WIN * 128], BF16))
                    for i in range(2)]
                S.add("vector", lambda e: e.memset(Mx[:, :], 0.0), w=["Mx"])
                S.add("vector", lambda e: e.memset(Mn[:, :], 0.0), w=["Mn"])
                wins = []
                hb0 = 0
                while hb0 < NHB:
                    wins.append((hb0, min(WIN, NHB - hb0)))
                    hb0 += WIN
                WW = WIN * 128

                def win_loads(wi):
                    if wi >= len(wins):
                        return
                    h0, nhb = wins[wi]
                    t1 = T1w[wi % 2]
                    t3 = T3w[wi % 2]
                    for k in range(NKD):
                        S.dma("sync", t1[:, WW * k:WW * k + 128 * nhb],
                              t1o[128 * k:128 * (k + 1), 128 * h0:128 * (h0 + nhb)],
                              r=["d.t0o"], w=[f"T1w{wi % 2}"])
                        S.dma("scalar", t3[:, WW * k:WW * k + 128 * nhb],
                              t3o[128 * k:128 * (k + 1), 128 * h0:128 * (h0 + nhb)],
                              r=["d.t2o"], w=[f"T3w{wi % 2}"])

                win_loads(0)
                win_loads(1)
                for wi, (h0, nhb) in enumerate(wins):
                    t1 = T1w[wi % 2]
                    t1r = f"T1w{wi % 2}"
                    t3 = T3w[wi % 2]
                    t3r = f"T3w{wi % 2}"
                    for j in range(nhb):
                        hb = h0 + j
                        reg = 2048 * (hb % 2)
                        hbbb = hbb[hb % 2]
                        hbr = f"hbb{hb % 2}"
                        for k in range(NKD):
                            for hf in range(2):
                                S.mm(PS[:, reg + 512 * hf:reg + 512 * (hf + 1)],
                                     t1[:, WW * k + 128 * j:WW * k + 128 * (j + 1)],
                                     xTs(k, 512 * hf, 512 * (hf + 1)),
                                     start=(k == 0), stop=(k == NKD - 1),
                                     r=[t1r, "xT"], w=pres(reg + 512 * hf, 512))
                                S.mm(PS[:, reg + 1024 + 512 * hf:reg + 1024 + 512 * (hf + 1)],
                                     t3[:, WW * k + 128 * j:WW * k + 128 * (j + 1)],
                                     xTs(k, 512 * hf, 512 * (hf + 1)),
                                     start=(k == 0), stop=(k == NKD - 1),
                                     r=[t3r, "xT"], w=pres(reg + 1024 + 512 * hf, 512))
                        S.add("vector", lambda e, reg=reg: e.tensor_mul(
                            rscr[:, 0:1024], PS[:, reg:reg + 1024], B1[:, :]),
                            r=pres(reg, 1024) + ["B1", "rscr"], w=["rscr"])
                        S.add("scalar", lambda e: e.activation(
                            f1scr[:, 0:1024], rscr[:, 0:1024], AF.Silu),
                            r=["rscr"], w=["f1scr"])
                        S.add("vector", lambda e: e.tensor_mul(
                            f1scr[:, 0:1024], f1scr[:, 0:1024], B3[:, :]),
                            r=["f1scr", "B3"], w=["f1scr"])
                        S.add("vector", lambda e, reg=reg, hbbb=hbbb: e.tensor_mul(
                            hbbb[:, :], f1scr[:, 0:1024], PS[:, reg + 1024:reg + 2048]),
                            r=["f1scr"] + pres(reg + 1024, 1024), w=[hbr])
                        S.add("vector", lambda e, hbbb=hbbb: e.tensor_tensor(
                            Mx[:, :], Mx[:, :], hbbb[:, :], OP.max),
                            r=["Mx", hbr], w=["Mx"])
                        S.add("vector", lambda e, hbbb=hbbb: e.tensor_tensor(
                            Mn[:, :], Mn[:, :], hbbb[:, :], OP.min),
                            r=["Mn", hbr], w=["Mn"])
                        S.dma("gpsimd", h_dr[128 * hb:128 * (hb + 1), :], hbbb[:, :],
                              r=[hbr], w=["d.h"])
                    win_loads(wi + 2)

                S.add("vector", lambda e: e.tensor_scalar(
                    Mn[:, :], Mn[:, :], -1.0, None, OP.mult), r=["Mn"], w=["Mn"])
                S.add("vector", lambda e: e.tensor_max(Mx[:, :], Mx[:, :], Mn[:, :]),
                      r=["Mx", "Mn"], w=["Mx"])
                S.add("vector", lambda e: e.tensor_copy(zbf[:, 0:1024], Mx[:, :]),
                      r=["Mx", "zbf"], w=["zbf"])
                for c in range(CH):
                    S.mm(PS[:, 128 * c:128 * (c + 1)], zbf[:, 128 * c:128 * (c + 1)],
                         idn[:, :], r=["zbf", "c.idn"], w=pres(128 * c, 128))
                S.add("vector", lambda e: e.tensor_copy(rscr[:, 0:1024], PS[:, 0:1024]),
                      r=pres(0, 1024) + ["rscr"], w=["rscr"])
                for c in range(CH):
                    S.add("vector", lambda e, c=c: e.reduce_max(
                        stat[:, HCM + c:HCM + c + 1], rscr[:, 128 * c:128 * (c + 1)], AX),
                        r=["rscr"], w=["st.hcm"])
                S.add("vector", lambda e: e.tensor_scalar(
                    stc(HCM, 8), stc(HCM, 8), EPS, None, OP.max),
                    r=["st.hcm"], w=["st.hcm"])
                S.add("tensor", lambda e: e.transpose(PS[0:8, 0:128], stc(HCM, 8),
                                                      idnf[:, :]),
                      r=["st.hcm", "c.idnf"], w=pres(0, 128))
                S.add("vector", lambda e: e.tensor_copy(rtmp[:, :], PS[0:8, 0:128]),
                      r=pres(0, 128), w=["rtmp"])
                S.dma("sync", rowb[:, :], rtmp[0:8, 0:128], r=["rtmp"], w=["d.rowb"])
                S.dma("sync", row1[0:1, :], rowb.ap().rearrange("a b -> (a b)").rearrange("(o n) -> o n", o=1),
                      r=["d.rowb"], w=["row1"])
                S.add("vector", lambda e: e.reciprocal(row2[0:1, :], row1[0:1, :]),
                      r=["row1", "row2"], w=["row2"])
                S.add("vector", lambda e: e.tensor_scalar(
                    row2[0:1, :], row2[0:1, :], 127.0, None, OP.mult),
                    r=["row2"], w=["row2"])
                for hf in range(2):
                    S.mm(PS[:, 512 * hf:512 * (hf + 1)], onesr[:, :],
                         row2[0:1, 512 * hf:512 * (hf + 1)],
                         r=["c.onesr", "row2"], w=pres(512 * hf, 512))
                S.add("vector", lambda e: e.tensor_copy(Bh[:, :], PS[:, 0:1024]),
                      r=pres(0, 1024), w=["Bh"])
                S.add("vector", lambda e: e.tensor_scalar(
                    row2[0:1, :], row1[0:1, :], msc[0:1, 9:10],
                    -1.0 / 127.0, OP.mult, OP.mult),
                    r=["row1", "msc", "row2"], w=["row2"])
                for hf in range(2):
                    S.mm(PS[:, 2048 + 512 * hf:2048 + 512 * (hf + 1)], onesr[:, :],
                         row2[0:1, 512 * hf:512 * (hf + 1)],
                         r=["c.onesr", "row2"], w=pres(2048 + 512 * hf, 512))
                S.add("vector", lambda e: e.tensor_copy(Bo[:, :], PS[:, 2048:3072]),
                      r=pres(2048, 1024), w=["Bo"])
                S.emit(nc, sems)

            s4b = ExitStack()
            with s4b:
                T2w = [s4b.enter_context(nc.sbuf_tensor(f"T2w{i}", [128, 512], BF16))
                       for i in range(2)]
                x2p = s4b.enter_context(nc.sbuf_tensor("x2p", [128, CH * D], BF16))
                ft = [s4b.enter_context(nc.sbuf_tensor(f"ft{i}", [128, 4096], BF16))
                      for i in range(2)]
                outst2 = [s4b.enter_context(nc.sbuf_tensor(f"outst{i}", [128, 2048],
                                                           F32)) for i in range(2)]

                for og in range(4):
                    ftb = ft[og % 2]
                    ftr = f"ft{og % 2}"
                    for k in range(NHB):
                        hbbb = hbb[k % 2]
                        hbr = f"hbb{k % 2}"
                        zb = zhn[k % 2]
                        zbr = f"zhn{k % 2}"
                        if og == 0:
                            S.dma("gpsimd", hbbb[:, :], h_dr[128 * k:128 * (k + 1), :],
                                  r=["d.h"], w=[hbr])
                            S.add("vector", lambda e, hbbb=hbbb: e.tensor_mul(
                                rscr[:, 0:1024], hbbb[:, :], Bh[:, :]),
                                r=[hbr, "Bh", "rscr"], w=["rscr"])
                            round_neg(0, zb[:, :], 1024, zbr)
                            S.dma("sync", zh_dr[128 * k:128 * (k + 1), :], zb[:, :],
                                  r=[zbr], w=["d.zh"])
                        else:
                            S.dma("gpsimd", zb[:, :], zh_dr[128 * k:128 * (k + 1), :],
                                  r=["d.zh"], w=[zbr])
                        t2 = T2w[k % 2]
                        t2r = f"T2w{k % 2}"
                        S.dma("scalar", t2[:, :],
                              t2o[128 * k:128 * (k + 1), 512 * og:512 * (og + 1)],
                              r=["d.t1o"], w=[t2r])
                        for ob in range(4):
                            for hf in range(2):
                                off = 1024 * ob + 512 * hf
                                S.mm(PS[:, off:off + 512],
                                     t2[:, 128 * ob:128 * (ob + 1)],
                                     zb[:, 512 * hf:512 * (hf + 1)],
                                     start=(k == 0), stop=(k == NHB - 1),
                                     r=[t2r, zbr], w=pres(off, 512))
                    for ob in range(4):
                        S.add("vector", lambda e, ob=ob, ftb=ftb: e.tensor_mul(
                            ftb[:, 1024 * ob:1024 * (ob + 1)],
                            PS[:, 1024 * ob:1024 * (ob + 1)], Bo[:, :]),
                            r=pres(1024 * ob, 1024) + ["Bo"], w=[ftr])
                    for ob in range(4):
                        for c in range(CH):
                            off = 1024 * ob + 128 * c
                            S.mm(PS[:, off:off + 128],
                                 ftb[:, off:off + 128], idn[:, :],
                                 r=[ftr, "c.idn"], w=pres(off, 128))
                    for ob in range(4):
                        fb = 4 * og + ob
                        x2v = v3(x2p, CH * D, 128 * fb, D, CH, 1, 128)
                        x1v = v3(ox1, CH * D, 128 * fb, D, CH, 1, 128)
                        psv = v3(PS, 4096, 1024 * ob, 128, CH, 1, 128)
                        S.add("vector", lambda e, a=x2v, p=psv, b=x1v:
                              e.scalar_tensor_tensor(a, p, 1.0, b, OP.mult, OP.add),
                              r=pres(1024 * ob, 1024) + ["ox1", "x2p"], w=["x2p"])
                tap("x2p", x2p[:, :], "x2p")
                tap("stat", stat[:, :], "st.dc")
                if DBG:
                    S.dma("sync", tp["msc"][:, :], msc[:, :], r=["msc"])

                for c in range(CH):
                    S.add("vector", lambda e, c=c: e.reduce_sum(
                        stat[:, X2MU + c:X2MU + c + 1], x2p[:, D * c:D * (c + 1)], AX),
                        r=["x2p"], w=["st.x2mu"])
                    S.add("scalar", lambda e, c=c: e.activation(
                        rscr[:, :], x2p[:, D * c:D * (c + 1)], AF.Square,
                        accum_out=stat[:, X2SSQ + c:X2SSQ + c + 1]),
                        r=["x2p"], w=["rscr", "st.x2ssq"])
                S.add("vector", lambda e: e.tensor_scalar(
                    stc(X2MU, 8), stc(X2MU, 8), 1.0 / D, None, OP.mult),
                    r=["st.x2mu"], w=["st.x2mu"])
                S.add("vector", lambda e: e.tensor_scalar(
                    stc(X2SSQ, 8), stc(X2SSQ, 8), 1.0 / D, None, OP.mult),
                    r=["st.x2ssq"], w=["st.x2ssq"])
                S.add("vector", lambda e: e.tensor_mul(
                    stc(SCR8, 8), stc(X2MU, 8), stc(X2MU, 8)),
                    r=["st.x2mu"], w=["st.s8"])
                S.add("vector", lambda e: e.tensor_sub(
                    stc(X2SSQ, 8), stc(X2SSQ, 8), stc(SCR8, 8)),
                    r=["st.x2ssq", "st.s8"], w=["st.x2ssq"])
                S.add("scalar", lambda e: e.activation(
                    stc(SCR8, 8), stc(X2SSQ, 8), AF.Sqrt, bias=eps_c[:, :]),
                    r=["st.x2ssq", "c.eps"], w=["st.s8"])
                S.add("vector", lambda e: e.reciprocal(stc(X2RS, 8), stc(SCR8, 8)),
                      r=["st.s8"], w=["st.x2rs"])
                for c in range(CH):
                    ob2 = outst2[c % 2]
                    obr = f"outst{c % 2}"
                    S.add("vector", lambda e, c=c, ob2=ob2: e.tensor_scalar(
                        ob2[:, :], x2p[:, D * c:D * (c + 1)],
                        stat[:, X2MU + c:X2MU + c + 1],
                        stat[:, X2RS + c:X2RS + c + 1], OP.subtract, OP.mult),
                        r=["x2p", "st.x2mu", "st.x2rs", obr], w=[obr])
                    S.dma(("sync", "scalar")[c % 2], out_d[CT * c:CT * (c + 1), :],
                          ob2[:, :], r=[obr], w=["d.out"])
                S.emit(nc, sems)
        S.enabled = True
        S.add("gpsimd", lambda e: None, barrier=True)
        S.emit(nc, sems)
    return nc


def kernel(L0_attn_x, L0_attn_Wq, L0_attn_Wk, L0_attn_Wkg1, L0_attn_Wkg2,
           L0_attn_bkg2, L0_attn_Wv, L0_attn_Wg, L0_attn_bg, L0_attn_Wo,
           L0_ffn_w1, L0_ffn_w2, L0_ffn_w3):
    f32 = np.float32
    bf16 = ml_dtypes.bfloat16
    x = np.ascontiguousarray(np.asarray(L0_attn_x, f32).reshape(NC * TOK, D))
    wq = np.ascontiguousarray(np.asarray(L0_attn_Wq, f32))
    wk = np.ascontiguousarray(np.asarray(L0_attn_Wk, f32))
    wkg1 = np.ascontiguousarray(np.asarray(L0_attn_Wkg1, f32))
    wkg2 = np.ascontiguousarray(np.asarray(L0_attn_Wkg2, f32))
    bkg2T = np.ascontiguousarray(np.asarray(L0_attn_bkg2, f32).reshape(8, 128).T)
    wv = np.ascontiguousarray(np.asarray(L0_attn_Wv, f32))
    wg = np.ascontiguousarray(np.asarray(L0_attn_Wg, f32))
    bgT = np.ascontiguousarray(np.asarray(L0_attn_bg, f32).reshape(16, 128).T)
    wo = np.ascontiguousarray(np.asarray(L0_attn_Wo, f32))
    w1 = np.ascontiguousarray(np.asarray(L0_ffn_w1, f32))
    w2 = np.ascontiguousarray(np.asarray(L0_ffn_w2, f32))
    w3 = np.ascontiguousarray(np.asarray(L0_ffn_w3, f32))

    idn = np.eye(128).astype(bf16)
    idnf = np.eye(128).astype(f32)
    tri = np.triu(np.ones((128, 128), f32))
    tri8 = np.tile(tri, (1, 8)).astype(bf16)

    nc = build()
    in_maps = []
    for i in range(NC):
        m = np.zeros((128, 16), f32)
        m[:, 0:i] = 1.0
        m[:, 8:16] = 1.0 - m[:, 0:8]
        in_maps.append({
            "x": x[TOK * i:TOK * (i + 1)],
            "wq": wq, "wk": wk, "wkg1": wkg1, "wkg2": wkg2, "bkg2T": bkg2T,
            "wv": wv, "wg": wg, "bgT": bgT, "wo": wo,
            "w1s": np.ascontiguousarray(w1.T[(D // NC) * i:(D // NC) * (i + 1)]),
            "w2s": np.ascontiguousarray(w2.T[(HID // NC) * i:(HID // NC) * (i + 1)]),
            "w3s": np.ascontiguousarray(w3.T[(D // NC) * i:(D // NC) * (i + 1)]),
            "idn": idn, "idnf": idnf, "tri8": tri8, "scanm": m,
        })
    res = run_bass_kernel_spmd(nc, in_maps, core_ids=list(range(NC)))
    LAST["res"] = res
    out = np.concatenate([res.results[i]["out"] for i in range(NC)], axis=0)
    return out.reshape(1, NC * TOK, D).astype(np.float32)


# version 16
# speedup vs baseline: 1.0690x; 1.0633x over previous
import numpy as np
import ml_dtypes
import concourse.bass as bass
import concourse.mybir as mybir
from concourse.bass_utils import run_bass_kernel_spmd
from contextlib import ExitStack

F32 = mybir.dt.float32
BF16 = mybir.dt.bfloat16
I32 = mybir.dt.int32
I8 = mybir.dt.int8
AF = mybir.ActivationFunctionType
OP = mybir.AluOpType
AX = mybir.AxisListType.X

NC = 8
TOK = 1024
CH = 8
CT = 128
D = 2048
H = 8
DV = 256
HID = 5632
NHB = HID // 128
NKD = D // 128
EPS = 1e-5
KS = 1.0 / 16.0
NWINV = 1.0 / float(HID * D)
WIN = 4
DBG = False
STOP = 99
LAST = {}

QUANT = {"pe": 1, "v": 1, "s": 1, "g": 1}


def quant(cls):
    return 16 if cls.startswith("D.") else 1
ENG_CLS = {"tensor": "pe", "vector": "v", "scalar": "s", "gpsimd": "g"}


class Sched:

    def __init__(self):
        self.steps = []
        self.ord_ = {c: 0 for c in QUANT}
        self.lastw = {}
        self.readers = {}
        self.enabled = True

    def section(self, n):
        self.enabled = (n <= STOP)

    def _dep_add(self, waits, cls, o):
        if o > waits.get(cls, 0):
            waits[cls] = o

    def add(self, eng, fn, *, cls=None, r=(), w=(), barrier=False):
        if not self.enabled and not barrier:
            return
        if cls is None:
            cls = ENG_CLS[eng]
        if cls not in self.ord_:
            self.ord_[cls] = 0
        ordinal = self.ord_[cls] + 1
        waits = {}
        if barrier:
            for c, o in self.ord_.items():
                if o > 0:
                    waits[c] = o
        for res in r:
            lw = self.lastw.get(res)
            if lw:
                self._dep_add(waits, lw[0], lw[1])
        for res in w:
            lw = self.lastw.get(res)
            if lw:
                self._dep_add(waits, lw[0], lw[1])
            for (c, o) in self.readers.get(res, ()):
                self._dep_add(waits, c, o)
        for res in r:
            self.readers.setdefault(res, []).append((cls, ordinal))
        for res in w:
            self.lastw[res] = (cls, ordinal)
            self.readers[res] = []
        self.steps.append((eng, cls, fn, ordinal, waits))
        self.ord_[cls] += 1

    def dma(self, eng, out, in_, *, r=(), w=(), **kw):
        res = w[0] if w else "misc"
        self.add(eng, lambda e, o=out, i=in_, k=dict(kw): e.dma_start(o, i, **k),
                 cls="D." + res, r=r, w=w)

    def mm(self, out, lhsT, rhs, start=True, stop=True, *, r=(), w=()):
        self.add("tensor",
                 lambda e, o=out, l=lhsT, rr=rhs, s1=start, s2=stop:
                 e.matmul(o, l, rr, start=s1, stop=s2), r=r, w=w)

    def emit(self, nc, sems):
        steps = self.steps
        with nc.Block() as block:
            SKIP_SAME = {"tensor": "pe", "scalar": "s", "gpsimd": "g"}

            def mk(name):
                def body(e):
                    wm = {}
                    mycls = SKIP_SAME.get(name)
                    for (eng, cls, fn, ordinal, waits) in steps:
                        if eng != name:
                            continue
                        for c, o in sorted(waits.items()):
                            if c == mycls:
                                continue
                            if o > wm.get(c, 0):
                                e.wait_ge(sems(c), o * quant(c))
                                wm[c] = o
                        ins = fn(e)
                        if ins is not None:
                            ins.then_inc(sems(cls), quant(cls))
                return body
            block.sync(mk("sync"))
            block.gpsimd(mk("gpsimd"))
            block.vector(mk("vector"))
            block.scalar(mk("scalar"))
            block.tensor(mk("tensor"))
        self.steps = []


def pres(off, n):
    return [f"P{b}" for b in range(off // 512, (off + n - 1) // 512 + 1)]


def build():
    nc = bass.Bass("TRN2", target_bir_lowering=False)

    x_d = nc.dram_tensor("x", [TOK, D], F32, kind="ExternalInput")
    wq_d = nc.dram_tensor("wq", [D, 1024], F32, kind="ExternalInput")
    wk_d = nc.dram_tensor("wk", [D, 1024], F32, kind="ExternalInput")
    wkg1_d = nc.dram_tensor("wkg1", [D, 16], F32, kind="ExternalInput")
    wkg2_d = nc.dram_tensor("wkg2", [16, 1024], F32, kind="ExternalInput")
    bkg2T_d = nc.dram_tensor("bkg2T", [128, 8], F32, kind="ExternalInput")
    wv_d = nc.dram_tensor("wv", [D, D], F32, kind="ExternalInput")
    wg_d = nc.dram_tensor("wg", [D, D], F32, kind="ExternalInput")
    bgT_d = nc.dram_tensor("bgT", [128, 16], F32, kind="ExternalInput")
    wo_d = nc.dram_tensor("wo", [D, D], F32, kind="ExternalInput")
    w1s_d = nc.dram_tensor("w1s", [D // NC, HID], F32, kind="ExternalInput")
    w2s_d = nc.dram_tensor("w2s", [HID // NC, D], F32, kind="ExternalInput")
    w3s_d = nc.dram_tensor("w3s", [D // NC, HID], F32, kind="ExternalInput")
    idn_d = nc.dram_tensor("idn", [128, 128], BF16, kind="ExternalInput")
    idnf_d = nc.dram_tensor("idnf", [128, 128], F32, kind="ExternalInput")
    tri8_d = nc.dram_tensor("tri8", [128, 1024], BF16, kind="ExternalInput")
    scanm_d = nc.dram_tensor("scanm", [128, 16], F32, kind="ExternalInput")
    out_d = nc.dram_tensor("out", [TOK, D], F32, kind="ExternalOutput")

    agi = nc.dram_tensor("agi", [128, 2064], F32)
    ago = nc.dram_tensor("ago", [NC * 128, 2064], F32)
    ari = nc.dram_tensor("ari", [1, 4], F32)
    aro = nc.dram_tensor("aro", [1, 4], F32)
    t1i = nc.dram_tensor("t1i", [D // NC, HID], I8)
    t1o = nc.dram_tensor("t1o", [D, HID], I8)
    t2i = nc.dram_tensor("t2i", [HID // NC, D], I8)
    t2o = nc.dram_tensor("t2o", [HID, D], I8)
    t3i = nc.dram_tensor("t3i", [D // NC, HID], I8)
    t3o = nc.dram_tensor("t3o", [D, HID], I8)
    rowb = nc.dram_tensor("rowb", [8, 128], F32)
    h_dr = nc.dram_tensor("h_dr", [HID, TOK], BF16)
    zh_dr = nc.dram_tensor("zh_dr", [HID, TOK], BF16)
    tp = {}
    if DBG:
        tp["xT"] = nc.dram_tensor("tp_xT", [128, NKD * TOK], BF16, kind="ExternalOutput")
        tp["Q"] = nc.dram_tensor("tp_Q", [128, H * TOK], F32, kind="ExternalOutput")
        tp["qdT"] = nc.dram_tensor("tp_qdT", [128, H * TOK], BF16, kind="ExternalOutput")
        tp["kd"] = nc.dram_tensor("tp_kd", [128, CH * H * 128], BF16, kind="ExternalOutput")
        tp["AT"] = nc.dram_tensor("tp_AT", [128, CH * H * 128], BF16, kind="ExternalOutput")
        tp["V"] = nc.dram_tensor("tp_V", [128, CH * D], BF16, kind="ExternalOutput")
        tp["L"] = nc.dram_tensor("tp_L", [128, 2048], F32, kind="ExternalOutput")
        tp["Sst"] = nc.dram_tensor("tp_Sst", [128, 2048], F32, kind="ExternalOutput")
        tp["orw"] = nc.dram_tensor("tp_orw", [128, CH * D], BF16, kind="ExternalOutput")
        tp["oo"] = nc.dram_tensor("tp_oo", [128, CH * D], BF16, kind="ExternalOutput")
        tp["x1"] = nc.dram_tensor("tp_x1", [128, CH * D], BF16, kind="ExternalOutput")
        tp["stat"] = nc.dram_tensor("tp_stat", [128, 512], F32, kind="ExternalOutput")
        tp["zxT"] = nc.dram_tensor("tp_zxT", [128, NKD * TOK], BF16, kind="ExternalOutput")
        tp["x2p"] = nc.dram_tensor("tp_x2p", [128, CH * D], BF16, kind="ExternalOutput")
        tp["msc"] = nc.dram_tensor("tp_msc", [1, 16], F32, kind="ExternalOutput")

    S = Sched()

    DLS, QLS, OMU, OSSQ, ORS, SCR64 = 0, 64, 72, 136, 200, 264
    XMU, XSSQ, XRS, SX, CMX, HCM = 328, 336, 344, 352, 360, 376
    DCORE, DPC, THR, NTHR, X2MU, X2SSQ, X2RS, SCR8 = 384, 400, 408, 412, 416, 424, 432, 440

    ctx = ExitStack()
    with ctx:
        idn = ctx.enter_context(nc.sbuf_tensor("idn_s", [128, 128], BF16))
        idnf = ctx.enter_context(nc.sbuf_tensor("idnf_s", [128, 128], F32))
        tri8 = ctx.enter_context(nc.sbuf_tensor("tri8_s", [128, 1024], BF16))
        scanm = ctx.enter_context(nc.sbuf_tensor("scanm_s", [128, 16], F32))
        bgT = ctx.enter_context(nc.sbuf_tensor("bgT_s", [128, 16], F32))
        nbkg = ctx.enter_context(nc.sbuf_tensor("nbkg", [128, 8], F32))
        ones_c = ctx.enter_context(nc.sbuf_tensor("ones_c", [128, 1], F32))
        onesr = ctx.enter_context(nc.sbuf_tensor("onesr", [1, 128], F32))
        eps_c = ctx.enter_context(nc.sbuf_tensor("eps_c", [128, 1], F32))
        Wb = [ctx.enter_context(nc.sbuf_tensor(f"Wb{i}", [128, 512], BF16))
              for i in range(4)]
        xT = ctx.enter_context(nc.sbuf_tensor("xT", [128, NKD * TOK], BF16))
        ox1 = ctx.enter_context(nc.sbuf_tensor("ox1", [128, CH * D], BF16))
        stat = ctx.enter_context(nc.sbuf_tensor("stat", [128, 512], F32))
        row1 = ctx.enter_context(nc.sbuf_tensor("row1", [1, 1024], F32))
        row2 = ctx.enter_context(nc.sbuf_tensor("row2", [1, 1024], F32))
        rtmp = ctx.enter_context(nc.sbuf_tensor("rtmp", [8, 128], F32))
        msc = ctx.enter_context(nc.sbuf_tensor("msc", [1, 16], F32))
        PS = ctx.enter_context(nc.psum_tensor("PS", [128, 4096], F32))
        _semmap = {}

        def sems(c):
            if c not in _semmap:
                _semmap[c] = ctx.enter_context(
                    nc.semaphore("sem_%d" % len(_semmap)))
            return _semmap[c]

        NQ = NKD * TOK

        def xTs(k, a, b):
            return xT[:, TOK * k + a: TOK * k + b]

        def stc(a, n=1):
            return stat[:, a:a + n]

        def v3(t, total_cols, offset, s1, n1, s2, n2):
            return bass.AP(t, offset, [[total_cols, 128], [s1, n1], [s2, n2]])

        def tap(name, buf, res):
            if DBG:
                S.dma("sync", tp[name][:, :], buf, r=[res])

        CONSTS = ["c.idn", "c.idnf", "c.tri8", "c.scanm", "c.bgT", "c.nbkg",
                  "c.ones", "c.onesr", "c.eps"]

        def transpose_tiles(srcs, src_res, dst_view, dst_res, ps_off=0):
            banks = pres(ps_off, 128 * len(srcs))
            for t, s in enumerate(srcs):
                S.mm(PS[:, ps_off + 128 * t: ps_off + 128 * (t + 1)], s, idn[:, :],
                     r=[src_res, "c.idn"], w=pres(ps_off + 128 * t, 128))
            n = len(srcs)
            pv = v3(PS, 4096, ps_off, 128, n, 1, 128)
            S.add("vector", lambda e, d=dst_view, p=pv: e.tensor_copy(d, p),
                  r=banks, w=[dst_res])

        S.dma("sync", idn[:, :], idn_d[:, :], w=["c.idn"])
        S.dma("sync", idnf[:, :], idnf_d[:, :], w=["c.idnf"])
        S.dma("sync", tri8[:, :], tri8_d[:, :], w=["c.tri8"])
        S.dma("sync", scanm[:, :], scanm_d[:, :], w=["c.scanm"])
        S.dma("sync", bgT[:, :], bgT_d[:, :], w=["c.bgT"])
        S.dma("sync", nbkg[:, :], bkg2T_d[:, :], w=["c.nbkg"])
        S.add("vector", lambda e: e.tensor_scalar(
            nbkg[:, :], nbkg[:, :], -1.0, None, OP.mult), r=["c.nbkg"], w=["c.nbkg"])
        S.add("vector", lambda e: e.memset(ones_c[:, :], 1.0), w=["c.ones"])
        S.add("vector", lambda e: e.memset(onesr[:, :], 1.0), w=["c.onesr"])
        S.add("vector", lambda e: e.memset(eps_c[:, :], EPS), w=["c.eps"])
        S.add("vector", lambda e: e.memset(stat[:, :], 0.0),
              w=["st.dls", "st.qls", "st.omu", "st.ossq", "st.ors", "st.s64",
                 "st.xmu", "st.xssq", "st.xrs", "st.sx", "st.cmx", "st.hcm",
                 "st.dc", "st.dpc", "st.thr", "st.nthr", "st.x2mu", "st.x2ssq",
                 "st.x2rs", "st.s8"])

        S.section(10)
        s1 = ExitStack()
        with s1:
            qdT = s1.enter_context(nc.sbuf_tensor("qdT", [128, H * TOK], BF16))
            wsum = s1.enter_context(nc.sbuf_tensor("wsum", [128, 24], F32))
            kd = s1.enter_context(nc.sbuf_tensor("kd", [128, CH * H * 128], BF16))
            AT = s1.enter_context(nc.sbuf_tensor("AT", [128, CH * H * 128], BF16))

            s1a = ExitStack()
            with s1a:
                xbf = [s1a.enter_context(nc.sbuf_tensor(f"xbf{i}", [128, D], BF16))
                       for i in range(2)]
                for c in range(CH):
                    xb = xbf[c % 2]
                    xr = f"xbf{c % 2}"
                    S.dma("gpsimd", xb[:, :], x_d[CT * c:CT * (c + 1), :], w=[xr])
                    transpose_tiles([xb[:, 128 * t:128 * (t + 1)] for t in range(NKD)],
                                    xr, v3(xT, NQ, CT * c, TOK, NKD, 1, CT), "xT",
                                    ps_off=2048)
                tap("xT", xT[:, :], "xT")
                S.emit(nc, sems)

            S.section(20)
            s1b = ExitStack()
            with s1b:
                Q = s1b.enter_context(nc.sbuf_tensor("Q", [128, H * TOK], F32))
                Etmp = s1b.enter_context(nc.sbuf_tensor("Etmp", [128, 1024], F32))
                Escr = [s1b.enter_context(nc.sbuf_tensor(f"Escr{i}", [128, 1024], F32))
                        for i in range(2)]
                Escr2 = [s1b.enter_context(nc.sbuf_tensor(f"Escr2{i}", [128, 1024], F32))
                         for i in range(2)]
                kkt2 = [s1b.enter_context(nc.sbuf_tensor(f"kkt2{i}", [128, 2048], BF16))
                        for i in range(2)]
                kdt2 = [s1b.enter_context(nc.sbuf_tensor(f"kdt2{i}", [128, 2048], BF16))
                        for i in range(2)]
                t1a = s1b.enter_context(nc.sbuf_tensor("t1a", [16, 1024], BF16))
                wkg1s = s1b.enter_context(nc.sbuf_tensor("wkg1s", [128, 256], BF16))
                wkg2a = s1b.enter_context(nc.sbuf_tensor("wkg2a", [16, 1024], BF16))
                nql = s1b.enter_context(nc.sbuf_tensor("nql", [128, 64], F32))

                S.dma("gpsimd", v3(wkg1s, 256, 0, 16, 16, 1, 16),
                      wkg1_d.ap().rearrange("(k p) c -> p k c", p=128), w=["wkg1s"])
                S.dma("gpsimd", wkg2a[0:16, :], wkg2_d[:, :], w=["wkg2a"])
                for k in range(NKD):
                    for hf in range(2):
                        S.mm(PS[0:16, 512 * hf:512 * (hf + 1)],
                             wkg1s[:, 16 * k:16 * (k + 1)],
                             xTs(k, 512 * hf, 512 * (hf + 1)),
                             start=(k == 0), stop=(k == NKD - 1),
                             r=["wkg1s", "xT"], w=pres(512 * hf, 512))
                S.add("vector", lambda e: e.tensor_copy(t1a[0:16, :], PS[0:16, 0:1024]),
                      r=pres(0, 1024), w=["t1a"])
                for h in range(H):
                    reg = 1024 + 1024 * (h % 2)
                    for hf in range(2):
                        S.mm(PS[:, reg + 512 * hf:reg + 512 * (hf + 1)],
                             wkg2a[:, 128 * h:128 * (h + 1)],
                             t1a[:, 512 * hf:512 * (hf + 1)],
                             r=["wkg2a", "t1a"], w=pres(reg + 512 * hf, 512))
                    S.add("scalar", lambda e, h=h, reg=reg: e.activation(
                        Etmp[:, :], PS[:, reg:reg + 1024], AF.Exp, scale=-1.0,
                        bias=nbkg[:, h:h + 1]),
                        r=pres(reg, 1024) + ["c.nbkg"], w=["Etmp"])
                    er = Escr[h % 2]
                    err = f"Escr{h % 2}"
                    S.add("scalar", lambda e, er=er: e.activation(
                        er[:, :], Etmp[:, :], AF.Ln, bias=ones_c[:, :]),
                        r=["Etmp", "c.ones"], w=[err])
                    for c in range(CH):
                        S.add("vector", lambda e, h=h, c=c, er=er: e.tensor_tensor_scan(
                            Q[:, TOK * h + CT * c:TOK * h + CT * (c + 1)],
                            er[:, CT * c:CT * (c + 1)],
                            er[:, CT * c:CT * (c + 1)], 0.0, OP.add, OP.bypass),
                            r=[err], w=["Q"])
                    qlv = v3(Q, H * TOK, TOK * h + CT - 1, 1, 1, CT, CH)
                    S.add("vector", lambda e, h=h, v=qlv: e.reduce_sum(
                        stat[:, QLS + h:QLS + h + 1], v, AX), r=["Q"], w=["st.qls"])
                    S.add("vector", lambda e, h=h, v=qlv: e.tensor_scalar(
                        nql[:, 8 * h:8 * h + 8], v, -KS, None, OP.mult),
                        r=["Q"], w=["nql"])
                    S.add("scalar", lambda e, h=h, v=qlv: e.activation(
                        stat[:, DLS + 8 * h:DLS + 8 * h + 8], v, AF.Exp, scale=-KS),
                        r=["Q"], w=["st.dls"])
                tap("Q", Q[:, :], "Q")

                for pr in range(4):
                    reg = 2048 * (pr % 2)
                    for k in range(NKD):
                        wb = Wb[(pr * NKD + k) % 4]
                        wbr = f"Wb{(pr * NKD + k) % 4}"
                        S.dma("gpsimd", wb[:, 0:256],
                              wq_d[128 * k:128 * (k + 1), 256 * pr:256 * (pr + 1)],
                              w=[wbr])
                        for hl in range(2):
                            for hf in range(2):
                                off = reg + 1024 * hl + 512 * hf
                                S.mm(PS[:, off:off + 512],
                                     wb[:, 128 * hl:128 * (hl + 1)],
                                     xTs(k, 512 * hf, 512 * (hf + 1)),
                                     start=(k == 0), stop=(k == NKD - 1),
                                     r=[wbr, "xT"], w=pres(off, 512))
                    for hl in range(2):
                        h = 2 * pr + hl
                        er = Escr[hl]
                        err = f"Escr{hl}"
                        S.add("scalar", lambda e, h=h, er=er: e.activation(
                            er[:, :], Q[:, TOK * h:TOK * (h + 1)], AF.Exp, scale=-KS),
                            r=["Q"], w=[err])
                        S.add("vector", lambda e, h=h, hl=hl, reg=reg, er=er:
                              e.tensor_mul(qdT[:, TOK * h:TOK * (h + 1)],
                                           PS[:, reg + 1024 * hl:reg + 1024 * (hl + 1)],
                                           er[:, :]),
                              r=pres(reg + 1024 * hl, 1024) + [err], w=["qdT"])
                tap("qdT", qdT[:, :], "qdT")

                for pr in range(4):
                    reg = 2048 * (pr % 2)
                    oreg = 2048 * ((pr + 1) % 2)
                    kk = kkt2[pr % 2]
                    kkr = f"kkt2{pr % 2}"
                    kdt = kdt2[pr % 2]
                    kdtr = f"kdt2{pr % 2}"
                    for k in range(NKD):
                        wb = Wb[(pr * NKD + k) % 4]
                        wbr = f"Wb{(pr * NKD + k) % 4}"
                        S.dma("gpsimd", wb[:, 0:256],
                              wk_d[128 * k:128 * (k + 1), 256 * pr:256 * (pr + 1)],
                              w=[wbr])
                        for hl in range(2):
                            for hf in range(2):
                                off = reg + 1024 * hl + 512 * hf
                                S.mm(PS[:, off:off + 512],
                                     wb[:, 128 * hl:128 * (hl + 1)],
                                     xTs(k, 512 * hf, 512 * (hf + 1)),
                                     start=(k == 0), stop=(k == NKD - 1),
                                     r=[wbr, "xT"], w=pres(off, 512))
                    for hl in range(2):
                        h = 2 * pr + hl
                        er = Escr[hl]
                        err = f"Escr{hl}"
                        er2 = Escr2[hl]
                        err2 = f"Escr2{hl}"
                        S.add("scalar", lambda e, h=h, er=er: e.activation(
                            er[:, :], Q[:, TOK * h:TOK * (h + 1)], AF.Exp, scale=KS),
                            r=["Q"], w=[err])
                        S.add("vector", lambda e, hl=hl, reg=reg, kk=kk, er=er:
                              e.scalar_tensor_tensor(
                                  kk[:, 1024 * hl:1024 * (hl + 1)],
                                  PS[:, reg + 1024 * hl:reg + 1024 * (hl + 1)],
                                  KS, er[:, :], OP.mult, OP.mult),
                              r=pres(reg + 1024 * hl, 1024) + [err], w=[kkr])
                        for c in range(CH):
                            S.add("scalar", lambda e, h=h, c=c, er2=er2: e.activation(
                                er2[:, CT * c:CT * (c + 1)],
                                Q[:, TOK * h + CT * c:TOK * h + CT * (c + 1)],
                                AF.Exp, scale=KS, bias=nql[:, 8 * h + c:8 * h + c + 1]),
                                r=["Q", "nql"], w=[err2])
                        S.add("vector", lambda e, hl=hl, reg=reg, kdt=kdt, er2=er2:
                              e.scalar_tensor_tensor(
                                  kdt[:, 1024 * hl:1024 * (hl + 1)],
                                  PS[:, reg + 1024 * hl:reg + 1024 * (hl + 1)],
                                  KS, er2[:, :], OP.mult, OP.mult),
                              r=pres(reg + 1024 * hl, 1024) + [err2], w=[kdtr])
                    for hl in range(2):
                        h = 2 * pr + hl
                        for c in range(CH):
                            off = oreg + 1024 * hl + 128 * c
                            S.mm(PS[:, off:off + 128],
                                 kk[:, 1024 * hl + 128 * c:1024 * hl + 128 * (c + 1)],
                                 qdT[:, TOK * h + CT * c:TOK * h + CT * (c + 1)],
                                 r=[kkr, "qdT"], w=pres(off, 128))
                        atv = v3(AT, CH * H * 128, 128 * h, 1024, CH, 1, 128)
                        psv = v3(PS, 4096, oreg + 1024 * hl, 128, CH, 1, 128)
                        t8v = v3(tri8, 1024, 0, 128, CH, 1, 128)
                        S.add("vector", lambda e, a=atv, b=psv, t=t8v:
                              e.tensor_mul(a, b, t),
                              r=pres(oreg + 1024 * hl, 1024) + ["c.tri8"], w=["AT"])
                    for hl in range(2):
                        for c in range(CH):
                            off = oreg + 1024 * hl + 128 * c
                            S.mm(PS[:, off:off + 128],
                                 kdt[:, 1024 * hl + 128 * c:1024 * hl + 128 * (c + 1)],
                                 idn[:, :],
                                 r=[kdtr, "c.idn"], w=pres(off, 128))
                    for hl in range(2):
                        h = 2 * pr + hl
                        kdv = v3(kd, CH * H * 128, 128 * h, 1024, CH, 1, 128)
                        psv = v3(PS, 4096, oreg + 1024 * hl, 128, CH, 1, 128)
                        S.add("vector", lambda e, a=kdv, b=psv: e.tensor_copy(a, b),
                              r=pres(oreg + 1024 * hl, 1024), w=["kd"])
                tap("kd", kd[:, :], "kd")
                tap("AT", AT[:, :], "AT")
                S.emit(nc, sems)

            S.section(30)
            if True:
                fstg = [s1.enter_context(nc.sbuf_tensor(f"fstgA{i}", [128, 2048], F32))
                        for i in range(2)]
                fjunk2 = [s1.enter_context(nc.sbuf_tensor(f"fjunkA{i}", [128, 2048],
                                                          I8)) for i in range(2)]
                fj22 = [s1.enter_context(nc.sbuf_tensor("fj2", [128, 2048], I8))] * 2
                fjunk = s1.enter_context(nc.sbuf_tensor("fjunkB", [128, 2048], BF16))
                S.add("vector", lambda e: e.memset(wsum[:, :], 0.0), w=["wsum"])
                tiles = []
                for w_i, wd in enumerate((w1s_d, w2s_d, w3s_d)):
                    R, Cc = wd.shape
                    ti = 0
                    for rr0 in range(0, R, 128):
                        rr = min(128, R - rr0)
                        for c0 in range(0, Cc, 2048):
                            cw = min(2048, Cc - c0)
                            tiles.append((w_i, wd, rr0, rr, c0, cw, ti))
                            ti += 1
                for tn, (w_i, wd, rr0, rr, c0, cw, ti) in enumerate(tiles):
                    fb = fstg[tn % 2]
                    fr = f"fstg{tn % 2}"
                    S.dma("sync", fb[0:rr, 0:cw], wd[rr0:rr0 + rr, c0:c0 + cw], w=[fr])
                    S.add("scalar", lambda e, rr=rr, cw=cw, w_i=w_i, ti=ti, fb=fb:
                          e.activation(fjunk[0:rr, 0:cw], fb[0:rr, 0:cw], AF.Abs,
                                       accum_out=wsum[0:rr, 6 * w_i + ti:6 * w_i + ti + 1]),
                          r=[fr], w=["fjunkB", "wsum"])
                for w_i in range(3):
                    S.add("scalar", lambda e, w_i=w_i:
                          e.activation(fjunk[:, 0:6], wsum[:, 6 * w_i:6 * w_i + 6],
                                       AF.Identity,
                                       accum_out=wsum[:, 18 + w_i:19 + w_i]),
                          r=["wsum"], w=["fjunkB", "wsum"])
            S.section(31)
            if True:
                V = ox1
                vtmp = [s1.enter_context(nc.sbuf_tensor(f"vtmp{i}", [128, 2048], BF16))
                        for i in range(2)]
                L = s1.enter_context(nc.sbuf_tensor("L", [128, 2048], F32))
                Sst = s1.enter_context(nc.sbuf_tensor("Sst", [128, 2048], F32))
                Sb = [s1.enter_context(nc.sbuf_tensor(f"Sb{i}", [128, 2048], BF16))
                      for i in range(2)]
                AGl = [s1.enter_context(nc.sbuf_tensor(f"AGl{i}", [128, 2064], F32))
                       for i in range(2)]

                for vp in range(8):
                    reg = 2048 * (vp % 2)
                    vt = vtmp[vp % 2]
                    vtr = f"vtmp{vp % 2}"
                    for k in range(NKD):
                        wb = Wb[(vp * NKD + k) % 4]
                        wbr = f"Wb{(vp * NKD + k) % 4}"
                        S.dma("gpsimd", wb[:, 0:256],
                              wv_d[128 * k:128 * (k + 1), 256 * vp:256 * (vp + 1)],
                              w=[wbr])
                        for mb in range(2):
                            for hf in range(2):
                                off = reg + 1024 * mb + 512 * hf
                                S.mm(PS[:, off:off + 512],
                                     wb[:, 128 * mb:128 * (mb + 1)],
                                     xTs(k, 512 * hf, 512 * (hf + 1)),
                                     start=(k == 0), stop=(k == NKD - 1),
                                     r=[wbr, "xT"], w=pres(off, 512))
                    S.add("vector", lambda e, reg=reg, vt=vt:
                          e.tensor_copy(vt[:, :], PS[:, reg:reg + 2048]),
                          r=pres(reg, 2048), w=[vtr])
                    for mb in range(2):
                        for c in range(CH):
                            off = reg + 1024 * mb + 128 * c
                            S.mm(PS[:, off:off + 128],
                                 vt[:, 1024 * mb + 128 * c:1024 * mb + 128 * (c + 1)],
                                 idn[:, :],
                                 r=[vtr, "c.idn"], w=pres(off, 128))
                    for mb in range(2):
                        fb = 2 * vp + mb
                        vv = v3(V, CH * D, 128 * fb, D, CH, 1, 128)
                        psv = v3(PS, 4096, reg + 1024 * mb, 128, CH, 1, 128)
                        S.add("vector", lambda e, a=vv, b=psv: e.tensor_copy(a, b),
                              r=pres(reg + 1024 * mb, 1024), w=["ox1"])
                    if vp == 2:
                        S.mm(PS[0:1, 0:3], ones_c[:, :], wsum[:, 18:21],
                             r=["c.ones", "wsum"], w=pres(0, 3))
                        S.add("vector", lambda e: e.tensor_copy(
                            msc[0:1, 0:3], PS[0:1, 0:3]), r=pres(0, 3), w=["msc"])
                        S.add("vector", lambda e: e.memset(msc[0:1, 3:4], 0.0),
                              r=["msc"], w=["msc"])
                        S.dma("sync", ari[:, :], msc[0:1, 0:4], r=["msc"], w=["d.ari"])
                tap("V", V[:, :], "ox1")

                S.section(32)
                S.add("vector", lambda e: e.memset(L[:, :], 0.0), w=["L"])
                for c in range(CH):
                    reg = 2048 * (c % 2)
                    for h in range(H):
                        off = reg + 256 * h
                        S.mm(PS[:, off:off + 256],
                             kd[:, 1024 * c + 128 * h:1024 * c + 128 * (h + 1)],
                             V[:, D * c + DV * h:D * c + DV * (h + 1)],
                             r=["kd", "ox1"], w=pres(off, 256))
                    for h in range(H):
                        off = reg + 256 * h
                        S.add("vector", lambda e, c=c, h=h, off=off:
                              e.scalar_tensor_tensor(
                                  L[:, 256 * h:256 * (h + 1)], L[:, 256 * h:256 * (h + 1)],
                                  stat[:, DLS + 8 * h + c:DLS + 8 * h + c + 1],
                                  PS[:, off:off + 256], OP.mult, OP.add),
                              r=["L", "st.dls"] + pres(off, 256), w=["L"])
                tap("L", L[:, :], "L")
                S.add("gpsimd", lambda e: e.collective_compute(
                    "AllReduce", OP.add, replica_groups=[list(range(NC))],
                    ins=[ari.ap()], outs=[aro.ap()]),
                    cls="cc.ar", r=["d.ari"], w=["d.aro"])
                S.dma("scalar", msc[0:1, 4:8], aro[:, :], r=["d.aro"], w=["msc"])
                S.section(33)
                S.add("scalar", lambda e: e.activation(
                    stc(DCORE, 8), stc(QLS, 8), AF.Exp, scale=-KS),
                    r=["st.qls"], w=["st.dc"])
                S.dma("sync", agi[:, 0:2048], L[:, :], r=["L"], w=["d.agi"])
                S.dma("sync", agi[:, 2048:2056], stc(DCORE, 8), r=["st.dc"], w=["d.agi"])
                S.add("vector", lambda e: e.memset(stc(DCORE + 8, 8), 0.0), w=["st.dc8"])
                S.dma("sync", agi[:, 2056:2064], stc(DCORE + 8, 8), r=["st.dc8"],
                      w=["d.agi"])
                S.add("gpsimd", lambda e: e.collective_compute(
                    "AllGather", OP.bypass, replica_groups=[list(range(NC))],
                    ins=[agi.ap()], outs=[ago.ap()]),
                    cls="cc.st", r=["d.agi"], w=["d.ago"])
                S.add("vector", lambda e: e.tensor_scalar(
                    msc[0:1, 8:11], msc[0:1, 4:7], NWINV, EPS, OP.mult, OP.max),
                    r=["msc"], w=["msc"])
                S.add("vector", lambda e: e.tensor_scalar(
                    msc[0:1, 12:15], msc[0:1, 8:11], 0.5, None, OP.mult),
                    r=["msc"], w=["msc"])
                S.mm(PS[:, 0:3], onesr[:, :], msc[0:1, 12:15],
                     r=["c.onesr", "msc"], w=pres(0, 3))
                S.add("vector", lambda e: e.tensor_copy(stc(THR, 3), PS[:, 0:3]),
                      r=pres(0, 3), w=["st.thr"])
                S.add("vector", lambda e: e.tensor_scalar(
                    stc(NTHR, 3), stc(THR, 3), -1.0, None, OP.mult),
                    r=["st.thr"], w=["st.nthr"])

                tern_tiles = []
                for w_i, wd in ((2, w3s_d), (0, w1s_d), (1, w2s_d)):
                    R, Cc = wd.shape
                    ti = 0
                    for rr0 in range(0, R, 128):
                        rr = min(128, R - rr0)
                        for c0 in range(0, Cc, 2048):
                            cw = min(2048, Cc - c0)
                            tern_tiles.append((w_i, wd, rr0, rr, c0, cw, ti))
                            ti += 1
                NTT = len(tern_tiles)
                NT1 = sum(1 for t in tern_tiles if t[0] != 1)

                def tern_load(tn):
                    if tn >= NTT:
                        return
                    w_i, wd, rr0, rr, c0, cw, ti = tern_tiles[tn]
                    fb = fstg[tn % 2]
                    S.dma("sync", fb[0:rr, 0:cw], wd[rr0:rr0 + rr, c0:c0 + cw],
                          w=[f"fstg{tn % 2}"])

                def tern_ops(tn):
                    w_i, wd, rr0, rr, c0, cw, ti = tern_tiles[tn]
                    td = (t1i, t2i, t3i)[w_i]
                    fb = fstg[tn % 2]
                    fr = f"fstg{tn % 2}"
                    fjb = fjunk2[tn % 2]
                    fjr = f"fjunk{tn % 2}"
                    fj2b = fj22[tn % 2]
                    fj2r = f"fjj{tn % 2}"
                    S.add("vector", lambda e, rr=rr, cw=cw, w_i=w_i, fb=fb, fjb=fjb:
                          e.tensor_scalar(fjb[0:rr, 0:cw], fb[0:rr, 0:cw],
                                          stat[0:rr, THR + w_i:THR + w_i + 1],
                                          None, OP.is_ge),
                          r=[fr, "st.thr"], w=[fjr])
                    S.add("vector", lambda e, rr=rr, cw=cw, w_i=w_i, fb=fb, fj2b=fj2b:
                          e.tensor_scalar(fj2b[0:rr, 0:cw], fb[0:rr, 0:cw],
                                          stat[0:rr, NTHR + w_i:NTHR + w_i + 1],
                                          None, OP.is_le),
                          r=[fr, "st.nthr"], w=[fj2r])
                    S.add("vector", lambda e, rr=rr, cw=cw, fjb=fjb, fj2b=fj2b:
                          e.tensor_sub(fjb[0:rr, 0:cw], fjb[0:rr, 0:cw],
                                       fj2b[0:rr, 0:cw]),
                          r=[fjr, fj2r], w=[fjr])
                    S.dma("sync", td[rr0:rr0 + rr, c0:c0 + cw], fjb[0:rr, 0:cw],
                          r=[fjr], w=[f"d.t{w_i}i"])
                    tern_load(tn + 2)

                tern_load(0)
                tern_load(1)

                S.section(34)
                S.add("vector", lambda e: e.memset(Sst[:, :], 0.0),
                      r=["Sst"], w=["Sst"])
                S.add("scalar", lambda e: e.copy(Sb[0][:, :], Sst[:, :]),
                      r=["Sst"], w=["Sb0"])

                def cmb(j):
                    if j == 0:
                        S.add("vector", lambda e: e.memset(L[:, :], 0.0),
                              r=["L"], w=["L"])
                    ag = AGl[j % 2]
                    agr = f"AGl{j % 2}"
                    S.dma("scalar", ag[:, :], ago[128 * j:128 * (j + 1), :],
                          r=["d.ago"], w=[agr])
                    S.add("vector", lambda e, j=j, ag=ag: e.tensor_scalar(
                        stc(DPC, 8), ag[:, 2048:2056],
                        scanm[:, j:j + 1], scanm[:, 8 + j:9 + j], OP.mult, OP.add),
                        r=[agr, "c.scanm"], w=["st.dpc"])
                    for h in range(H):
                        S.add("vector", lambda e, h=h: e.tensor_scalar(
                            L[:, 256 * h:256 * (h + 1)], L[:, 256 * h:256 * (h + 1)],
                            stat[:, DPC + h:DPC + h + 1], None, OP.mult),
                            r=["L", "st.dpc"], w=["L"])
                        S.add("vector", lambda e, j=j, h=h, ag=ag:
                              e.scalar_tensor_tensor(
                                  L[:, 256 * h:256 * (h + 1)],
                                  ag[:, 256 * h:256 * (h + 1)],
                                  scanm[:, j:j + 1], L[:, 256 * h:256 * (h + 1)],
                                  OP.mult, OP.add),
                              r=[agr, "c.scanm", "L"], w=["L"])

                for c in range(CH):
                    reg = 0
                    greg = 2048
                    sb = Sb[c % 2]
                    sbr = f"Sb{c % 2}"
                    sbn = Sb[(c + 1) % 2]
                    sbnr = f"Sb{(c + 1) % 2}"
                    for h in range(H):
                        off = reg + 256 * h
                        S.mm(PS[:, off:off + 256],
                             AT[:, 1024 * c + 128 * h:1024 * c + 128 * (h + 1)],
                             V[:, D * c + DV * h:D * c + DV * (h + 1)],
                             start=True, stop=False,
                             r=["AT", "ox1"], w=pres(off, 256))
                        S.mm(PS[:, off:off + 256],
                             qdT[:, TOK * h + CT * c:TOK * h + CT * (c + 1)],
                             sb[:, 256 * h:256 * (h + 1)],
                             start=False, stop=True,
                             r=["qdT", sbr], w=pres(off, 256))
                    for h in range(H):
                        off = greg + 256 * h
                        S.mm(PS[:, off:off + 256],
                             kd[:, 1024 * c + 128 * h:1024 * c + 128 * (h + 1)],
                             V[:, D * c + DV * h:D * c + DV * (h + 1)],
                             r=["kd", "ox1"], w=pres(off, 256))
                    for h in range(H):
                        off = greg + 256 * h
                        S.add("vector", lambda e, c=c, h=h, off=off:
                              e.scalar_tensor_tensor(
                                  Sst[:, 256 * h:256 * (h + 1)],
                                  Sst[:, 256 * h:256 * (h + 1)],
                                  stat[:, DLS + 8 * h + c:DLS + 8 * h + c + 1],
                                  PS[:, off:off + 256], OP.mult, OP.add),
                              r=["Sst", "st.dls"] + pres(off, 256), w=["Sst"])
                    if c < CH - 1:
                        S.add("scalar", lambda e, sbn=sbn: e.copy(sbn[:, :], Sst[:, :]),
                              r=["Sst"], w=[sbnr])
                    S.add("scalar", lambda e, c=c: e.copy(
                        ox1[:, D * c:D * (c + 1)], PS[:, 0:2048]),
                        r=pres(0, 2048), w=["ox1"])
                    if c >= 3:
                        cmb(c - 3)
                    if c == CH - 1:
                        cmb(5)
                        cmb(6)
                    for tn in range(-(-NT1 * c // CH), -(-NT1 * (c + 1) // CH)):
                        tern_ops(tn)

                tap("Sst", L[:, :], "L")

                for c in range(CH):
                    S.add("scalar", lambda e: e.copy(Sb[0][:, :], L[:, :]),
                          r=["L"], w=["Sb0"])
                    for h in range(H):
                        off = 256 * h
                        S.mm(PS[:, off:off + 256],
                             qdT[:, TOK * h + CT * c:TOK * h + CT * (c + 1)],
                             Sb[0][:, 256 * h:256 * (h + 1)],
                             r=["qdT", "Sb0"], w=pres(off, 256))
                    S.add("vector", lambda e, c=c: e.tensor_add(
                        ox1[:, D * c:D * (c + 1)], ox1[:, D * c:D * (c + 1)],
                        PS[:, 0:2048]),
                        r=["ox1"] + pres(0, 2048), w=["ox1"])
                    if c < CH - 1:
                        for h in range(H):
                            S.add("vector", lambda e, c=c, h=h: e.tensor_scalar(
                                L[:, 256 * h:256 * (h + 1)],
                                L[:, 256 * h:256 * (h + 1)],
                                stat[:, DLS + 8 * h + c:DLS + 8 * h + c + 1],
                                None, OP.mult),
                                r=["L", "st.dls"], w=["L"])
                    ov3 = v3(ox1, CH * D, D * c, 256, 8, 1, 256)
                    S.add("vector", lambda e, c=c, o=ov3: e.reduce_sum(
                        stat[:, OMU + 8 * c:OMU + 8 * c + 8], o, AX),
                        r=["ox1"], w=["st.omu"])
                    for h in range(H):
                        S.add("scalar", lambda e, c=c, h=h: e.activation(
                            vtmp[0][:, 0:256],
                            ox1[:, D * c + DV * h:D * c + DV * (h + 1)], AF.Square,
                            accum_out=stat[:, OSSQ + 8 * c + h:OSSQ + 8 * c + h + 1]),
                            r=["ox1"], w=["vtmp0", "st.ossq"])
                tap("orw", ox1[:, :], "ox1")

            S.section(35)
            S.add("vector", lambda e: e.tensor_scalar(
                stc(OMU, 64), stc(OMU, 64), 1.0 / DV, None, OP.mult),
                r=["st.omu"], w=["st.omu"])
            S.add("vector", lambda e: e.tensor_scalar(
                stc(OSSQ, 64), stc(OSSQ, 64), 1.0 / DV, None, OP.mult),
                r=["st.ossq"], w=["st.ossq"])
            S.add("vector", lambda e: e.tensor_mul(
                stc(SCR64, 64), stc(OMU, 64), stc(OMU, 64)),
                r=["st.omu"], w=["st.s64"])
            S.add("vector", lambda e: e.tensor_sub(
                stc(OSSQ, 64), stc(OSSQ, 64), stc(SCR64, 64)),
                r=["st.ossq", "st.s64"], w=["st.ossq"])
            S.add("scalar", lambda e: e.activation(
                stc(SCR64, 64), stc(OSSQ, 64), AF.Sqrt, bias=eps_c[:, :]),
                r=["st.ossq", "c.eps"], w=["st.s64"])
            S.add("vector", lambda e: e.reciprocal(stc(ORS, 64), stc(SCR64, 64)),
                  r=["st.s64"], w=["st.ors"])
            for c in range(CH):
                for h in range(H):
                    S.add("vector", lambda e, c=c, h=h: e.tensor_scalar(
                        ox1[:, D * c + DV * h:D * c + DV * (h + 1)],
                        ox1[:, D * c + DV * h:D * c + DV * (h + 1)],
                        stat[:, OMU + 8 * c + h:OMU + 8 * c + h + 1],
                        stat[:, ORS + 8 * c + h:ORS + 8 * c + h + 1],
                        OP.subtract, OP.mult),
                        r=["ox1", "st.omu", "st.ors"], w=["ox1"])
            for tn in range(NT1, NTT):
                tern_ops(tn)

            S.section(40)
            if True:
                S.add("gpsimd", lambda e: e.collective_compute(
                    "AllGather", OP.bypass, replica_groups=[list(range(NC))],
                    ins=[t3i.ap()], outs=[t3o.ap()]),
                    cls="cc.t2", r=["d.t2i"], w=["d.t2o"])

                S.emit(nc, sems)

        S.section(50)
        s3 = ExitStack()
        with s3:
            ooT = s3.enter_context(nc.sbuf_tensor("ooT", [128, NKD * TOK], BF16))
            gt2 = [s3.enter_context(nc.sbuf_tensor(f"gt2{i}", [128, 2048], BF16))
                   for i in range(2)]
            att2 = [s3.enter_context(nc.sbuf_tensor(f"att2{i}", [128, 2048], BF16))
                    for i in range(2)]
            xs = [s3.enter_context(nc.sbuf_tensor(f"xs{i}", [128, 1024], F32))
                  for i in range(2)]
            Wf = [s3.enter_context(nc.sbuf_tensor(f"Wf{i}", [128, 256], F32))
                  for i in range(2)]

            def wload(wd_src, k, pr, wb, wbr):
                src = wd_src[128 * k:128 * (k + 1), 256 * pr:256 * (pr + 1)]
                if k % 2 == 0:
                    S.dma("gpsimd", wb[:, 0:256], src, w=[wbr])
                else:
                    wf = Wf[(k // 2) % 2]
                    wfr = f"Wf{(k // 2) % 2}"
                    S.dma("sync", wf[:, :], src, w=[wfr])
                    S.add("scalar", lambda e, wf=wf, wb=wb: e.copy(
                        wb[:, 0:256], wf[:, :]), r=[wfr], w=[wbr])

            ox1v = ox1.ap().rearrange("p (c f) -> p c f", c=CH)
            S.add("gpsimd", lambda e: e.collective_compute(
                "AllGather", OP.bypass, replica_groups=[list(range(NC))],
                ins=[t1i.ap()], outs=[t1o.ap()]),
                cls="cc.t0", r=["d.t0i"], w=["d.t0o"])
            tap("oo_pre", ox1[:, :], "ox1") if False else None
            for pg in range(8):
                reg = 2048 * (pg % 2)
                gt = gt2[pg % 2]
                gtr = f"gt2{pg % 2}"
                for k in range(NKD):
                    wb = Wb[(pg * NKD + k) % 4]
                    wbr = f"Wb{(pg * NKD + k) % 4}"
                    wload(wg_d, k, pg, wb, wbr)
                    for mb in range(2):
                        for hf in range(2):
                            off = reg + 1024 * mb + 512 * hf
                            S.mm(PS[:, off:off + 512],
                                 wb[:, 128 * mb:128 * (mb + 1)],
                                 xTs(k, 512 * hf, 512 * (hf + 1)),
                                 start=(k == 0), stop=(k == NKD - 1),
                                 r=[wbr, "xT"], w=pres(off, 512))
                for mb in range(2):
                    fb = 2 * pg + mb
                    off = reg + 1024 * mb
                    S.add("scalar", lambda e, fb=fb, off=off, gt=gt, mb=mb:
                          e.activation(gt[:, 1024 * mb:1024 * (mb + 1)],
                                       PS[:, off:off + 1024],
                                       AF.Silu, bias=bgT[:, fb:fb + 1]),
                          r=pres(off, 1024) + ["c.bgT"], w=[gtr])
                for mb in range(2):
                    for c in range(CH):
                        off = reg + 1024 * mb + 128 * c
                        S.mm(PS[:, off:off + 128],
                             gt[:, 1024 * mb + 128 * c:1024 * mb + 128 * (c + 1)],
                             idn[:, :], r=[gtr, "c.idn"], w=pres(off, 128))
                for mb in range(2):
                    fb = 2 * pg + mb
                    ov = v3(ox1, CH * D, 128 * fb, D, CH, 1, 128)
                    psv = v3(PS, 4096, reg + 1024 * mb, 128, CH, 1, 128)
                    S.add("vector", lambda e, a=ov, b=psv: e.tensor_mul(a, b, a),
                          r=pres(reg + 1024 * mb, 1024) + ["ox1"], w=["ox1"])
            tap("oo", ox1[:, :], "ox1")

            for kf in range(NKD):
                reg = 2048 * (kf % 2)
                for c in range(CH):
                    off = reg + 128 * c
                    S.mm(PS[:, off:off + 128],
                         ox1[:, D * c + 128 * kf:D * c + 128 * (kf + 1)], idn[:, :],
                         r=["ox1", "c.idn"], w=pres(off, 128))
                S.add("vector", lambda e, kf=kf, reg=reg: e.tensor_copy(
                    ooT[:, TOK * kf:TOK * (kf + 1)], PS[:, reg:reg + 1024]),
                    r=pres(reg, 1024), w=["ooT"])

            for po in range(8):
                reg = 2048 * (po % 2)
                at = att2[po % 2]
                atr = f"att2{po % 2}"
                for k in range(NKD):
                    wb = Wb[(po * NKD + k) % 4]
                    wbr = f"Wb{(po * NKD + k) % 4}"
                    wload(wo_d, k, po, wb, wbr)
                    for mb in range(2):
                        for hf in range(2):
                            off = reg + 1024 * mb + 512 * hf
                            S.mm(PS[:, off:off + 512],
                                 wb[:, 128 * mb:128 * (mb + 1)],
                                 ooT[:, TOK * k + 512 * hf:TOK * k + 512 * (hf + 1)],
                                 start=(k == 0), stop=(k == NKD - 1),
                                 r=[wbr, "ooT"], w=pres(off, 512))
                S.add("vector", lambda e, reg=reg, at=at:
                      e.tensor_copy(at[:, :], PS[:, reg:reg + 2048]),
                      r=pres(reg, 2048), w=[atr])
                for mb in range(2):
                    for c in range(CH):
                        off = reg + 1024 * mb + 128 * c
                        S.mm(PS[:, off:off + 128],
                             at[:, 1024 * mb + 128 * c:1024 * mb + 128 * (c + 1)],
                             idn[:, :], r=[atr, "c.idn"], w=pres(off, 128))
                for mb in range(2):
                    ob = 2 * po + mb
                    xsb = xs[mb]
                    xsr = f"xs{mb}"
                    xin = bass.AP(x_d, 128 * ob, [[D, 128], [128 * D, CH], [1, 128]])
                    xsv = v3(xsb, 1024, 0, 128, CH, 1, 128)
                    S.dma("sync", xsv, xin, w=[xsr])
                    ov = v3(ox1, CH * D, 128 * ob, D, CH, 1, 128)
                    psv = v3(PS, 4096, reg + 1024 * mb, 128, CH, 1, 128)
                    S.add("vector", lambda e, o=ov, p=psv, xv=xsv:
                          e.scalar_tensor_tensor(o, p, 1.0, xv, OP.mult, OP.add),
                          r=pres(reg + 1024 * mb, 1024) + [xsr, "ox1"], w=["ox1"])
            S.add("gpsimd", lambda e: e.collective_compute(
                "AllGather", OP.bypass, replica_groups=[list(range(NC))],
                ins=[t2i.ap()], outs=[t2o.ap()]),
                cls="cc.t1", r=["d.t1i"], w=["d.t1o"])

            for c in range(CH):
                S.add("vector", lambda e, c=c: e.reduce_sum(
                    stat[:, XMU + c:XMU + c + 1], ox1[:, D * c:D * (c + 1)], AX),
                    r=["ox1"], w=["st.xmu"])
                S.add("scalar", lambda e, c=c: e.activation(
                    att2[0][:, :], ox1[:, D * c:D * (c + 1)], AF.Square,
                    accum_out=stat[:, XSSQ + c:XSSQ + c + 1]),
                    r=["ox1"], w=["att20", "st.xssq"])
            S.add("vector", lambda e: e.tensor_scalar(
                stc(XMU, 8), stc(XMU, 8), 1.0 / D, None, OP.mult),
                r=["st.xmu"], w=["st.xmu"])
            S.add("vector", lambda e: e.tensor_scalar(
                stc(XSSQ, 8), stc(XSSQ, 8), 1.0 / D, None, OP.mult),
                r=["st.xssq"], w=["st.xssq"])
            S.add("vector", lambda e: e.tensor_mul(
                stc(SCR8, 8), stc(XMU, 8), stc(XMU, 8)), r=["st.xmu"], w=["st.s8"])
            S.add("vector", lambda e: e.tensor_sub(
                stc(XSSQ, 8), stc(XSSQ, 8), stc(SCR8, 8)),
                r=["st.xssq", "st.s8"], w=["st.xssq"])
            S.add("scalar", lambda e: e.activation(
                stc(SCR8, 8), stc(XSSQ, 8), AF.Sqrt, bias=eps_c[:, :]),
                r=["st.xssq", "c.eps"], w=["st.s8"])
            S.add("vector", lambda e: e.reciprocal(stc(XRS, 8), stc(SCR8, 8)),
                  r=["st.s8"], w=["st.xrs"])
            for c in range(CH):
                S.add("vector", lambda e, c=c: e.tensor_scalar(
                    ox1[:, D * c:D * (c + 1)], ox1[:, D * c:D * (c + 1)],
                    stat[:, XMU + c:XMU + c + 1], stat[:, XRS + c:XRS + c + 1],
                    OP.subtract, OP.mult),
                    r=["ox1", "st.xmu", "st.xrs"], w=["ox1"])
            tap("x1", ox1[:, :], "ox1")
            S.emit(nc, sems)

        S.section(60)
        s4 = ExitStack()
        with s4:
            rscr = s4.enter_context(nc.sbuf_tensor("rscr", [128, 2048], F32))
            f1scr = s4.enter_context(nc.sbuf_tensor("f1scr", [128, 1024], F32))
            pfscr = s4.enter_context(nc.sbuf_tensor("pfscr", [128, 1024], F32))
            ri = s4.enter_context(nc.sbuf_tensor("ri", [128, 1024], I32))
            zbf = s4.enter_context(nc.sbuf_tensor("zbf", [128, 2048], BF16))
            B1 = s4.enter_context(nc.sbuf_tensor("B1", [128, 1024], F32))
            B3 = s4.enter_context(nc.sbuf_tensor("B3", [128, 1024], F32))
            Bh = s4.enter_context(nc.sbuf_tensor("Bh", [128, 1024], F32))
            Bo = s4.enter_context(nc.sbuf_tensor("Bo", [128, 1024], F32))
            hbb = [s4.enter_context(nc.sbuf_tensor(f"hbb{i}", [128, 1024], BF16))
                   for i in range(2)]
            zhn = [s4.enter_context(nc.sbuf_tensor(f"zhn{i}", [128, 1024], BF16))
                   for i in range(2)]

            def round_neg(src_off, dst_bf16, w, dst_res):
                src = rscr[:, src_off:src_off + w]
                S.add("vector", lambda e, s_=src, w=w: e.tensor_copy(ri[:, 0:w], s_),
                      r=["rscr"], w=["ri"])
                S.add("vector", lambda e, w=w: e.tensor_copy(f1scr[:, 0:w], ri[:, 0:w]),
                      r=["ri"], w=["f1scr"])
                S.add("vector", lambda e, s_=src, w=w: e.tensor_sub(
                    pfscr[:, 0:w], s_, f1scr[:, 0:w]),
                    r=["rscr", "f1scr"], w=["pfscr"])
                S.add("vector", lambda e, w=w: e.scalar_tensor_tensor(
                    f1scr[:, 0:w], pfscr[:, 0:w], 0.5, f1scr[:, 0:w], OP.is_ge, OP.add),
                    r=["pfscr", "f1scr"], w=["f1scr"])
                S.add("vector", lambda e, w=w, d=dst_bf16: e.scalar_tensor_tensor(
                    d, pfscr[:, 0:w], -0.5, f1scr[:, 0:w], OP.is_le, OP.subtract),
                    r=["pfscr", "f1scr"], w=[dst_res])

            for c in range(CH):
                S.add("vector", lambda e, c=c: e.reduce_max(
                    stat[:, SX + c:SX + c + 1], ox1[:, D * c:D * (c + 1)], AX,
                    apply_absolute_value=True), r=["ox1"], w=["st.sx"])
            S.add("vector", lambda e: e.tensor_scalar(
                stc(CMX, 8), stc(SX, 8), EPS, None, OP.max),
                r=["st.sx"], w=["st.cmx"])
            S.add("vector", lambda e: e.reciprocal(stc(SX, 8), stc(CMX, 8)),
                  r=["st.cmx"], w=["st.sx"])
            S.add("vector", lambda e: e.tensor_scalar(
                stc(SX, 8), stc(SX, 8), 127.0, None, OP.mult),
                r=["st.sx"], w=["st.sx"])
            for c in range(CH):
                S.add("vector", lambda e, c=c: e.tensor_scalar(
                    rscr[:, :], ox1[:, D * c:D * (c + 1)],
                    stat[:, SX + c:SX + c + 1], None, OP.mult),
                    r=["ox1", "st.sx", "rscr"], w=["rscr"])
                round_neg(0, zbf[:, 0:1024], 1024, "zbf")
                round_neg(1024, zbf[:, 1024:2048], 1024, "zbf")
                transpose_tiles([zbf[:, 128 * t:128 * (t + 1)] for t in range(NKD)],
                                "zbf", v3(xT, NQ, CT * c, TOK, NKD, 1, CT), "xT",
                                ps_off=2048 * (c % 2))
            tap("zxT", xT[:, :], "xT")

            S.add("tensor", lambda e: e.transpose(PS[0:8, 0:128], stc(CMX, 8),
                                                  idnf[:, :]),
                  r=["st.cmx", "c.idnf"], w=pres(0, 128))
            S.add("vector", lambda e: e.tensor_copy(rtmp[:, :], PS[0:8, 0:128]),
                  r=pres(0, 128), w=["rtmp"])
            S.dma("sync", rowb[:, :], rtmp[0:8, 0:128], r=["rtmp"], w=["d.rowb"])
            S.dma("sync", row1[0:1, :], rowb.ap().rearrange("a b -> (a b)").rearrange("(o n) -> o n", o=1),
                  r=["d.rowb"], w=["row1"])
            for (Bt, Btr, mi) in ((B1, "B1", 8), (B3, "B3", 10)):
                S.add("vector", lambda e, mi=mi: e.tensor_scalar(
                    row2[0:1, :], row1[0:1, :], msc[0:1, mi:mi + 1],
                    -1.0 / 127.0, OP.mult, OP.mult),
                    r=["row1", "msc", "row2"], w=["row2"])
                for hf in range(2):
                    S.mm(PS[:, 512 * hf:512 * (hf + 1)], onesr[:, :],
                         row2[0:1, 512 * hf:512 * (hf + 1)],
                         r=["c.onesr", "row2"], w=pres(512 * hf, 512))
                S.add("vector", lambda e, b=Bt: e.tensor_copy(b[:, :], PS[:, 0:1024]),
                      r=pres(0, 1024), w=[Btr])

            s4a = ExitStack()
            with s4a:
                Mx = s4a.enter_context(nc.sbuf_tensor("Mx", [128, 1024], F32))
                Mn = s4a.enter_context(nc.sbuf_tensor("Mn", [128, 1024], F32))
                T1w = [s4a.enter_context(
                    nc.sbuf_tensor(f"T1w{i}", [128, NKD * WIN * 128], BF16))
                    for i in range(2)]
                T3w = [s4a.enter_context(
                    nc.sbuf_tensor(f"T3w{i}", [128, NKD * WIN * 128], BF16))
                    for i in range(2)]
                L1b = [s4a.enter_context(nc.sbuf_tensor(f"L1b{i}", [128, 512], I8))
                       for i in range(2)]
                L3b = [s4a.enter_context(nc.sbuf_tensor(f"L3b{i}", [128, 512], I8))
                       for i in range(2)]
                S.add("vector", lambda e: e.memset(Mx[:, :], 0.0), w=["Mx"])
                S.add("vector", lambda e: e.memset(Mn[:, :], 0.0), w=["Mn"])
                wins = []
                hb0 = 0
                while hb0 < NHB:
                    wins.append((hb0, min(WIN, NHB - hb0)))
                    hb0 += WIN
                WW = WIN * 128

                def win_loads(wi):
                    if wi >= len(wins):
                        return
                    h0, nhb = wins[wi]
                    t1 = T1w[wi % 2]
                    t3 = T3w[wi % 2]
                    for k in range(NKD):
                        l1 = L1b[k % 2]
                        l3 = L3b[k % 2]
                        S.dma("sync", l1[:, 0:128 * nhb],
                              t1o[128 * k:128 * (k + 1), 128 * h0:128 * (h0 + nhb)],
                              r=["d.t0o"], w=[f"L1b{k % 2}"])
                        S.add("vector", lambda e, t1=t1, l1=l1, k=k, nhb=nhb:
                              e.tensor_copy(t1[:, WW * k:WW * k + 128 * nhb],
                                            l1[:, 0:128 * nhb]),
                              r=[f"L1b{k % 2}"], w=[f"T1w{wi % 2}"])
                        S.dma("scalar", l3[:, 0:128 * nhb],
                              t3o[128 * k:128 * (k + 1), 128 * h0:128 * (h0 + nhb)],
                              r=["d.t2o"], w=[f"L3b{k % 2}"])
                        S.add("vector", lambda e, t3=t3, l3=l3, k=k, nhb=nhb:
                              e.tensor_copy(t3[:, WW * k:WW * k + 128 * nhb],
                                            l3[:, 0:128 * nhb]),
                              r=[f"L3b{k % 2}"], w=[f"T3w{wi % 2}"])

                win_loads(0)
                win_loads(1)
                for wi, (h0, nhb) in enumerate(wins):
                    t1 = T1w[wi % 2]
                    t1r = f"T1w{wi % 2}"
                    t3 = T3w[wi % 2]
                    t3r = f"T3w{wi % 2}"
                    for j in range(nhb):
                        hb = h0 + j
                        reg = 2048 * (hb % 2)
                        hbbb = hbb[hb % 2]
                        hbr = f"hbb{hb % 2}"
                        for k in range(NKD):
                            for hf in range(2):
                                S.mm(PS[:, reg + 512 * hf:reg + 512 * (hf + 1)],
                                     t1[:, WW * k + 128 * j:WW * k + 128 * (j + 1)],
                                     xTs(k, 512 * hf, 512 * (hf + 1)),
                                     start=(k == 0), stop=(k == NKD - 1),
                                     r=[t1r, "xT"], w=pres(reg + 512 * hf, 512))
                                S.mm(PS[:, reg + 1024 + 512 * hf:reg + 1024 + 512 * (hf + 1)],
                                     t3[:, WW * k + 128 * j:WW * k + 128 * (j + 1)],
                                     xTs(k, 512 * hf, 512 * (hf + 1)),
                                     start=(k == 0), stop=(k == NKD - 1),
                                     r=[t3r, "xT"], w=pres(reg + 1024 + 512 * hf, 512))
                        S.add("vector", lambda e, reg=reg: e.tensor_mul(
                            rscr[:, 0:1024], PS[:, reg:reg + 1024], B1[:, :]),
                            r=pres(reg, 1024) + ["B1", "rscr"], w=["rscr"])
                        S.add("scalar", lambda e: e.activation(
                            f1scr[:, 0:1024], rscr[:, 0:1024], AF.Silu),
                            r=["rscr"], w=["f1scr"])
                        S.add("vector", lambda e: e.tensor_mul(
                            f1scr[:, 0:1024], f1scr[:, 0:1024], B3[:, :]),
                            r=["f1scr", "B3"], w=["f1scr"])
                        S.add("vector", lambda e, reg=reg, hbbb=hbbb: e.tensor_mul(
                            hbbb[:, :], f1scr[:, 0:1024], PS[:, reg + 1024:reg + 2048]),
                            r=["f1scr"] + pres(reg + 1024, 1024), w=[hbr])
                        S.add("vector", lambda e, hbbb=hbbb: e.tensor_tensor(
                            Mx[:, :], Mx[:, :], hbbb[:, :], OP.max),
                            r=["Mx", hbr], w=["Mx"])
                        S.add("vector", lambda e, hbbb=hbbb: e.tensor_tensor(
                            Mn[:, :], Mn[:, :], hbbb[:, :], OP.min),
                            r=["Mn", hbr], w=["Mn"])
                        S.dma("gpsimd", h_dr[128 * hb:128 * (hb + 1), :], hbbb[:, :],
                              r=[hbr], w=["d.h"])
                    win_loads(wi + 2)

                S.add("vector", lambda e: e.tensor_scalar(
                    Mn[:, :], Mn[:, :], -1.0, None, OP.mult), r=["Mn"], w=["Mn"])
                S.add("vector", lambda e: e.tensor_max(Mx[:, :], Mx[:, :], Mn[:, :]),
                      r=["Mx", "Mn"], w=["Mx"])
                S.add("vector", lambda e: e.tensor_copy(zbf[:, 0:1024], Mx[:, :]),
                      r=["Mx", "zbf"], w=["zbf"])
                for c in range(CH):
                    S.mm(PS[:, 128 * c:128 * (c + 1)], zbf[:, 128 * c:128 * (c + 1)],
                         idn[:, :], r=["zbf", "c.idn"], w=pres(128 * c, 128))
                S.add("vector", lambda e: e.tensor_copy(rscr[:, 0:1024], PS[:, 0:1024]),
                      r=pres(0, 1024) + ["rscr"], w=["rscr"])
                for c in range(CH):
                    S.add("vector", lambda e, c=c: e.reduce_max(
                        stat[:, HCM + c:HCM + c + 1], rscr[:, 128 * c:128 * (c + 1)], AX),
                        r=["rscr"], w=["st.hcm"])
                S.add("vector", lambda e: e.tensor_scalar(
                    stc(HCM, 8), stc(HCM, 8), EPS, None, OP.max),
                    r=["st.hcm"], w=["st.hcm"])
                S.add("tensor", lambda e: e.transpose(PS[0:8, 0:128], stc(HCM, 8),
                                                      idnf[:, :]),
                      r=["st.hcm", "c.idnf"], w=pres(0, 128))
                S.add("vector", lambda e: e.tensor_copy(rtmp[:, :], PS[0:8, 0:128]),
                      r=pres(0, 128), w=["rtmp"])
                S.dma("sync", rowb[:, :], rtmp[0:8, 0:128], r=["rtmp"], w=["d.rowb"])
                S.dma("sync", row1[0:1, :], rowb.ap().rearrange("a b -> (a b)").rearrange("(o n) -> o n", o=1),
                      r=["d.rowb"], w=["row1"])
                S.add("vector", lambda e: e.reciprocal(row2[0:1, :], row1[0:1, :]),
                      r=["row1", "row2"], w=["row2"])
                S.add("vector", lambda e: e.tensor_scalar(
                    row2[0:1, :], row2[0:1, :], 127.0, None, OP.mult),
                    r=["row2"], w=["row2"])
                for hf in range(2):
                    S.mm(PS[:, 512 * hf:512 * (hf + 1)], onesr[:, :],
                         row2[0:1, 512 * hf:512 * (hf + 1)],
                         r=["c.onesr", "row2"], w=pres(512 * hf, 512))
                S.add("vector", lambda e: e.tensor_copy(Bh[:, :], PS[:, 0:1024]),
                      r=pres(0, 1024), w=["Bh"])
                S.add("vector", lambda e: e.tensor_scalar(
                    row2[0:1, :], row1[0:1, :], msc[0:1, 9:10],
                    -1.0 / 127.0, OP.mult, OP.mult),
                    r=["row1", "msc", "row2"], w=["row2"])
                for hf in range(2):
                    S.mm(PS[:, 2048 + 512 * hf:2048 + 512 * (hf + 1)], onesr[:, :],
                         row2[0:1, 512 * hf:512 * (hf + 1)],
                         r=["c.onesr", "row2"], w=pres(2048 + 512 * hf, 512))
                S.add("vector", lambda e: e.tensor_copy(Bo[:, :], PS[:, 2048:3072]),
                      r=pres(2048, 1024), w=["Bo"])
                S.emit(nc, sems)

            s4b = ExitStack()
            with s4b:
                T2w = [s4b.enter_context(nc.sbuf_tensor(f"T2w{i}", [128, 512], BF16))
                       for i in range(2)]
                L2b = [s4b.enter_context(nc.sbuf_tensor(f"L2b{i}", [128, 512], I8))
                       for i in range(2)]
                x2p = s4b.enter_context(nc.sbuf_tensor("x2p", [128, CH * D], BF16))
                ft = [s4b.enter_context(nc.sbuf_tensor(f"ft{i}", [128, 4096], BF16))
                      for i in range(2)]
                outst2 = [s4b.enter_context(nc.sbuf_tensor(f"outst{i}", [128, 2048],
                                                           F32)) for i in range(2)]

                for og in range(4):
                    ftb = ft[og % 2]
                    ftr = f"ft{og % 2}"
                    for k in range(NHB):
                        hbbb = hbb[k % 2]
                        hbr = f"hbb{k % 2}"
                        zb = zhn[k % 2]
                        zbr = f"zhn{k % 2}"
                        if og == 0:
                            S.dma("gpsimd", hbbb[:, :], h_dr[128 * k:128 * (k + 1), :],
                                  r=["d.h"], w=[hbr])
                            S.add("vector", lambda e, hbbb=hbbb: e.tensor_mul(
                                rscr[:, 0:1024], hbbb[:, :], Bh[:, :]),
                                r=[hbr, "Bh", "rscr"], w=["rscr"])
                            round_neg(0, zb[:, :], 1024, zbr)
                            S.dma("sync", zh_dr[128 * k:128 * (k + 1), :], zb[:, :],
                                  r=[zbr], w=["d.zh"])
                        else:
                            S.dma("gpsimd", zb[:, :], zh_dr[128 * k:128 * (k + 1), :],
                                  r=["d.zh"], w=[zbr])
                        t2 = T2w[k % 2]
                        t2r = f"T2w{k % 2}"
                        l2 = L2b[k % 2]
                        S.dma("scalar", l2[:, :],
                              t2o[128 * k:128 * (k + 1), 512 * og:512 * (og + 1)],
                              r=["d.t1o"], w=[f"L2b{k % 2}"])
                        S.add("vector", lambda e, t2=t2, l2=l2: e.tensor_copy(
                            t2[:, :], l2[:, :]), r=[f"L2b{k % 2}"], w=[t2r])
                        for ob in range(4):
                            for hf in range(2):
                                off = 1024 * ob + 512 * hf
                                S.mm(PS[:, off:off + 512],
                                     t2[:, 128 * ob:128 * (ob + 1)],
                                     zb[:, 512 * hf:512 * (hf + 1)],
                                     start=(k == 0), stop=(k == NHB - 1),
                                     r=[t2r, zbr], w=pres(off, 512))
                    for ob in range(4):
                        S.add("vector", lambda e, ob=ob, ftb=ftb: e.tensor_mul(
                            ftb[:, 1024 * ob:1024 * (ob + 1)],
                            PS[:, 1024 * ob:1024 * (ob + 1)], Bo[:, :]),
                            r=pres(1024 * ob, 1024) + ["Bo"], w=[ftr])
                    for ob in range(4):
                        for c in range(CH):
                            off = 1024 * ob + 128 * c
                            S.mm(PS[:, off:off + 128],
                                 ftb[:, off:off + 128], idn[:, :],
                                 r=[ftr, "c.idn"], w=pres(off, 128))
                    for ob in range(4):
                        fb = 4 * og + ob
                        x2v = v3(x2p, CH * D, 128 * fb, D, CH, 1, 128)
                        x1v = v3(ox1, CH * D, 128 * fb, D, CH, 1, 128)
                        psv = v3(PS, 4096, 1024 * ob, 128, CH, 1, 128)
                        S.add("vector", lambda e, a=x2v, p=psv, b=x1v:
                              e.scalar_tensor_tensor(a, p, 1.0, b, OP.mult, OP.add),
                              r=pres(1024 * ob, 1024) + ["ox1", "x2p"], w=["x2p"])
                tap("x2p", x2p[:, :], "x2p")
                tap("stat", stat[:, :], "st.dc")
                if DBG:
                    S.dma("sync", tp["msc"][:, :], msc[:, :], r=["msc"])

                for c in range(CH):
                    S.add("vector", lambda e, c=c: e.reduce_sum(
                        stat[:, X2MU + c:X2MU + c + 1], x2p[:, D * c:D * (c + 1)], AX),
                        r=["x2p"], w=["st.x2mu"])
                    S.add("scalar", lambda e, c=c: e.activation(
                        rscr[:, :], x2p[:, D * c:D * (c + 1)], AF.Square,
                        accum_out=stat[:, X2SSQ + c:X2SSQ + c + 1]),
                        r=["x2p"], w=["rscr", "st.x2ssq"])
                S.add("vector", lambda e: e.tensor_scalar(
                    stc(X2MU, 8), stc(X2MU, 8), 1.0 / D, None, OP.mult),
                    r=["st.x2mu"], w=["st.x2mu"])
                S.add("vector", lambda e: e.tensor_scalar(
                    stc(X2SSQ, 8), stc(X2SSQ, 8), 1.0 / D, None, OP.mult),
                    r=["st.x2ssq"], w=["st.x2ssq"])
                S.add("vector", lambda e: e.tensor_mul(
                    stc(SCR8, 8), stc(X2MU, 8), stc(X2MU, 8)),
                    r=["st.x2mu"], w=["st.s8"])
                S.add("vector", lambda e: e.tensor_sub(
                    stc(X2SSQ, 8), stc(X2SSQ, 8), stc(SCR8, 8)),
                    r=["st.x2ssq", "st.s8"], w=["st.x2ssq"])
                S.add("scalar", lambda e: e.activation(
                    stc(SCR8, 8), stc(X2SSQ, 8), AF.Sqrt, bias=eps_c[:, :]),
                    r=["st.x2ssq", "c.eps"], w=["st.s8"])
                S.add("vector", lambda e: e.reciprocal(stc(X2RS, 8), stc(SCR8, 8)),
                      r=["st.s8"], w=["st.x2rs"])
                for c in range(CH):
                    ob2 = outst2[c % 2]
                    obr = f"outst{c % 2}"
                    S.add("vector", lambda e, c=c, ob2=ob2: e.tensor_scalar(
                        ob2[:, :], x2p[:, D * c:D * (c + 1)],
                        stat[:, X2MU + c:X2MU + c + 1],
                        stat[:, X2RS + c:X2RS + c + 1], OP.subtract, OP.mult),
                        r=["x2p", "st.x2mu", "st.x2rs", obr], w=[obr])
                    S.dma(("sync", "scalar")[c % 2], out_d[CT * c:CT * (c + 1), :],
                          ob2[:, :], r=[obr], w=["d.out"])
                S.emit(nc, sems)
        S.enabled = True
        S.add("gpsimd", lambda e: None, barrier=True)
        S.emit(nc, sems)
    return nc


def kernel(L0_attn_x, L0_attn_Wq, L0_attn_Wk, L0_attn_Wkg1, L0_attn_Wkg2,
           L0_attn_bkg2, L0_attn_Wv, L0_attn_Wg, L0_attn_bg, L0_attn_Wo,
           L0_ffn_w1, L0_ffn_w2, L0_ffn_w3):
    f32 = np.float32
    bf16 = ml_dtypes.bfloat16
    x = np.ascontiguousarray(np.asarray(L0_attn_x, f32).reshape(NC * TOK, D))
    wq = np.ascontiguousarray(np.asarray(L0_attn_Wq, f32))
    wk = np.ascontiguousarray(np.asarray(L0_attn_Wk, f32))
    wkg1 = np.ascontiguousarray(np.asarray(L0_attn_Wkg1, f32))
    wkg2 = np.ascontiguousarray(np.asarray(L0_attn_Wkg2, f32))
    bkg2T = np.ascontiguousarray(np.asarray(L0_attn_bkg2, f32).reshape(8, 128).T)
    wv = np.ascontiguousarray(np.asarray(L0_attn_Wv, f32))
    wg = np.ascontiguousarray(np.asarray(L0_attn_Wg, f32))
    bgT = np.ascontiguousarray(np.asarray(L0_attn_bg, f32).reshape(16, 128).T)
    wo = np.ascontiguousarray(np.asarray(L0_attn_Wo, f32))
    w1 = np.ascontiguousarray(np.asarray(L0_ffn_w1, f32))
    w2 = np.ascontiguousarray(np.asarray(L0_ffn_w2, f32))
    w3 = np.ascontiguousarray(np.asarray(L0_ffn_w3, f32))

    idn = np.eye(128).astype(bf16)
    idnf = np.eye(128).astype(f32)
    tri = np.triu(np.ones((128, 128), f32))
    tri8 = np.tile(tri, (1, 8)).astype(bf16)

    nc = build()
    in_maps = []
    for i in range(NC):
        m = np.zeros((128, 16), f32)
        m[:, 0:i] = 1.0
        m[:, 8:16] = 1.0 - m[:, 0:8]
        in_maps.append({
            "x": x[TOK * i:TOK * (i + 1)],
            "wq": wq, "wk": wk, "wkg1": wkg1, "wkg2": wkg2, "bkg2T": bkg2T,
            "wv": wv, "wg": wg, "bgT": bgT, "wo": wo,
            "w1s": np.ascontiguousarray(w1.T[(D // NC) * i:(D // NC) * (i + 1)]),
            "w2s": np.ascontiguousarray(w2.T[(HID // NC) * i:(HID // NC) * (i + 1)]),
            "w3s": np.ascontiguousarray(w3.T[(D // NC) * i:(D // NC) * (i + 1)]),
            "idn": idn, "idnf": idnf, "tri8": tri8, "scanm": m,
        })
    res = run_bass_kernel_spmd(nc, in_maps, core_ids=list(range(NC)))
    LAST["res"] = res
    out = np.concatenate([res.results[i]["out"] for i in range(NC)], axis=0)
    return out.reshape(1, NC * TOK, D).astype(np.float32)
